# version 04
# baseline (speedup 1.0000x reference)
import math
import numpy as np
from contextlib import ExitStack

import concourse.bass as bass
import concourse.mybir as mybir
from concourse.bass_utils import run_bass_kernel_spmd

F32 = mybir.dt.float32
BF16 = mybir.dt.bfloat16
AF = mybir.ActivationFunctionType
ALU = mybir.AluOpType
AX = mybir.AxisListType

NCORES = 8
D = 1024
SEQ = 2048
TPC = 4096
HV = 16

C_INV_D, C_EPS, C_NEG1, C_QK8 = 0, 1, 2, 3



def _pos_encoding_T2():
    L, dim = SEQ, 64
    positions = np.arange(L, dtype=np.float32)[:, None]
    div_term = np.exp(np.arange(0, dim, 2, dtype=np.float32) * -(np.log(10000.0) / dim)).astype(np.float32)
    ang = (positions * div_term).astype(np.float32)
    pe = np.zeros((L, dim), dtype=np.float32)
    pe[:, 0::2] = np.sin(ang)
    pe[:, 1::2] = np.cos(ang)
    peT = np.ascontiguousarray(pe.T)
    return np.ascontiguousarray(np.concatenate([peT, peT], axis=0))


def _blk_lhsT(W):
    O, I = W.shape
    return np.ascontiguousarray(W.reshape(O // 128, 128, I // 128, 128).transpose(0, 3, 2, 1))


def _col_chunks(v):
    return np.ascontiguousarray(np.asarray(v, np.float32).reshape(-1, 128).T)


def _rep(v, n=128):
    v = np.asarray(v, np.float32).reshape(-1)
    return np.ascontiguousarray(np.broadcast_to(v.reshape(1, -1), (n, v.size)))


def prep_shared(inputs):
    f = lambda k: np.ascontiguousarray(np.asarray(inputs[k], np.float32))
    w = {}
    perm = np.empty(3072, np.int64)
    for part in range(3):
        for h in range(16):
            base = part * 1024 + h * 64
            perm[base:base + 64] = h * 192 + part * 64 + np.arange(64)
    wqkv = f("L0_attn_w_qkv")[:, perm]
    w["wqkvb"] = np.ascontiguousarray(wqkv.reshape(8, 128, 3072))
    w["bqkv"] = _col_chunks(f("L0_attn_b_qkv")[perm])
    w["woutb"] = np.ascontiguousarray(f("L0_attn_w_out").reshape(8, 128, 1024))
    w["boutb"] = _rep(f("L0_attn_b_out"))
    w["pos2"] = _pos_encoding_T2()
    w["onesb"] = np.ones((1, 128), np.float32)
    for nm, key in (("wq", "L1_attn_wq"), ("wk", "L1_attn_wk"), ("wv", "L1_attn_wv")):
        m = np.transpose(f(key), (1, 0, 2)).reshape(1024, 1024)
        w[nm + "b"] = np.ascontiguousarray(m.reshape(8, 128, 1024))
    w["bqc"] = _col_chunks(f("L1_attn_bq").reshape(-1))
    w["bkc"] = _col_chunks(f("L1_attn_bk").reshape(-1))
    w["bvb"] = _rep(f("L1_attn_bv").reshape(-1))
    w["wob"] = np.ascontiguousarray(f("L1_attn_wo").T.reshape(8, 128, 1024))
    w["bob"] = _rep(f("L1_attn_bo"))
    for L in (0, 1):
        p = f"L{L}_ffn_"
        w[f"w1b{L}"] = _blk_lhsT(f(p + "w1_w"))
        w[f"w2b{L}"] = _blk_lhsT(f(p + "w2_w"))
        w[f"h1b{L}"] = _blk_lhsT(f(p + "h1_w"))
        w[f"h2b{L}"] = _blk_lhsT(f(p + "h2_w"))
        w[f"w3b{L}"] = np.ascontiguousarray(f(p + "w3_w").T.reshape(32, 128, 1024))
        w[f"sc1{L}"] = _col_chunks(f(p + "w1_s"))
        w[f"sc2{L}"] = _col_chunks(f(p + "w2_s"))
        w[f"sch1{L}"] = _col_chunks(f(p + "h1_s"))
        w[f"sch2{L}"] = _col_chunks(f(p + "h2_s"))
        w[f"s3b{L}"] = _rep(f(p + "w3_s"))
    consts = np.zeros((128, 8), np.float32)
    consts[:, C_INV_D] = 1.0 / D
    consts[:, C_EPS] = 1e-5
    consts[:, C_NEG1] = -1.0
    consts[:, C_QK8] = 0.125
    w["consts"] = consts
    w["ident"] = np.eye(128, dtype=np.float32)
    return w



SEMS = ("dX", "dS", "gD", "tS", "aS", "vS", "pS")


DMA_SEMS = ("dX", "dS", "gD")


class Sched:
    def __init__(self):
        self.c = {n: 0 for n in SEMS}
        self.marks = {}
        self.sem = {}
        self.pfx = ""

    def _k(self, key):
        return (self.pfx,) + tuple(key)

    def inc(self, name, key=None, n=1):
        self.c[name] += n
        if key is not None:
            self.marks[self._k(key)] = (name, self.c[name])
        return self.c[name]

    def setmark(self, key, name):
        self.marks[self._k(key)] = (name, self.c[name])

    def alias(self, key, srckey):
        self.marks[self._k(key)] = self.marks[self._k(srckey)]

    def wait(self, eng, key):
        name, cnt = self.marks[self._k(key)]
        if name in DMA_SEMS:
            cnt *= 16
        eng.wait_ge(self.sem[name], cnt)

    def wait_max(self, eng, keys):
        best = {}
        for k in keys:
            name, cnt = self.marks[self._k(k)]
            best[name] = max(best.get(name, 0), cnt)
        for name, cnt in best.items():
            if name in DMA_SEMS:
                cnt *= 16
            eng.wait_ge(self.sem[name], cnt)

    def snapshot(self):
        return dict(self.c)

    def barrier(self, eng, snap):
        for name, cnt in snap.items():
            if cnt <= 0:
                continue
            eng.wait_ge(self.sem[name], cnt * 16 if name in DMA_SEMS else cnt)

    def final(self, name):
        return self.c[name]



def ln_prepass(S, pfx, G):
    S.inc("aS", (pfx, "sq", G))
    S.inc("vS", (pfx, "var", G))
    S.inc("aS", (pfx, "sqrt", G))
    S.inc("vS", (pfx, "rstdb", G))
    S.inc("vS", (pfx, "apply", G))
    S.inc("tS", (pfx, "tr", G))
    S.inc("vS", (pfx, "xevac", G))


def ln_emit_vector(v, S, pfx, G, xchunk, stats, consts, pTRb, xlnT_dst3d, xln_tok,
                   extra_apply_waits=()):
    b = 8 * (G % 2)
    S.wait(v, (pfx, "xload", G))
    v.reduce_sum(stats[:, b + 0:b + 1], xchunk[:, :], axis=AX.X)
    S.wait(v, (pfx, "sq", G))
    v.drain()
    v.tensor_scalar_mul(stats[:, b + 2:b + 3], stats[:, b + 0:b + 1], consts[:, C_INV_D:C_INV_D + 1])
    v.drain()
    v.tensor_mul(stats[:, b + 3:b + 4], stats[:, b + 2:b + 3], stats[:, b + 2:b + 3])
    v.drain()
    v.scalar_tensor_tensor(stats[:, b + 4:b + 5], stats[:, b + 1:b + 2],
                           consts[:, C_INV_D:C_INV_D + 1], stats[:, b + 3:b + 4],
                           ALU.mult, ALU.subtract).then_inc(S.sem["vS"], 1)
    S.wait(v, (pfx, "sqrt", G))
    v.reciprocal(stats[:, b + 6:b + 7], stats[:, b + 5:b + 6])
    v.drain()
    v.tensor_scalar(stats[:, b + 7:b + 8], stats[:, b + 2:b + 3], stats[:, b + 6:b + 7],
                    consts[:, C_NEG1:C_NEG1 + 1], ALU.mult, ALU.mult).then_inc(S.sem["vS"], 1)
    v.drain()
    if extra_apply_waits:
        S.wait_max(v, extra_apply_waits)
    v.tensor_scalar(xln_tok[:, :], xchunk[:, :], stats[:, b + 6:b + 7],
                    stats[:, b + 7:b + 8], ALU.mult, ALU.add).then_inc(S.sem["vS"], 1)
    S.wait(v, (pfx, "tr", G))
    v.tensor_copy(xlnT_dst3d, pTRb).then_inc(S.sem["vS"], 1)


def ln_emit_scalar(s, S, pfx, G, xchunk, stats, consts, sqscr):
    b = 8 * (G % 2)
    S.wait(s, (pfx, "xload", G))
    s.activation(sqscr[:, :], xchunk[:, :], AF.Square,
                 accum_out=stats[:, b + 1:b + 2]).then_inc(S.sem["aS"], 1)
    S.wait(s, (pfx, "var", G))
    s.activation(stats[:, b + 5:b + 6], stats[:, b + 4:b + 5], AF.Sqrt,
                 bias=consts[:, C_EPS:C_EPS + 1]).then_inc(S.sem["aS"], 1)



def ln_emit_tensor(t, S, pfx, G, xln_tok, identbf, pTRb_slots, wait_keys):
    S.wait_max(t, wait_keys + [(pfx, "apply", G)])
    ins = None
    for r in range(8):
        ins = t.transpose(pTRb_slots[r], xln_tok[:, r * 128:(r + 1) * 128], identbf[:, :])
    ins.then_inc(S.sem["tS"], 1)



def emit_ffn(nc, block, S, sh, dr, snap):
    FT = 1024
    NT = TPC // FT
    dxin, dxout = dr["xin"], dr["xout"]
    dw1, dw2, dh1, dh2, dw3 = dr["w1b"], dr["w2b"], dr["h1b"], dr["h2b"], dr["w3b"]
    dsc1, dsc2, dsh1, dsh2, ds3 = dr["sc1"], dr["sc2"], dr["sch1"], dr["sch2"], dr["s3b"]
    consts, ident32, identbf = sh["consts"], sh["ident32"], sh["identbf"]
    xchunk, sqscr, stats = sh["xchunk"], sh["sqscr"], sh["stats"]
    xr4, outv, outp = sh["xr4"], sh["outv"], sh["outp"]
    S.inc("dX", n=5); S.setmark(("init",), "dX")
    for i in range(NT):
        for tc in range(8):
            G = i * 8 + tc
            S.inc("dX", ("f", "xload", G))
            ln_prepass(S, "f", G)
        for c in range(32):
            bi = i * 32 + c
            S.inc("gD", n=2); S.setmark(("wB", bi), "gD")
            S.inc("tS", ("mmB", bi))
            S.inc("aS", ("sil", bi))
            S.inc("vS", ("g1", bi))
        for st_ in range(2):
            for c in range(32):
                gi = i * 64 + st_ * 32 + c
                S.inc("gD", ("wCDh", 2 * gi))
                S.inc("gD", ("wCDh", 2 * gi + 1))
                S.inc("tS", ("mmCDh", 2 * gi))
                S.inc("tS", ("mmCD", gi))
                S.alias(("mmCDh", 2 * gi + 1), ("mmCD", gi))
                S.inc("aS", ("gelu", gi))
        for half in range(2):
            for r in range(32):
                ei = (i * 2 + half) * 32 + r
                S.inc("gD", ("wE", ei))
                S.inc("tS", ("mmE", ei))
            for t4 in range(4):
                idx = i * 8 + half * 4 + t4
                S.inc("dX", ("xr", idx))
                S.inc("vS", ("ev1", idx))
                S.inc("pS", ("ev2", idx))
                S.inc("dS", ("store", idx))

    with ExitStack() as st:
        sb = lambda n, shape, dt: st.enter_context(nc.sbuf_tensor("sb_" + S.pfx + n, shape, dt))
        xln_tok = [sb("xlntok0", [128, 1024], BF16), sb("xlntok1", [128, 1024], BF16)]
        xlnT = sb("xlnT", [128, 8 * 1024], BF16)
        g_a = sb("g_a", [128, 32 * 1024], BF16)
        g_b = sb("g_b", [128, 32 * 1024], BF16)
        wb1 = [sb("wb1_0", [128, 1024], BF16), sb("wb1_1", [128, 1024], BF16)]
        wb2 = [sb("wb2_0", [128, 1024], BF16), sb("wb2_1", [128, 1024], BF16)]
        wbig = [sb("wbig0", [128, 2048], BF16), sb("wbig1", [128, 2048], BF16)]
        tsil = [sb("tsil0", [128, 1024], BF16), sb("tsil1", [128, 1024], BF16)]
        sc1 = sb("sc1", [128, 32], F32)
        sc2 = sb("sc2", [128, 32], F32)
        sch1 = sb("sch1", [128, 32], F32)
        sch2 = sb("sch2", [128, 32], F32)
        s3b = sb("s3b", [128, 1024], F32)
        pA = st.enter_context(nc.psum_tensor(S.pfx + "pA", [128, 1024], F32))
        pB = st.enter_context(nc.psum_tensor(S.pfx + "pB", [128, 1024], F32))
        pC = st.enter_context(nc.psum_tensor(S.pfx + "pC", [128, 1024], F32))
        pE = st.enter_context(nc.psum_tensor(S.pfx + "pE", [128, 1024], F32))
        psums = [pA, pB, pC, pE]

        pTRb = pE[:, :].bitcast(BF16)
        pTR_slots = [pTRb[:, r * 128:(r + 1) * 128] for r in range(8)]

        def xlnT_dst3d(tc):
            return xlnT[:, :].rearrange("p (r t) -> p r t", t=1024)[:, :, tc * 128:(tc + 1) * 128]

        def pTR_src3d():
            return pTRb.rearrange("p (r t) -> p r t", t=128)[:, 0:8, :]

        @block.sync
        def _(sync):
            S.barrier(sync, snap)
            for dst, srcdram in ((sc1, dsc1), (sc2, dsc2),
                                 (sch1, dsh1), (sch2, dsh2), (s3b, ds3)):
                sync.dma_start(out=dst[:, :], in_=srcdram[:, :]).then_inc(S.sem["dX"], 16)
            for i in range(NT):
                for tc in range(8):
                    G = i * 8 + tc
                    if G >= 2:
                        S.wait(sync, ("f", "apply", G - 2))
                    sync.dma_start(out=xchunk[G % 2][:, :],
                                   in_=dxin[G * 128:(G + 1) * 128, :]).then_inc(S.sem["dX"], 16)
                for half in range(2):
                    for t4 in range(4):
                        idx = i * 8 + half * 4 + t4
                        if idx >= 1:
                            S.wait(sync, ("ev2", idx - 1))
                        row0 = i * 1024 + (half * 4 + t4) * 128
                        sync.dma_start(out=xr4[:, :],
                                       in_=dxin[row0:row0 + 128, :]).then_inc(S.sem["dX"], 16)
                        S.wait(sync, ("ev2", idx))
                        sync.dma_start(out=dxout[row0:row0 + 128, :],
                                       in_=outp[:, :]).then_inc(S.sem["dS"], 16)
            sync.wait_ge(S.sem["dS"], 16 * S.final("dS"))

        @block.gpsimd
        def _(g):
            S.barrier(g, snap)
            for i in range(NT):
                for c in range(32):
                    bi = i * 32 + c
                    if c >= 2:
                        S.wait(g, ("mmB", bi - 2))
                    elif i >= 1:
                        S.wait(g, ("mmE", (i - 1) * 64 + 62 + c))
                    g.dma_start(out=wb1[c % 2][:, :],
                                in_=dw1[c].rearrange("p r j -> p (r j)")).then_inc(S.sem["gD"], 16)
                    g.dma_start(out=wb2[c % 2][:, :],
                                in_=dw2[c].rearrange("p r j -> p (r j)")).then_inc(S.sem["gD"], 16)
                for st_ in range(2):
                    dh = dh1 if st_ == 0 else dh2
                    for c in range(32):
                        gi = i * 64 + st_ * 32 + c
                        for hp in range(2):
                            hi = 2 * gi + hp
                            if hi >= 2:
                                S.wait(g, ("mmCDh", hi - 2))
                            g.dma_start(out=wbig[hi % 2][:, :],
                                        in_=dh[c].rearrange("p r j -> p (r j)")[:, hp * 2048:(hp + 1) * 2048]
                                        ).then_inc(S.sem["gD"], 16)
                for half in range(2):
                    for r in range(32):
                        ei = (i * 2 + half) * 32 + r
                        if half == 0 and r < 2:
                            S.wait(g, ("mmB", i * 32 + 30 + r))
                        else:
                            S.wait(g, ("mmE", ei - 2))
                        g.dma_start(out=wb1[ei % 2][:, :],
                                    in_=dw3[r][:, :]).then_inc(S.sem["gD"], 16)
                    for t4 in range(4):
                        idx = i * 8 + half * 4 + t4
                        S.wait_max(g, [("ev1", idx), ("xr", idx)] +
                                   ([("store", idx - 1)] if idx >= 1 else []))
                        g.tensor_add(outp[:, :], outv[:, :],
                                     xr4[:, :]).then_inc(S.sem["pS"], 1)

        @block.tensor
        def _(t):
            S.barrier(t, snap)
            for i in range(NT):
                for tc in range(8):
                    G = i * 8 + tc
                    waits = []
                    if G >= 1:
                        waits.append(("f", "xevac", G - 1))
                    if i >= 1:
                        waits.append(("ev1", i * 8 - 1))
                    ln_emit_tensor(t, S, "f", G, xln_tok[G % 2], identbf, pTR_slots, waits)
                for c in range(32):
                    bi = i * 32 + c
                    pX, pY = (pA, pB) if c % 2 == 0 else (pC, pE)
                    waits = [("wB", bi)]
                    if c == 0:
                        waits.append(("f", "xevac", i * 8 + 7))
                    if bi >= 2:
                        waits += [("sil", bi - 2), ("g1", bi - 2)]
                    if c < 2 and i >= 1:
                        waits.append(("ev1", i * 8 - 1))
                    S.wait_max(t, waits)
                    ins = None
                    for r in range(8):
                        for n in range(2):
                            ins = t.matmul(pX[:, n * 512:(n + 1) * 512],
                                           wb1[c % 2][:, r * 128:(r + 1) * 128],
                                           xlnT[:, r * 1024 + n * 512: r * 1024 + (n + 1) * 512],
                                           start=(r == 0), stop=(r == 7))
                    for r in range(8):
                        for n in range(2):
                            ins = t.matmul(pY[:, n * 512:(n + 1) * 512],
                                           wb2[c % 2][:, r * 128:(r + 1) * 128],
                                           xlnT[:, r * 1024 + n * 512: r * 1024 + (n + 1) * 512],
                                           start=(r == 0), stop=(r == 7))
                    ins.then_inc(S.sem["tS"], 1)
                for st_ in range(2):
                    gsrc = g_a if st_ == 0 else g_b
                    for c in range(32):
                        gi = i * 64 + st_ * 32 + c
                        pX = pA if c % 2 == 0 else pC
                        waits = [("wCDh", 2 * gi)]
                        if st_ == 0 and c == 0:
                            waits.append(("g1", i * 32 + 31))
                        if st_ == 1 and c == 0:
                            waits.append(("gelu", i * 64 + 31))
                        if gi >= 2:
                            waits.append(("gelu", gi - 2))
                        if c < 2 and st_ == 0:
                            waits += [("sil", i * 32 + 30 + c)]
                        S.wait_max(t, waits)
                        ins = None
                        for r in range(32):
                            if r == 16:
                                ins.then_inc(S.sem["tS"], 1)
                                S.wait(t, ("wCDh", 2 * gi + 1))
                            for n in range(2):
                                ins = t.matmul(pX[:, n * 512:(n + 1) * 512],
                                               wbig[(2 * gi + r // 16) % 2][:, (r % 16) * 128:(r % 16 + 1) * 128],
                                               gsrc[:, r * 1024 + n * 512: r * 1024 + (n + 1) * 512],
                                               start=(r == 0), stop=(r == 31))
                        ins.then_inc(S.sem["tS"], 1)
                for half in range(2):
                    if half == 1:
                        S.wait_max(t, [("ev1", i * 8 + 3)])
                    for r in range(32):
                        ei = (i * 2 + half) * 32 + r
                        waits = [("wE", ei), ("gelu", i * 64 + 32 + r)]
                        if half == 0 and r == 0:
                            waits += [("gelu", i * 64 + 63), ("g1", i * 32 + 31)]
                        S.wait_max(t, waits)
                        ins = None
                        for t4 in range(4):
                            tloc = half * 4 + t4
                            for n in range(2):
                                ins = t.matmul(psums[t4][:, n * 512:(n + 1) * 512],
                                               g_a[:, r * 1024 + tloc * 128: r * 1024 + (tloc + 1) * 128],
                                               wb1[ei % 2][:, n * 512:(n + 1) * 512],
                                               start=(r == 0), stop=(r == 31))
                        ins.then_inc(S.sem["tS"], 1)

        @block.vector
        def _(v):
            S.barrier(v, snap)
            S.wait(v, ("init",))
            for i in range(NT):
                for tc in range(8):
                    G = i * 8 + tc
                    extra = [("f", "tr", G - 2)] if G >= 2 else []
                    ln_emit_vector(v, S, "f", G, xchunk[G % 2], stats, consts,
                                   pTR_src3d(), xlnT_dst3d(tc), xln_tok[G % 2], extra)
                for c in range(32):
                    bi = i * 32 + c
                    pY = pB if c % 2 == 0 else pE
                    S.wait(v, ("sil", bi))
                    v.scalar_tensor_tensor(g_a[:, c * 1024:(c + 1) * 1024], pY[:, :],
                                           sc2[:, c:c + 1], tsil[c % 2][:, :],
                                           ALU.mult, ALU.mult).then_inc(S.sem["vS"], 1)
                for half in range(2):
                    for t4 in range(4):
                        idx = i * 8 + half * 4 + t4
                        waits = [("mmE", (i * 2 + half) * 32 + 31)]
                        if idx >= 1:
                            waits.append(("ev2", idx - 1))
                        S.wait_max(v, waits)
                        v.tensor_mul(outv[:, :],
                                     psums[t4][:, :], s3b[:, :]).then_inc(S.sem["vS"], 1)

        @block.scalar
        def _(s):
            S.barrier(s, snap)
            S.wait(s, ("init",))
            for i in range(NT):
                for tc in range(8):
                    G = i * 8 + tc
                    ln_emit_scalar(s, S, "f", G, xchunk[G % 2], stats, consts, sqscr)
                for c in range(32):
                    bi = i * 32 + c
                    pX = pA if c % 2 == 0 else pC
                    waits = [("mmB", bi)]
                    if bi >= 2:
                        waits.append(("g1", bi - 2))
                    S.wait_max(s, waits)
                    s.activation(tsil[c % 2][:, :], pX[:, :], AF.Silu,
                                 scale=sc1[:, c:c + 1]).then_inc(S.sem["aS"], 1)
                for st_ in range(2):
                    gdst = g_b if st_ == 0 else g_a
                    sch = sch1 if st_ == 0 else sch2
                    for c in range(32):
                        gi = i * 64 + st_ * 32 + c
                        pX = pA if c % 2 == 0 else pC
                        S.wait(s, ("mmCD", gi))
                        s.activation(gdst[:, c * 1024:(c + 1) * 1024], pX[:, :], AF.Gelu,
                                     scale=sch[:, c:c + 1]).then_inc(S.sem["aS"], 1)



def emit_attn0(nc, block, S, sh, dr, snap):
    AT = 512
    NTI = TPC // AT
    dxin, dxout = dr["xin"], dr["xout"]
    dwqkv, dbqkv, dwout, dbout, dpos = dr["wqkvb"], dr["bqkv"], dr["woutb"], dr["boutb"], dr["pos2"]
    dones = dr["onesb"]
    consts, ident32, identbf = sh["consts"], sh["ident32"], sh["identbf"]
    xchunk, sqscr, stats = sh["xchunk"], sh["sqscr"], sh["stats"]
    xr4, outv, outp = sh["xr4"], sh["outv"], sh["outp"]
    S.inc("dX", n=4); S.setmark(("init",), "dX")
    S.inc("gD", n=16); S.setmark(("winit",), "gD")
    S.inc("vS", ("onesbf",))
    for ti in range(NTI):
        for tc in range(4):
            G = ti * 4 + tc
            S.inc("dX", ("a", "xload", G))
            ln_prepass(S, "a", G)
        for oc in range(24):
            S.inc("tS", ("mmQ", ti, oc))
            S.inc("vS", ("qev", ti, oc))
        for si in range(3):
            for tc in range(4):
                S.inc("tS", ("trQ", ti, si, tc))
                S.inc("aS", ("tevQ", ti, si, tc))
        for tc in range(4):
            S.inc("pS", ("qkp", ti, tc))
            S.inc("vS", ("nm8", ti, tc))
            S.inc("aS", ("exp", ti, tc))
            S.inc("pS", ("pvp", ti, tc))
            S.inc("vS", ("otok", ti, tc))
        for tc in range(4):
            S.inc("tS", ("trO", ti, tc))
            S.inc("aS", ("tevO", ti, tc))
        for tc in range(4):
            idx = ti * 4 + tc
            S.inc("tS", ("mmW", ti, tc))
            S.inc("dX", ("wxr", idx))
            S.inc("vS", ("oadd", idx))
            S.inc("dS", ("store", idx))

    with ExitStack() as st:
        sb = lambda n, shape, dt: st.enter_context(nc.sbuf_tensor("sb_" + S.pfx + n, shape, dt))
        xln_tok = [sb("xlntok0", [128, 1024], BF16), sb("xlntok1", [128, 1024], BF16)]
        xlnT = sb("xlnT", [128, 4096], BF16)
        wqkv_sb = sb("wqkv", [128, 8 * 3072], BF16)
        wout_sb = sb("wout", [128, 8 * 1024], BF16)
        bqkv = sb("bqkv", [128, 24], F32)
        boutb = sb("boutb", [128, 1024], F32)
        pos2 = sb("pos2", [128, 2048], F32)
        qT = sb("qT", [128, 4096], BF16)
        kT = sb("kT", [128, 4096], BF16)
        vT = sb("vT", [128, 4096], BF16)
        q_tok = sb("q_tok", [128, 4096], BF16)
        k_tok = sb("k_tok", [128, 4096], BF16)
        v_tok = sb("v_tok", [128, 4096], BF16)
        prod = sb("prod", [128, 1024], BF16)
        prod2 = sb("prod2", [128, 1024], BF16)
        prodP = sb("prodP", [128, 1024], F32)
        prodP2 = sb("prodP2", [128, 1024], F32)
        Sm = sb("Sm", [128, 256], F32)
        Ee = sb("Ee", [128, 256], BF16)
        esum = sb("esum", [128, 16], F32)
        rec = sb("rec", [128, 16], F32)
        nm8 = sb("nm8", [128, 16], F32)
        Oraw = sb("Oraw", [128, 1024], F32)
        O_bf = sb("O_bf", [128, 4096], BF16)
        OT = sb("OT", [128, 4096], BF16)
        ones1 = sb("ones1", [1, 128], F32)
        onesbf = sb("onesbf", [1, 128], BF16)
        bout_bf = sb("bout_bf", [1, 1024], BF16)
        pQ2 = st.enter_context(nc.psum_tensor(S.pfx + "pQ2", [128, 1024], F32))
        pTR = st.enter_context(nc.psum_tensor(S.pfx + "pTR", [128, 1024], F32))
        pW = [st.enter_context(nc.psum_tensor(S.pfx + "pW0", [128, 1024], F32)),
              st.enter_context(nc.psum_tensor(S.pfx + "pW1", [128, 1024], F32))]

        pTRb = pTR[:, :].bitcast(BF16)
        pTR_slots = [pTRb[:, r * 128:(r + 1) * 128] for r in range(8)]

        def xlnT_dst3d(tc):
            return xlnT[:, :].rearrange("p (r t) -> p r t", t=512)[:, :, tc * 128:(tc + 1) * 128]

        def pTR_src3d():
            return pTRb.rearrange("p (r t) -> p r t", t=128)[:, 0:8, :]

        toks = None

        @block.sync
        def _(sync):
            S.barrier(sync, snap)
            for dst, srcdram in ((bqkv, dbqkv), (boutb, dbout), (pos2, dpos),
                                 (ones1, dones)):
                sync.dma_start(out=dst[:, :], in_=srcdram[:, :]).then_inc(S.sem["dX"], 16)
            for ti in range(NTI):
                for tc in range(4):
                    G = ti * 4 + tc
                    if G >= 2:
                        S.wait(sync, ("a", "apply", G - 2))
                    sync.dma_start(out=xchunk[G % 2][:, :],
                                   in_=dxin[G * 128:(G + 1) * 128, :]).then_inc(S.sem["dX"], 16)
                for tc in range(4):
                    idx = ti * 4 + tc
                    if idx >= 1:
                        S.wait(sync, ("oadd", idx - 1))
                    row0 = idx * 128
                    sync.dma_start(out=xr4[:, :],
                                   in_=dxin[row0:row0 + 128, :]).then_inc(S.sem["dX"], 16)
                    S.wait(sync, ("oadd", idx))
                    sync.dma_start(out=dxout[row0:row0 + 128, :],
                                   in_=outp[:, :]).then_inc(S.sem["dS"], 16)
            sync.wait_ge(S.sem["dS"], 16 * S.final("dS"))

        @block.gpsimd
        def _(g):
            S.barrier(g, snap)
            for r in range(8):
                g.dma_start(out=wqkv_sb[:, r * 3072:(r + 1) * 3072],
                            in_=dwqkv[r][:, :]).then_inc(S.sem["gD"], 16)
            for r in range(8):
                g.dma_start(out=wout_sb[:, r * 1024:(r + 1) * 1024],
                            in_=dwout[r][:, :]).then_inc(S.sem["gD"], 16)
            for ti in range(NTI):
                for tc in range(4):
                    waits = []
                    if tc == 0:
                        waits += [("tevQ", ti, 0, 3), ("tevQ", ti, 1, 3)]
                    if ti > 0 or tc > 0:
                        pt, pc = (ti, tc - 1) if tc > 0 else (ti - 1, 3)
                        waits.append(("exp", pt, pc))
                    if waits:
                        S.wait_max(g, waits)
                    k3 = k_tok[:, tc * 1024:(tc + 1) * 1024].rearrange("p (g d) -> p g d", d=64)
                    p3 = prodP[:, :].rearrange("p (g d) -> p g d", d=64)
                    for h in range(HV, 16):
                        q1 = q_tok[:, tc * 1024 + h * 64: tc * 1024 + (h + 1) * 64]
                        g.tensor_tensor(p3, q1.unsqueeze(1).broadcast_to([128, 16, 64]),
                                        k3, ALU.mult)
                        g.drain()
                        w = 32
                        while w >= 2:
                            g.tensor_add(p3[:, :, 0:w], p3[:, :, 0:w], p3[:, :, w:2 * w])
                            g.drain()
                            w //= 2
                        g.tensor_add(Sm[:, h * 16:(h + 1) * 16].unsqueeze(2),
                                     p3[:, :, 0:1], p3[:, :, 1:2])
                        g.drain()
                    g.engine_nop().then_inc(S.sem["pS"], 1)
                    waits = [("exp", ti, tc)]
                    if tc == 0:
                        waits.append(("tevQ", ti, 2, 3))
                    if ti > 0 or tc > 0:
                        pt, pc = (ti, tc - 1) if tc > 0 else (ti - 1, 3)
                        waits.append(("otok", pt, pc))
                    S.wait_max(g, waits)
                    v3 = v_tok[:, tc * 1024:(tc + 1) * 1024].rearrange("p (d g) -> p d g", g=16)
                    p32 = prodP2[:, :].rearrange("p (d g) -> p d g", g=16)
                    for h in range(HV, 16):
                        e1 = Ee[:, h * 16:(h + 1) * 16]
                        g.tensor_tensor(p32, e1.unsqueeze(1).broadcast_to([128, 64, 16]),
                                        v3, ALU.mult)
                        g.drain()
                        w = 8
                        while w >= 2:
                            g.tensor_add(p32[:, :, 0:w], p32[:, :, 0:w], p32[:, :, w:2 * w])
                            g.drain()
                            w //= 2
                        g.tensor_add(Oraw[:, h * 64:(h + 1) * 64].unsqueeze(2),
                                     p32[:, :, 0:1], p32[:, :, 1:2])
                        g.drain()
                    g.engine_nop().then_inc(S.sem["pS"], 1)

        @block.tensor
        def _(t):
            S.barrier(t, snap)
            ptr_prev = []
            for ti in range(NTI):
                for tc in range(4):
                    G = ti * 4 + tc
                    waits = list(ptr_prev)
                    ln_emit_tensor(t, S, "a", G, xln_tok[G % 2], identbf, pTR_slots, waits)
                    ptr_prev = [("a", "xevac", G)]
                for oc in range(24):
                    waits = []
                    if oc == 0:
                        waits += [("a", "xevac", ti * 4 + 3), ("winit",)]
                    if oc >= 2:
                        waits.append(("qev", ti, oc - 2))
                    if waits:
                        S.wait_max(t, waits)
                    ins = None
                    for r in range(8):
                        ins = t.matmul(pQ2[:, (oc % 2) * 512:(oc % 2) * 512 + 512],
                                       wqkv_sb[:, r * 3072 + oc * 128: r * 3072 + (oc + 1) * 128],
                                       xlnT[:, r * 512:(r + 1) * 512],
                                       start=(r == 0), stop=(r == 7))
                    ins.then_inc(S.sem["tS"], 1)
                for si, srcT in enumerate((qT, kT, vT)):
                    for tc in range(4):
                        waits = list(ptr_prev)
                        if tc == 0:
                            waits.append(("qev", ti, si * 8 + 7))
                        S.wait_max(t, waits)
                        ins = None
                        for j in range(8):
                            ins = t.transpose(pTR_slots[j],
                                              srcT[:, j * 512 + tc * 128: j * 512 + (tc + 1) * 128],
                                              identbf[:, :])
                        ins.then_inc(S.sem["tS"], 1)
                        ptr_prev = [("tevQ", ti, si, tc)]
                for tc in range(4):
                    waits = list(ptr_prev) + [("otok", ti, tc)]
                    S.wait_max(t, waits)
                    ins = None
                    for j in range(8):
                        ins = t.transpose(pTR_slots[j],
                                          O_bf[:, tc * 1024 + j * 128: tc * 1024 + (j + 1) * 128],
                                          identbf[:, :])
                    ins.then_inc(S.sem["tS"], 1)
                    ptr_prev = [("tevO", ti, tc)]
                for tc in range(4):
                    idx = ti * 4 + tc
                    waits = [("tevO", ti, tc)]
                    if idx == 0:
                        waits.append(("onesbf",))
                    if idx >= 2:
                        waits.append(("oadd", idx - 2))
                    S.wait_max(t, waits)
                    ins = None
                    for n in range(2):
                        ins = t.matmul(pW[tc % 2][:, n * 512:(n + 1) * 512],
                                       onesbf[0:1, :],
                                       bout_bf[0:1, n * 512:(n + 1) * 512],
                                       start=True, stop=False)
                    for r in range(8):
                        for n in range(2):
                            ins = t.matmul(pW[tc % 2][:, n * 512:(n + 1) * 512],
                                           OT[:, r * 512 + tc * 128: r * 512 + (tc + 1) * 128],
                                           wout_sb[:, r * 1024 + n * 512: r * 1024 + (n + 1) * 512],
                                           start=False, stop=(r == 7))
                    ins.then_inc(S.sem["tS"], 1)

        @block.vector
        def _(v):
            S.barrier(v, snap)
            S.wait(v, ("init",))
            v.tensor_copy(onesbf[0:1, :], ones1[0:1, :])
            v.drain()
            v.tensor_copy(bout_bf[0:1, :], boutb[0:1, :]).then_inc(S.sem["vS"], 1)
            for ti in range(NTI):
                for tc in range(4):
                    G = ti * 4 + tc
                    extra = [("a", "tr", G - 2)] if G >= 2 else []
                    ln_emit_vector(v, S, "a", G, xchunk[G % 2], stats, consts,
                                   pTR_src3d(), xlnT_dst3d(tc), xln_tok[G % 2], extra)
                for oc in range(24):
                    S.wait(v, ("mmQ", ti, oc))
                    ps = pQ2[:, (oc % 2) * 512:(oc % 2) * 512 + 512]
                    if oc < 8:
                        v.tensor_scalar_add(qT[:, oc * 512:(oc + 1) * 512], ps,
                                            bqkv[:, oc:oc + 1]).then_inc(S.sem["vS"], 1)
                    elif oc < 16:
                        j = oc - 8
                        v.scalar_tensor_tensor(kT[:, j * 512:(j + 1) * 512], ps,
                                               bqkv[:, oc:oc + 1],
                                               pos2[:, (ti % 4) * 512:(ti % 4 + 1) * 512],
                                               ALU.add, ALU.add).then_inc(S.sem["vS"], 1)
                    else:
                        j = oc - 16
                        v.tensor_scalar_add(vT[:, j * 512:(j + 1) * 512], ps,
                                            bqkv[:, oc:oc + 1]).then_inc(S.sem["vS"], 1)
                for tc in range(4):
                    if tc == 0:
                        S.wait_max(v, [("tevQ", ti, 0, 3), ("tevQ", ti, 1, 3)])
                    k3 = k_tok[:, tc * 1024:(tc + 1) * 1024].rearrange("p (g d) -> p g d", d=64)
                    for h in range(HV):
                        q1 = q_tok[:, tc * 1024 + h * 64: tc * 1024 + (h + 1) * 64]
                        v.tensor_tensor(prod[:, :].rearrange("p (g d) -> p g d", d=64),
                                        q1.unsqueeze(1).broadcast_to([128, 16, 64]), k3, ALU.mult)
                        v.drain()
                        v.tensor_reduce(Sm[:, h * 16:(h + 1) * 16],
                                        prod[:, :].rearrange("p (g d) -> p g d", d=64),
                                        axis=AX.X, op=ALU.add)
                        v.drain()
                    S.wait(v, ("qkp", ti, tc))
                    v.tensor_reduce(nm8[:, :], Sm[:, :].rearrange("p (h g) -> p h g", g=16),
                                    axis=AX.X, op=ALU.max, negate=True)
                    v.drain()
                    v.tensor_scalar_mul(nm8[:, :], nm8[:, :], consts[:, C_QK8:C_QK8 + 1])
                    v.drain()
                    v.engine_nop().then_inc(S.sem["vS"], 1)
                    S.wait(v, ("exp", ti, tc))
                    v.reciprocal(rec[:, :], esum[:, :])
                    v.drain()
                    v3 = v_tok[:, tc * 1024:(tc + 1) * 1024].rearrange("p (d g) -> p d g", g=16)
                    if tc == 0:
                        S.wait(v, ("tevQ", ti, 2, 3))
                    for h in range(HV):
                        e1 = Ee[:, h * 16:(h + 1) * 16]
                        v.tensor_tensor(prod2[:, :].rearrange("p (d g) -> p d g", g=16),
                                        e1.unsqueeze(1).broadcast_to([128, 64, 16]), v3, ALU.mult)
                        v.drain()
                        v.tensor_reduce(Oraw[:, h * 64:(h + 1) * 64],
                                        prod2[:, :].rearrange("p (d g) -> p d g", g=16),
                                        axis=AX.X, op=ALU.add)
                        v.drain()
                    S.wait(v, ("pvp", ti, tc))
                    if ti >= 1:
                        S.wait(v, ("trO", ti - 1, tc))
                    v.tensor_tensor(O_bf[:, tc * 1024:(tc + 1) * 1024].rearrange("p (h d) -> p h d", d=64),
                                    Oraw[:, :].rearrange("p (h d) -> p h d", d=64),
                                    rec[:, :].unsqueeze(2).broadcast_to([128, 16, 64]),
                                    ALU.mult).then_inc(S.sem["vS"], 1)
                for tc in range(4):
                    idx = ti * 4 + tc
                    waits = [("mmW", ti, tc), ("wxr", idx)]
                    if idx >= 1:
                        waits.append(("store", idx - 1))
                    S.wait_max(v, waits)
                    v.tensor_add(outp[:, :], pW[tc % 2][:, :], xr4[:, :]).then_inc(S.sem["vS"], 1)

        @block.scalar
        def _(s):
            S.barrier(s, snap)
            for ti in range(NTI):
                for tc in range(4):
                    G = ti * 4 + tc
                    ln_emit_scalar(s, S, "a", G, xchunk[G % 2], stats, consts, sqscr)
                for si, dtok in enumerate((q_tok, k_tok, v_tok)):
                    for tc in range(4):
                        S.wait(s, ("trQ", ti, si, tc))
                        if si < 2:
                            dst3 = dtok[:, tc * 1024:(tc + 1) * 1024].rearrange("p (j k) -> p j k", k=128)
                            s.copy(dst3, pTR_src3d()).then_inc(S.sem["aS"], 1)
                        else:
                            dst3 = dtok[:, tc * 1024:(tc + 1) * 1024].rearrange(
                                "p (d g) -> p g d", g=16)
                            src3 = pTRb.rearrange("p (j d) -> p j d", d=64)[:, 0:16, :]
                            s.copy(dst3, src3).then_inc(S.sem["aS"], 1)
                for tc in range(4):
                    waits = [("nm8", ti, tc)]
                    if ti > 0 or tc > 0:
                        pt, pc = (ti, tc - 1) if tc > 0 else (ti - 1, 3)
                        waits.append(("pvp", pt, pc))
                    S.wait_max(s, waits)
                    ins = None
                    for h in range(16):
                        ins = s.activation(Ee[:, h * 16:(h + 1) * 16], Sm[:, h * 16:(h + 1) * 16],
                                           AF.Exp, scale=consts[:, C_QK8:C_QK8 + 1],
                                           bias=nm8[:, h:h + 1], accum_out=esum[:, h:h + 1])
                    ins.then_inc(S.sem["aS"], 1)
                for tc in range(4):
                    S.wait(s, ("trO", ti, tc))
                    dst3 = OT[:, :].rearrange("p (j t) -> p j t", t=512)[:, :, tc * 128:(tc + 1) * 128]
                    s.copy(dst3, pTR_src3d()).then_inc(S.sem["aS"], 1)


def attn0_inputs(w, xcore):
    return {"xin": xcore, "wqkvb": w["wqkvb"], "bqkv": w["bqkv"], "woutb": w["woutb"],
            "boutb": w["boutb"], "pos2": w["pos2"], "consts": w["consts"], "ident": w["ident"]}



def emit_attn1(nc, block, S, sh, dr, snap):
    NSEQ = TPC // SEQ
    NCH = SEQ // 128
    dxin, dxout = dr["xin"], dr["xout"]
    dwq, dwk, dwv = dr["wqb"], dr["wkb"], dr["wvb"]
    dbq, dbk, dbv, dwo, dbo = dr["bqc"], dr["bkc"], dr["bvb"], dr["wob"], dr["bob"]
    consts, ident32, identbf = sh["consts"], sh["ident32"], sh["identbf"]
    xchunk, sqscr, stats = sh["xchunk"], sh["sqscr"], sh["stats"]
    xr4, outv, outp = sh["xr4"], sh["outv"], sh["outp"]
    S.inc("dX", n=4); S.setmark(("init",), "dX")
    S.inc("gD", n=8); S.setmark(("wob",), "gD")
    for q in range(NSEQ):
        for cc in range(NCH):
            G = q * NCH + cc
            S.inc("dX", ("l", "xload", G))
            ln_prepass(S, "l", G)
        for stg in range(3):
            S.inc("gD", n=8); S.setmark(("wld", q, stg), "gD")
        for stg in range(2):
            for oc in range(8):
                S.inc("tS", ("mmP", q, stg, oc))
                S.inc("vS", ("pev", q, stg, oc))
        for tc in range(16):
            S.inc("tS", ("mmV", q, tc))
            S.inc("vS", ("vev", q, tc))
        for h in range(16):
            for sc in range(16):
                ai = (q * 16 + h) * 16 + sc
                S.inc("tS", ("mmS", ai))
                S.inc("vS", ("nms", ai))
                S.inc("aS", ("exp", ai))
                S.inc("tS", ("trP", ai))
                S.inc("vS", ("tevP", ai))
                S.inc("tS", ("mmO", ai))
                S.inc("vS", ("oev", ai))
                S.inc("tS", ("trB", ai))
                S.inc("vS", ("tevB", ai))
        for sc in range(16):
            oi = q * 16 + sc
            S.inc("tS", ("mmF", oi))
            S.inc("dX", ("oxr", oi))
            S.inc("vS", ("oev1", oi))
            S.inc("pS", ("oev2", oi))
            S.inc("dS", ("store", oi))

    with ExitStack() as st:
        sb = lambda n, shape, dt: st.enter_context(nc.sbuf_tensor("sb_" + S.pfx + n, shape, dt))
        xln_tok = [sb("xlntok0", [128, 1024], BF16), sb("xlntok1", [128, 1024], BF16)]
        xlnT = sb("xlnT", [128, 16384], BF16)
        attT = xlnT
        wsh = sb("wsh", [128, 8192], BF16)
        wob_sb = sb("wob", [128, 8192], BF16)
        bqc = sb("bqc", [128, 8], F32)
        bkc = sb("bkc", [128, 8], F32)
        bvb = sb("bvb", [128, 1024], F32)
        bob = sb("bob", [128, 1024], F32)
        qT = sb("qT", [128, 16384], BF16)
        kT = sb("kT", [128, 16384], BF16)
        v_tok = sb("v_tok", [128, 16384], BF16)
        P_sb = sb("P_sb", [128, 2048], BF16)
        PT = sb("PT", [128, 2048], BF16)
        O_s = sb("O_s", [128, 64], BF16)
        nm1 = sb("nm1", [128, 1], F32)
        esum1 = sb("esum1", [128, 1], F32)
        rec1 = sb("rec1", [128, 1], F32)
        pS = st.enter_context(nc.psum_tensor(S.pfx + "pS", [128, 2048], F32))
        pTr = st.enter_context(nc.psum_tensor(S.pfx + "pTr", [128, 1024], F32))
        pO = st.enter_context(nc.psum_tensor(S.pfx + "pO", [128, 64], F32))

        pTrb = pTr[:, :].bitcast(BF16)
        pTR_slots = [pTrb[:, r * 128:(r + 1) * 128] for r in range(16)]

        def xlnT_dst3d(cc):
            return xlnT[:, :].rearrange("p (r t) -> p r t", t=2048)[:, :, cc * 128:(cc + 1) * 128]

        def pTR_src3d(n=8):
            return pTrb.rearrange("p (r t) -> p r t", t=128)[:, 0:n, :]

        qpb = lambda h: (h % 2) * 64

        @block.sync
        def _(sync):
            S.barrier(sync, snap)
            for dst, srcdram in ((bqc, dbq), (bkc, dbk), (bvb, dbv), (bob, dbo)):
                sync.dma_start(out=dst[:, :], in_=srcdram[:, :]).then_inc(S.sem["dX"], 16)
            for q in range(NSEQ):
                for cc in range(NCH):
                    G = q * NCH + cc
                    if G >= 2:
                        S.wait(sync, ("l", "apply", G - 2))
                    sync.dma_start(out=xchunk[G % 2][:, :],
                                   in_=dxin[G * 128:(G + 1) * 128, :]).then_inc(S.sem["dX"], 16)
                for sc in range(16):
                    oi = q * 16 + sc
                    if oi >= 1:
                        S.wait(sync, ("oev2", oi - 1))
                    row0 = oi * 128
                    sync.dma_start(out=xr4[:, :],
                                   in_=dxin[row0:row0 + 128, :]).then_inc(S.sem["dX"], 16)
                    S.wait(sync, ("oev2", oi))
                    sync.dma_start(out=dxout[row0:row0 + 128, :],
                                   in_=outp[:, :]).then_inc(S.sem["dS"], 16)
            sync.wait_ge(S.sem["dS"], 16 * S.final("dS"))

        @block.gpsimd
        def _(g):
            S.barrier(g, snap)
            for r in range(8):
                g.dma_start(out=wob_sb[:, r * 1024:(r + 1) * 1024],
                            in_=dwo[r][:, :]).then_inc(S.sem["gD"], 16)
            for q in range(NSEQ):
                for stg, dw in enumerate((dwq, dwk, dwv)):
                    if stg == 0 and q >= 1:
                        S.wait(g, ("mmV", q - 1, 15))
                    elif stg == 1:
                        S.wait(g, ("mmP", q, 0, 7))
                    elif stg == 2:
                        S.wait(g, ("mmP", q, 1, 7))
                    for r in range(8):
                        g.dma_start(out=wsh[:, r * 1024:(r + 1) * 1024],
                                    in_=dw[r][:, :]).then_inc(S.sem["gD"], 16)
                for sc in range(16):
                    oi = q * 16 + sc
                    S.wait_max(g, [("oev1", oi), ("oxr", oi)] +
                               ([("store", oi - 1)] if oi >= 1 else []))
                    g.tensor_add(outp[:, :], outv[:, :], xr4[:, :]).then_inc(S.sem["pS"], 1)

        @block.tensor
        def _(t):
            S.barrier(t, snap)
            ptr_prev = []
            for q in range(NSEQ):
                for cc in range(NCH):
                    G = q * NCH + cc
                    waits = list(ptr_prev)
                    ln_emit_tensor(t, S, "l", G, xln_tok[G % 2], identbf,
                                   pTR_slots[0:8], waits)
                    ptr_prev = [("l", "xevac", G)]
                for stg in range(2):
                    for oc in range(8):
                        waits = [("wld", q, stg)]
                        if stg == 0 and oc == 0:
                            waits.append(("l", "xevac", q * NCH + NCH - 1))
                        prev = ("pev", q, stg, oc - 1) if oc >= 1 else \
                               (("pev", q, 0, 7) if stg == 1 else None)
                        if prev is None and q >= 1 and stg == 0:
                            prev = ("mmF", q * 16 - 1)
                            prev = ("oev1", q * 16 - 1)
                        if prev is not None:
                            waits.append(prev)
                        S.wait_max(t, waits)
                        ins = None
                        for n in range(4):
                            for r in range(8):
                                ins = t.matmul(pS[:, n * 512:(n + 1) * 512],
                                               wsh[:, r * 1024 + oc * 128: r * 1024 + (oc + 1) * 128],
                                               xlnT[:, r * 2048 + n * 512: r * 2048 + (n + 1) * 512],
                                               start=(r == 0), stop=(r == 7))
                        ins.then_inc(S.sem["tS"], 1)
                for tc in range(16):
                    waits = [("wld", q, 2)]
                    if tc < 2:
                        waits.append(("pev", q, 1, 7))
                    else:
                        waits.append(("vev", q, tc - 2))
                    S.wait_max(t, waits)
                    half = (tc % 2) * 1024
                    ins = None
                    for r in range(8):
                        for n in range(2):
                            ins = t.matmul(pS[:, half + n * 512: half + (n + 1) * 512],
                                           xlnT[:, r * 2048 + tc * 128: r * 2048 + (tc + 1) * 128],
                                           wsh[:, r * 1024 + n * 512: r * 1024 + (n + 1) * 512],
                                           start=(r == 0), stop=(r == 7))
                    ins.then_inc(S.sem["tS"], 1)
                for h in range(16):
                    for sc in range(16):
                        ai = (q * 16 + h) * 16 + sc
                        waits = []
                        if h == 0 and sc == 0:
                            waits.append(("vev", q, 15))
                        if ai >= 1:
                            waits.append(("exp", ai - 1))
                        S.wait_max(t, waits)
                        ins = None
                        for n in range(4):
                            ins = t.matmul(pS[:, n * 512:(n + 1) * 512],
                                           qT[qpb(h):qpb(h) + 64,
                                              (h // 2) * 2048 + sc * 128: (h // 2) * 2048 + (sc + 1) * 128],
                                           kT[qpb(h):qpb(h) + 64,
                                              (h // 2) * 2048 + n * 512: (h // 2) * 2048 + (n + 1) * 512],
                                           start=True, stop=True)
                        ins.then_inc(S.sem["tS"], 1)
                        waits = [("exp", ai)] + list(ptr_prev)
                        S.wait_max(t, waits)
                        ins = None
                        for j in range(16):
                            ins = t.transpose(pTR_slots[j],
                                              P_sb[:, j * 128:(j + 1) * 128], identbf[:, :])
                        ins.then_inc(S.sem["tS"], 1)
                        waits = [("tevP", ai)]
                        if ai >= 1:
                            waits.append(("oev", ai - 1))
                        S.wait_max(t, waits)
                        ins = None
                        for j in range(16):
                            ins = t.matmul(pO[:, :], PT[:, j * 128:(j + 1) * 128],
                                           v_tok[:, j * 1024 + h * 64: j * 1024 + (h + 1) * 64],
                                           start=(j == 0), stop=(j == 15))
                        ins.then_inc(S.sem["tS"], 1)
                        S.wait_max(t, [("oev", ai), ("tevP", ai)])
                        t.transpose(pTrb[0:64, 0:128], O_s[:, :], identbf[:, :]).then_inc(S.sem["tS"], 1)
                        ptr_prev = [("tevB", ai)]
                for sc in range(16):
                    oi = q * 16 + sc
                    waits = []
                    if sc == 0:
                        waits += [("tevB", (q * 16 + 15) * 16 + 15), ("exp", (q * 16 + 15) * 16 + 15)]
                    if oi >= 2:
                        waits.append(("oev1", oi - 2))
                    if waits:
                        S.wait_max(t, waits)
                    half = (sc % 2) * 1024
                    ins = None
                    for r in range(8):
                        for n in range(2):
                            ins = t.matmul(pS[:, half + n * 512: half + (n + 1) * 512],
                                           attT[:, r * 2048 + sc * 128: r * 2048 + (sc + 1) * 128],
                                           wob_sb[:, r * 1024 + n * 512: r * 1024 + (n + 1) * 512],
                                           start=(r == 0), stop=(r == 7))
                    ins.then_inc(S.sem["tS"], 1)

        @block.vector
        def _(v):
            S.barrier(v, snap)
            S.wait(v, ("init",))
            for q in range(NSEQ):
                for cc in range(NCH):
                    G = q * NCH + cc
                    extra = [("l", "tr", G - 2)] if G >= 2 else []
                    ln_emit_vector(v, S, "l", G, xchunk[G % 2], stats, consts,
                                   pTR_src3d(8), xlnT_dst3d(cc), xln_tok[G % 2], extra)
                for stg in range(2):
                    bc = bqc if stg == 0 else bkc
                    dst = qT if stg == 0 else kT
                    for oc in range(8):
                        S.wait(v, ("mmP", q, stg, oc))
                        v.tensor_scalar_add(dst[:, oc * 2048:(oc + 1) * 2048], pS[:, :],
                                            bc[:, oc:oc + 1]).then_inc(S.sem["vS"], 1)
                for tc in range(16):
                    S.wait(v, ("mmV", q, tc))
                    half = (tc % 2) * 1024
                    v.tensor_add(v_tok[:, tc * 1024:(tc + 1) * 1024],
                                 pS[:, half:half + 1024], bvb[:, :]).then_inc(S.sem["vS"], 1)
                for h in range(16):
                    for sc in range(16):
                        ai = (q * 16 + h) * 16 + sc
                        S.wait(v, ("mmS", ai))
                        v.reduce_max(nm1[:, :], pS[:, :], axis=AX.X, negate=True)
                        v.drain()
                        v.tensor_scalar_mul(nm1[:, :], nm1[:, :], consts[:, C_QK8:C_QK8 + 1])
                        v.drain()
                        v.engine_nop().then_inc(S.sem["vS"], 1)
                        S.wait(v, ("exp", ai))
                        v.reciprocal(rec1[:, :], esum1[:, :])
                        v.drain()
                        S.wait(v, ("trP", ai))
                        v.tensor_copy(PT[:, :].rearrange("p (j t) -> p j t", t=128),
                                      pTR_src3d(16)).then_inc(S.sem["vS"], 1)
                        S.wait(v, ("mmO", ai))
                        v.tensor_scalar_mul(O_s[:, :], pO[:, :],
                                            rec1[:, :]).then_inc(S.sem["vS"], 1)
                        S.wait(v, ("trB", ai))
                        v.tensor_copy(attT[qpb(h):qpb(h) + 64,
                                           (h // 2) * 2048 + sc * 128:(h // 2) * 2048 + (sc + 1) * 128],
                                      pTrb[0:64, 0:128]).then_inc(S.sem["vS"], 1)
                for sc in range(16):
                    oi = q * 16 + sc
                    waits = [("mmF", oi)]
                    if oi >= 1:
                        waits.append(("oev2", oi - 1))
                    S.wait_max(v, waits)
                    half = (sc % 2) * 1024
                    v.tensor_add(outv[:, :], pS[:, half:half + 1024],
                                 bob[:, :]).then_inc(S.sem["vS"], 1)

        @block.scalar
        def _(s):
            S.barrier(s, snap)
            for q in range(NSEQ):
                for cc in range(NCH):
                    G = q * NCH + cc
                    ln_emit_scalar(s, S, "l", G, xchunk[G % 2], stats, consts, sqscr)
                for h in range(16):
                    for sc in range(16):
                        ai = (q * 16 + h) * 16 + sc
                        S.wait(s, ("nms", ai))
                        s.activation(P_sb[:, :], pS[:, :], AF.Exp,
                                     scale=consts[:, C_QK8:C_QK8 + 1], bias=nm1[:, :],
                                     accum_out=esum1[:, :]).then_inc(S.sem["aS"], 1)


def attn1_inputs(w):
    return {"wqb": w["wqb"], "wkb": w["wkb"], "wvb": w["wvb"], "bqc": w["bqc"],
            "bkc": w["bkc"], "bvb": w["bvb"], "wob": w["wob"], "bob": w["bob"]}


def attn0_inputs(w):
    return {"wqkvb": w["wqkvb"], "bqkv": w["bqkv"], "woutb": w["woutb"],
            "boutb": w["boutb"], "pos2": w["pos2"], "onesb": w["onesb"]}


def ffn_inputs(w, L):
    keys = ("w1b", "w2b", "h1b", "h2b", "w3b", "sc1", "sc2", "sch1", "sch2", "s3b")
    return {k + str(L): w[k + str(L)] for k in keys}



def _dram_attn0(nc):
    return {
        "wqkvb": nc.dram_tensor("wqkvb", [8, 128, 3072], F32, kind="ExternalInput"),
        "bqkv": nc.dram_tensor("bqkv", [128, 24], F32, kind="ExternalInput"),
        "woutb": nc.dram_tensor("woutb", [8, 128, 1024], F32, kind="ExternalInput"),
        "boutb": nc.dram_tensor("boutb", [128, 1024], F32, kind="ExternalInput"),
        "pos2": nc.dram_tensor("pos2", [128, 2048], F32, kind="ExternalInput"),
        "onesb": nc.dram_tensor("onesb", [1, 128], F32, kind="ExternalInput"),
    }


def _dram_attn1(nc):
    return {
        "wqb": nc.dram_tensor("wqb", [8, 128, 1024], F32, kind="ExternalInput"),
        "wkb": nc.dram_tensor("wkb", [8, 128, 1024], F32, kind="ExternalInput"),
        "wvb": nc.dram_tensor("wvb", [8, 128, 1024], F32, kind="ExternalInput"),
        "bqc": nc.dram_tensor("bqc", [128, 8], F32, kind="ExternalInput"),
        "bkc": nc.dram_tensor("bkc", [128, 8], F32, kind="ExternalInput"),
        "bvb": nc.dram_tensor("bvb", [128, 1024], F32, kind="ExternalInput"),
        "wob": nc.dram_tensor("wob", [8, 128, 1024], F32, kind="ExternalInput"),
        "bob": nc.dram_tensor("bob", [128, 1024], F32, kind="ExternalInput"),
    }


def _dram_ffn(nc, L):
    s = str(L)
    return {
        "w1b": nc.dram_tensor("w1b" + s, [32, 128, 8, 128], F32, kind="ExternalInput"),
        "w2b": nc.dram_tensor("w2b" + s, [32, 128, 8, 128], F32, kind="ExternalInput"),
        "h1b": nc.dram_tensor("h1b" + s, [32, 128, 32, 128], F32, kind="ExternalInput"),
        "h2b": nc.dram_tensor("h2b" + s, [32, 128, 32, 128], F32, kind="ExternalInput"),
        "w3b": nc.dram_tensor("w3b" + s, [32, 128, 1024], F32, kind="ExternalInput"),
        "sc1": nc.dram_tensor("sc1" + s, [128, 32], F32, kind="ExternalInput"),
        "sc2": nc.dram_tensor("sc2" + s, [128, 32], F32, kind="ExternalInput"),
        "sch1": nc.dram_tensor("sch1" + s, [128, 32], F32, kind="ExternalInput"),
        "sch2": nc.dram_tensor("sch2" + s, [128, 32], F32, kind="ExternalInput"),
        "s3b": nc.dram_tensor("s3b" + s, [128, 1024], F32, kind="ExternalInput"),
    }



def _shared_and_init(nc, st, S, block, dcon, dide):
    sb = lambda n, shape, dt: st.enter_context(nc.sbuf_tensor("sh_" + n, shape, dt))
    sh = {
        "xchunk": [sb("xchunk0", [128, 1024], F32), sb("xchunk1", [128, 1024], F32)],
        "sqscr": sb("sqscr", [128, 1024], F32),
        "stats": sb("stats", [128, 16], F32),
        "consts": sb("consts", [128, 8], F32),
        "ident32": sb("ident32", [128, 128], F32),
        "identbf": sb("identbf", [128, 128], BF16),
        "xr4": sb("xr4", [128, 1024], F32),
        "outv": sb("outv", [128, 1024], F32),
        "outp": sb("outp", [128, 1024], F32),
    }
    S.pfx = "g:"
    S.inc("dX", n=2); S.setmark(("init",), "dX")
    S.inc("vS", ("identbf",))

    @block.sync
    def _(sync):
        sync.dma_start(out=sh["consts"][:, :], in_=dcon[:, :]).then_inc(S.sem["dX"], 16)
        sync.dma_start(out=sh["ident32"][:, :], in_=dide[:, :]).then_inc(S.sem["dX"], 16)

    @block.vector
    def _(v):
        S.wait(v, ("init",))
        v.tensor_copy(sh["identbf"][:, :], sh["ident32"][:, :]).then_inc(S.sem["vS"], 1)

    return sh


def build_fused_nc():
    nc = bass.Bass("TRN2", target_bir_lowering=False)
    dxin = nc.dram_tensor("xin", [TPC, D], F32, kind="ExternalInput")
    dx1 = nc.dram_tensor("x1", [TPC, D], F32)
    dx2 = nc.dram_tensor("x2", [TPC, D], F32)
    dx3 = nc.dram_tensor("x3", [TPC, D], F32)
    dxout = nc.dram_tensor("xout", [TPC, D], F32, kind="ExternalOutput")
    dcon = nc.dram_tensor("consts", [128, 8], F32, kind="ExternalInput")
    dide = nc.dram_tensor("ident", [128, 128], F32, kind="ExternalInput")
    dr_a0 = _dram_attn0(nc); dr_a0.update(xin=dxin, xout=dx1)
    dr_f0 = _dram_ffn(nc, 0); dr_f0.update(xin=dx1, xout=dx2)
    dr_l1 = _dram_attn1(nc); dr_l1.update(xin=dx2, xout=dx3)
    dr_f1 = _dram_ffn(nc, 1); dr_f1.update(xin=dx3, xout=dxout)

    S = Sched()
    with ExitStack() as st:
        for name in SEMS:
            S.sem[name] = st.enter_context(nc.semaphore(name))
        block = st.enter_context(nc.Block())
        sh = _shared_and_init(nc, st, S, block, dcon, dide)
        snap = S.snapshot()
        S.pfx = "a0:"
        emit_attn0(nc, block, S, sh, dr_a0, snap)
        snap = S.snapshot()
        S.pfx = "f0:"
        emit_ffn(nc, block, S, sh, dr_f0, snap)
        snap = S.snapshot()
        S.pfx = "l1:"
        emit_attn1(nc, block, S, sh, dr_l1, snap)
        snap = S.snapshot()
        S.pfx = "f1:"
        emit_ffn(nc, block, S, sh, dr_f1, snap)
    return nc



def _build_single(which, L=0):
    nc = bass.Bass("TRN2", target_bir_lowering=False)
    dxin = nc.dram_tensor("xin", [TPC, D], F32, kind="ExternalInput")
    dxout = nc.dram_tensor("xout", [TPC, D], F32, kind="ExternalOutput")
    dcon = nc.dram_tensor("consts", [128, 8], F32, kind="ExternalInput")
    dide = nc.dram_tensor("ident", [128, 128], F32, kind="ExternalInput")
    if which == "a0":
        dr = _dram_attn0(nc)
    elif which == "l1":
        dr = _dram_attn1(nc)
    else:
        dr = _dram_ffn(nc, L)
    dr.update(xin=dxin, xout=dxout)
    S = Sched()
    with ExitStack() as st:
        for name in SEMS:
            S.sem[name] = st.enter_context(nc.semaphore(name))
        block = st.enter_context(nc.Block())
        sh = _shared_and_init(nc, st, S, block, dcon, dide)
        snap = S.snapshot()
        S.pfx = which + ":"
        if which == "a0":
            emit_attn0(nc, block, S, sh, dr, snap)
        elif which == "l1":
            emit_attn1(nc, block, S, sh, dr, snap)
        else:
            emit_ffn(nc, block, S, sh, dr, snap)
    return nc



def kernel(**inputs):
    w = prep_shared(inputs)
    x = np.ascontiguousarray(np.asarray(inputs["L0_attn_x"], np.float32).reshape(NCORES * TPC, D))
    shared_w = {"consts": w["consts"], "ident": w["ident"]}
    shared_w.update(attn0_inputs(w))
    shared_w.update(ffn_inputs(w, 0))
    shared_w.update(attn1_inputs(w))
    shared_w.update(ffn_inputs(w, 1))

    nc = build_fused_nc()
    in_maps = []
    for c in range(NCORES):
        m = dict(shared_w)
        m["xin"] = np.ascontiguousarray(x[c * TPC:(c + 1) * TPC])
        in_maps.append(m)
    res = run_bass_kernel_spmd(nc, in_maps, core_ids=list(range(NCORES)))
    out = np.concatenate([res.results[c]["xout"] for c in range(NCORES)], axis=0)
    return out.reshape(16, 2048, D).astype(np.float32)


# version 05
# speedup vs baseline: 1.0365x; 1.0365x over previous
import math
import numpy as np
from contextlib import ExitStack

import concourse.bass as bass
import concourse.mybir as mybir
from concourse.bass_utils import run_bass_kernel_spmd

F32 = mybir.dt.float32
BF16 = mybir.dt.bfloat16
AF = mybir.ActivationFunctionType
ALU = mybir.AluOpType
AX = mybir.AxisListType

NCORES = 8
D = 1024
SEQ = 2048
TPC = 4096
HV = 16

C_INV_D, C_EPS, C_NEG1, C_QK8 = 0, 1, 2, 3



def _pos_encoding_T2():
    L, dim = SEQ, 64
    positions = np.arange(L, dtype=np.float32)[:, None]
    div_term = np.exp(np.arange(0, dim, 2, dtype=np.float32) * -(np.log(10000.0) / dim)).astype(np.float32)
    ang = (positions * div_term).astype(np.float32)
    pe = np.zeros((L, dim), dtype=np.float32)
    pe[:, 0::2] = np.sin(ang)
    pe[:, 1::2] = np.cos(ang)
    peT = np.ascontiguousarray(pe.T)
    return np.ascontiguousarray(np.concatenate([peT, peT], axis=0))


def _blk_lhsT(W):
    O, I = W.shape
    return np.ascontiguousarray(W.reshape(O // 128, 128, I // 128, 128).transpose(0, 3, 2, 1))


def _col_chunks(v):
    return np.ascontiguousarray(np.asarray(v, np.float32).reshape(-1, 128).T)


def _rep(v, n=128):
    v = np.asarray(v, np.float32).reshape(-1)
    return np.ascontiguousarray(np.broadcast_to(v.reshape(1, -1), (n, v.size)))


def prep_shared(inputs):
    f = lambda k: np.ascontiguousarray(np.asarray(inputs[k], np.float32))
    w = {}
    perm = np.empty(3072, np.int64)
    for part in range(3):
        for h in range(16):
            base = part * 1024 + h * 64
            perm[base:base + 64] = h * 192 + part * 64 + np.arange(64)
    wqkv = f("L0_attn_w_qkv")[:, perm]
    w["wqkvb"] = np.ascontiguousarray(wqkv.reshape(8, 128, 3072))
    w["bqkv"] = _col_chunks(f("L0_attn_b_qkv")[perm])
    w["woutb"] = np.ascontiguousarray(f("L0_attn_w_out").reshape(8, 128, 1024))
    w["boutb"] = _rep(f("L0_attn_b_out"))
    w["pos2"] = _pos_encoding_T2()
    w["onesb"] = np.ones((1, 128), np.float32)
    for nm, key in (("wq", "L1_attn_wq"), ("wk", "L1_attn_wk"), ("wv", "L1_attn_wv")):
        m = np.transpose(f(key), (1, 0, 2)).reshape(1024, 1024)
        w[nm + "b"] = np.ascontiguousarray(m.reshape(8, 128, 1024))
    w["bqc"] = _col_chunks(f("L1_attn_bq").reshape(-1))
    w["bkc"] = _col_chunks(f("L1_attn_bk").reshape(-1))
    w["bvb"] = _rep(f("L1_attn_bv").reshape(-1))
    w["wob"] = np.ascontiguousarray(f("L1_attn_wo").T.reshape(8, 128, 1024))
    w["bob"] = _rep(f("L1_attn_bo"))
    for L in (0, 1):
        p = f"L{L}_ffn_"
        w[f"w1b{L}"] = _blk_lhsT(f(p + "w1_w"))
        w[f"w2b{L}"] = _blk_lhsT(f(p + "w2_w"))
        w[f"h1b{L}"] = _blk_lhsT(f(p + "h1_w"))
        w[f"h2b{L}"] = _blk_lhsT(f(p + "h2_w"))
        w[f"w3b{L}"] = np.ascontiguousarray(f(p + "w3_w").T.reshape(32, 128, 1024))
        w[f"sc1{L}"] = _col_chunks(f(p + "w1_s"))
        w[f"sc2{L}"] = _col_chunks(f(p + "w2_s"))
        w[f"sch1{L}"] = _col_chunks(f(p + "h1_s"))
        w[f"sch2{L}"] = _col_chunks(f(p + "h2_s"))
        w[f"s3b{L}"] = _rep(f(p + "w3_s"))
    consts = np.zeros((128, 8), np.float32)
    consts[:, C_INV_D] = 1.0 / D
    consts[:, C_EPS] = 1e-5
    consts[:, C_NEG1] = -1.0
    consts[:, C_QK8] = 0.125
    w["consts"] = consts
    w["ident"] = np.eye(128, dtype=np.float32)
    return w



SEMS = ("dX", "dS", "gD", "tS", "aS", "vS", "pS")


DMA_SEMS = ("dX", "dS", "gD")


class Sched:
    def __init__(self):
        self.c = {n: 0 for n in SEMS}
        self.marks = {}
        self.sem = {}
        self.pfx = ""

    def _k(self, key):
        return (self.pfx,) + tuple(key)

    def inc(self, name, key=None, n=1):
        self.c[name] += n
        if key is not None:
            self.marks[self._k(key)] = (name, self.c[name])
        return self.c[name]

    def setmark(self, key, name):
        self.marks[self._k(key)] = (name, self.c[name])

    def alias(self, key, srckey):
        self.marks[self._k(key)] = self.marks[self._k(srckey)]

    def wait(self, eng, key):
        name, cnt = self.marks[self._k(key)]
        if name in DMA_SEMS:
            cnt *= 16
        eng.wait_ge(self.sem[name], cnt)

    def wait_max(self, eng, keys):
        best = {}
        for k in keys:
            name, cnt = self.marks[self._k(k)]
            best[name] = max(best.get(name, 0), cnt)
        for name, cnt in best.items():
            if name in DMA_SEMS:
                cnt *= 16
            eng.wait_ge(self.sem[name], cnt)

    def snapshot(self):
        return dict(self.c)

    def barrier(self, eng, snap):
        for name, cnt in snap.items():
            if cnt <= 0:
                continue
            eng.wait_ge(self.sem[name], cnt * 16 if name in DMA_SEMS else cnt)

    def final(self, name):
        return self.c[name]



def ln_prepass(S, pfx, G):
    S.inc("aS", (pfx, "sq", G))
    S.inc("vS", (pfx, "var", G))
    S.inc("aS", (pfx, "sqrt", G))
    S.inc("vS", (pfx, "rstdb", G))
    S.inc("vS", (pfx, "apply", G))
    S.inc("tS", (pfx, "tr", G))
    S.inc("vS", (pfx, "xevac", G))


def ln_emit_vector(v, S, pfx, G, xchunk, stats, consts, pTRb, xlnT_dst3d, xln_tok,
                   extra_apply_waits=()):
    b = 8 * (G % 2)
    S.wait(v, (pfx, "xload", G))
    v.reduce_sum(stats[:, b + 0:b + 1], xchunk[:, :], axis=AX.X)
    S.wait(v, (pfx, "sq", G))
    v.drain()
    v.tensor_scalar_mul(stats[:, b + 2:b + 3], stats[:, b + 0:b + 1], consts[:, C_INV_D:C_INV_D + 1])
    v.drain()
    v.tensor_mul(stats[:, b + 3:b + 4], stats[:, b + 2:b + 3], stats[:, b + 2:b + 3])
    v.drain()
    v.scalar_tensor_tensor(stats[:, b + 4:b + 5], stats[:, b + 1:b + 2],
                           consts[:, C_INV_D:C_INV_D + 1], stats[:, b + 3:b + 4],
                           ALU.mult, ALU.subtract).then_inc(S.sem["vS"], 1)
    S.wait(v, (pfx, "sqrt", G))
    v.reciprocal(stats[:, b + 6:b + 7], stats[:, b + 5:b + 6])
    v.drain()
    v.tensor_scalar(stats[:, b + 7:b + 8], stats[:, b + 2:b + 3], stats[:, b + 6:b + 7],
                    consts[:, C_NEG1:C_NEG1 + 1], ALU.mult, ALU.mult).then_inc(S.sem["vS"], 1)
    v.drain()
    if extra_apply_waits:
        S.wait_max(v, extra_apply_waits)
    v.tensor_scalar(xln_tok[:, :], xchunk[:, :], stats[:, b + 6:b + 7],
                    stats[:, b + 7:b + 8], ALU.mult, ALU.add).then_inc(S.sem["vS"], 1)
    S.wait(v, (pfx, "tr", G))
    v.tensor_copy(xlnT_dst3d, pTRb).then_inc(S.sem["vS"], 1)


def ln_emit_scalar(s, S, pfx, G, xchunk, stats, consts, sqscr):
    b = 8 * (G % 2)
    S.wait(s, (pfx, "xload", G))
    s.activation(sqscr[:, :], xchunk[:, :], AF.Square,
                 accum_out=stats[:, b + 1:b + 2]).then_inc(S.sem["aS"], 1)
    S.wait(s, (pfx, "var", G))
    s.activation(stats[:, b + 5:b + 6], stats[:, b + 4:b + 5], AF.Sqrt,
                 bias=consts[:, C_EPS:C_EPS + 1]).then_inc(S.sem["aS"], 1)



def ln_emit_tensor(t, S, pfx, G, xln_tok, identbf, pTRb_slots, wait_keys):
    S.wait_max(t, wait_keys + [(pfx, "apply", G)])
    ins = None
    for r in range(8):
        ins = t.transpose(pTRb_slots[r], xln_tok[:, r * 128:(r + 1) * 128], identbf[:, :])
    ins.then_inc(S.sem["tS"], 1)



def emit_ffn(nc, block, S, sh, dr, snap):
    FT = 1024
    NT = TPC // FT
    dxin, dxout = dr["xin"], dr["xout"]
    dw1, dw2, dh1, dh2, dw3 = dr["w1b"], dr["w2b"], dr["h1b"], dr["h2b"], dr["w3b"]
    dsc1, dsc2, dsh1, dsh2, ds3 = dr["sc1"], dr["sc2"], dr["sch1"], dr["sch2"], dr["s3b"]
    consts, ident32, identbf = sh["consts"], sh["ident32"], sh["identbf"]
    xchunk, sqscr, stats = sh["xchunk"], sh["sqscr"], sh["stats"]
    xr4, outv, outp = sh["xr4"], sh["outv"], sh["outp"]
    S.inc("dX", n=5); S.setmark(("init",), "dX")
    for i in range(NT):
        for tc in range(8):
            G = i * 8 + tc
            S.inc("dX", ("f", "xload", G))
            ln_prepass(S, "f", G)
        for c in range(32):
            bi = i * 32 + c
            S.inc("gD", n=2); S.setmark(("wB", bi), "gD")
            S.inc("tS", ("mmB", bi))
            S.inc("aS", ("sil", bi))
            S.inc("vS", ("g1", bi))
        for st_ in range(2):
            for c in range(32):
                gi = i * 64 + st_ * 32 + c
                S.inc("gD", ("wCDh", 2 * gi))
                S.inc("gD", ("wCDh", 2 * gi + 1))
                S.inc("tS", ("mmCDh", 2 * gi))
                S.inc("tS", ("mmCD", gi))
                S.alias(("mmCDh", 2 * gi + 1), ("mmCD", gi))
                S.inc("aS", ("gelu", gi))
        for half in range(2):
            for r in range(32):
                ei = (i * 2 + half) * 32 + r
                S.inc("gD", ("wE", ei))
                S.inc("tS", ("mmE", ei))
            for t4 in range(4):
                idx = i * 8 + half * 4 + t4
                S.inc("dX", ("xr", idx))
                S.inc("vS", ("ev1", idx))
                S.inc("pS", ("ev2", idx))
                S.inc("dS", ("store", idx))

    with ExitStack() as st:
        sb = lambda n, shape, dt: st.enter_context(nc.sbuf_tensor("sb_" + S.pfx + n, shape, dt))
        xln_tok = [sb("xlntok0", [128, 1024], BF16), sb("xlntok1", [128, 1024], BF16)]
        xlnT = sb("xlnT", [128, 8 * 1024], BF16)
        g_a = sb("g_a", [128, 32 * 1024], BF16)
        g_b = sb("g_b", [128, 32 * 1024], BF16)
        wb1 = [sb("wb1_0", [128, 1024], BF16), sb("wb1_1", [128, 1024], BF16)]
        wb2 = [sb("wb2_0", [128, 1024], BF16), sb("wb2_1", [128, 1024], BF16)]
        wbig = [sb("wbig0", [128, 2048], BF16), sb("wbig1", [128, 2048], BF16)]
        tsil = [sb("tsil0", [128, 1024], BF16), sb("tsil1", [128, 1024], BF16)]
        sc1 = sb("sc1", [128, 32], F32)
        sc2 = sb("sc2", [128, 32], F32)
        sch1 = sb("sch1", [128, 32], F32)
        sch2 = sb("sch2", [128, 32], F32)
        s3b = sb("s3b", [128, 1024], F32)
        pA = st.enter_context(nc.psum_tensor(S.pfx + "pA", [128, 1024], F32))
        pB = st.enter_context(nc.psum_tensor(S.pfx + "pB", [128, 1024], F32))
        pC = st.enter_context(nc.psum_tensor(S.pfx + "pC", [128, 1024], F32))
        pE = st.enter_context(nc.psum_tensor(S.pfx + "pE", [128, 1024], F32))
        psums = [pA, pB, pC, pE]

        pTRb = pE[:, :].bitcast(BF16)
        pTR_slots = [pTRb[:, r * 128:(r + 1) * 128] for r in range(8)]

        def xlnT_dst3d(tc):
            return xlnT[:, :].rearrange("p (r t) -> p r t", t=1024)[:, :, tc * 128:(tc + 1) * 128]

        def pTR_src3d():
            return pTRb.rearrange("p (r t) -> p r t", t=128)[:, 0:8, :]

        @block.sync
        def _(sync):
            S.barrier(sync, snap)
            for dst, srcdram in ((sc1, dsc1), (sc2, dsc2),
                                 (sch1, dsh1), (sch2, dsh2), (s3b, ds3)):
                sync.dma_start(out=dst[:, :], in_=srcdram[:, :]).then_inc(S.sem["dX"], 16)
            for i in range(NT):
                for tc in range(8):
                    G = i * 8 + tc
                    if G >= 2:
                        S.wait(sync, ("f", "apply", G - 2))
                    sync.dma_start(out=xchunk[G % 2][:, :],
                                   in_=dxin[G * 128:(G + 1) * 128, :]).then_inc(S.sem["dX"], 16)
                for half in range(2):
                    for t4 in range(4):
                        idx = i * 8 + half * 4 + t4
                        if idx >= 1:
                            S.wait(sync, ("ev2", idx - 1))
                        row0 = i * 1024 + (half * 4 + t4) * 128
                        sync.dma_start(out=xr4[:, :],
                                       in_=dxin[row0:row0 + 128, :]).then_inc(S.sem["dX"], 16)
                        S.wait(sync, ("ev2", idx))
                        sync.dma_start(out=dxout[row0:row0 + 128, :],
                                       in_=outp[:, :]).then_inc(S.sem["dS"], 16)
            sync.wait_ge(S.sem["dS"], 16 * S.final("dS"))

        @block.gpsimd
        def _(g):
            S.barrier(g, snap)
            for i in range(NT):
                for c in range(32):
                    bi = i * 32 + c
                    if c >= 2:
                        S.wait(g, ("mmB", bi - 2))
                    elif i >= 1:
                        S.wait(g, ("mmE", (i - 1) * 64 + 62 + c))
                    g.dma_start(out=wb1[c % 2][:, :],
                                in_=dw1[c].rearrange("p r j -> p (r j)")).then_inc(S.sem["gD"], 16)
                    g.dma_start(out=wb2[c % 2][:, :],
                                in_=dw2[c].rearrange("p r j -> p (r j)")).then_inc(S.sem["gD"], 16)
                for st_ in range(2):
                    dh = dh1 if st_ == 0 else dh2
                    for c in range(32):
                        gi = i * 64 + st_ * 32 + c
                        for hp in range(2):
                            hi = 2 * gi + hp
                            if hi >= 2:
                                S.wait(g, ("mmCDh", hi - 2))
                            g.dma_start(out=wbig[hi % 2][:, :],
                                        in_=dh[c].rearrange("p r j -> p (r j)")[:, hp * 2048:(hp + 1) * 2048]
                                        ).then_inc(S.sem["gD"], 16)
                for half in range(2):
                    for r in range(32):
                        ei = (i * 2 + half) * 32 + r
                        if half == 0 and r < 2:
                            S.wait(g, ("mmB", i * 32 + 30 + r))
                        else:
                            S.wait(g, ("mmE", ei - 2))
                        g.dma_start(out=wb1[ei % 2][:, :],
                                    in_=dw3[r][:, :]).then_inc(S.sem["gD"], 16)
                    for t4 in range(4):
                        idx = i * 8 + half * 4 + t4
                        S.wait_max(g, [("ev1", idx), ("xr", idx)] +
                                   ([("store", idx - 1)] if idx >= 1 else []))
                        g.tensor_add(outp[:, :], outv[:, :],
                                     xr4[:, :]).then_inc(S.sem["pS"], 1)

        @block.tensor
        def _(t):
            S.barrier(t, snap)
            for i in range(NT):
                for tc in range(8):
                    G = i * 8 + tc
                    waits = []
                    if G >= 1:
                        waits.append(("f", "xevac", G - 1))
                    if i >= 1:
                        waits.append(("ev1", i * 8 - 1))
                    ln_emit_tensor(t, S, "f", G, xln_tok[G % 2], identbf, pTR_slots, waits)
                for c in range(32):
                    bi = i * 32 + c
                    pX, pY = (pA, pB) if c % 2 == 0 else (pC, pE)
                    waits = [("wB", bi)]
                    if c == 0:
                        waits.append(("f", "xevac", i * 8 + 7))
                    if bi >= 2:
                        waits += [("sil", bi - 2), ("g1", bi - 2)]
                    if c < 2 and i >= 1:
                        waits.append(("ev1", i * 8 - 1))
                    S.wait_max(t, waits)
                    ins = None
                    for r in range(8):
                        for n in range(2):
                            ins = t.matmul(pX[:, n * 512:(n + 1) * 512],
                                           wb1[c % 2][:, r * 128:(r + 1) * 128],
                                           xlnT[:, r * 1024 + n * 512: r * 1024 + (n + 1) * 512],
                                           start=(r == 0), stop=(r == 7))
                    for r in range(8):
                        for n in range(2):
                            ins = t.matmul(pY[:, n * 512:(n + 1) * 512],
                                           wb2[c % 2][:, r * 128:(r + 1) * 128],
                                           xlnT[:, r * 1024 + n * 512: r * 1024 + (n + 1) * 512],
                                           start=(r == 0), stop=(r == 7))
                    ins.then_inc(S.sem["tS"], 1)
                for st_ in range(2):
                    gsrc = g_a if st_ == 0 else g_b
                    for c in range(32):
                        gi = i * 64 + st_ * 32 + c
                        pX = pA if c % 2 == 0 else pC
                        waits = [("wCDh", 2 * gi)]
                        if st_ == 0 and c == 0:
                            waits.append(("g1", i * 32 + 31))
                        if st_ == 1 and c == 0:
                            waits.append(("gelu", i * 64 + 31))
                        if gi >= 2:
                            waits.append(("gelu", gi - 2))
                        if c < 2 and st_ == 0:
                            waits += [("sil", i * 32 + 30 + c)]
                        S.wait_max(t, waits)
                        ins = None
                        for r in range(32):
                            if r == 16:
                                ins.then_inc(S.sem["tS"], 1)
                                S.wait(t, ("wCDh", 2 * gi + 1))
                            for n in range(2):
                                ins = t.matmul(pX[:, n * 512:(n + 1) * 512],
                                               wbig[(2 * gi + r // 16) % 2][:, (r % 16) * 128:(r % 16 + 1) * 128],
                                               gsrc[:, r * 1024 + n * 512: r * 1024 + (n + 1) * 512],
                                               start=(r == 0), stop=(r == 31))
                        ins.then_inc(S.sem["tS"], 1)
                for half in range(2):
                    if half == 1:
                        S.wait_max(t, [("ev1", i * 8 + 3)])
                    for r in range(32):
                        ei = (i * 2 + half) * 32 + r
                        waits = [("wE", ei), ("gelu", i * 64 + 32 + r)]
                        if half == 0 and r == 0:
                            waits += [("gelu", i * 64 + 63), ("g1", i * 32 + 31)]
                        S.wait_max(t, waits)
                        ins = None
                        for t4 in range(4):
                            tloc = half * 4 + t4
                            for n in range(2):
                                ins = t.matmul(psums[t4][:, n * 512:(n + 1) * 512],
                                               g_a[:, r * 1024 + tloc * 128: r * 1024 + (tloc + 1) * 128],
                                               wb1[ei % 2][:, n * 512:(n + 1) * 512],
                                               start=(r == 0), stop=(r == 31))
                        ins.then_inc(S.sem["tS"], 1)

        @block.vector
        def _(v):
            S.barrier(v, snap)
            S.wait(v, ("init",))
            for i in range(NT):
                for tc in range(8):
                    G = i * 8 + tc
                    extra = [("f", "tr", G - 2)] if G >= 2 else []
                    ln_emit_vector(v, S, "f", G, xchunk[G % 2], stats, consts,
                                   pTR_src3d(), xlnT_dst3d(tc), xln_tok[G % 2], extra)
                for c in range(32):
                    bi = i * 32 + c
                    pY = pB if c % 2 == 0 else pE
                    S.wait(v, ("sil", bi))
                    v.scalar_tensor_tensor(g_a[:, c * 1024:(c + 1) * 1024], pY[:, :],
                                           sc2[:, c:c + 1], tsil[c % 2][:, :],
                                           ALU.mult, ALU.mult).then_inc(S.sem["vS"], 1)
                for half in range(2):
                    for t4 in range(4):
                        idx = i * 8 + half * 4 + t4
                        waits = [("mmE", (i * 2 + half) * 32 + 31)]
                        if idx >= 1:
                            waits.append(("ev2", idx - 1))
                        S.wait_max(v, waits)
                        v.tensor_mul(outv[:, :],
                                     psums[t4][:, :], s3b[:, :]).then_inc(S.sem["vS"], 1)

        @block.scalar
        def _(s):
            S.barrier(s, snap)
            S.wait(s, ("init",))
            for i in range(NT):
                for tc in range(8):
                    G = i * 8 + tc
                    ln_emit_scalar(s, S, "f", G, xchunk[G % 2], stats, consts, sqscr)
                for c in range(32):
                    bi = i * 32 + c
                    pX = pA if c % 2 == 0 else pC
                    waits = [("mmB", bi)]
                    if bi >= 2:
                        waits.append(("g1", bi - 2))
                    S.wait_max(s, waits)
                    s.activation(tsil[c % 2][:, :], pX[:, :], AF.Silu,
                                 scale=sc1[:, c:c + 1]).then_inc(S.sem["aS"], 1)
                for st_ in range(2):
                    gdst = g_b if st_ == 0 else g_a
                    sch = sch1 if st_ == 0 else sch2
                    for c in range(32):
                        gi = i * 64 + st_ * 32 + c
                        pX = pA if c % 2 == 0 else pC
                        S.wait(s, ("mmCD", gi))
                        s.activation(gdst[:, c * 1024:(c + 1) * 1024], pX[:, :], AF.Gelu,
                                     scale=sch[:, c:c + 1]).then_inc(S.sem["aS"], 1)



def emit_attn0(nc, block, S, sh, dr, snap):
    AT = 512
    NTI = TPC // AT
    dxin, dxout = dr["xin"], dr["xout"]
    dwqkv, dbqkv, dwout, dbout, dpos = dr["wqkvb"], dr["bqkv"], dr["woutb"], dr["boutb"], dr["pos2"]
    dones = dr["onesb"]
    consts, ident32, identbf = sh["consts"], sh["ident32"], sh["identbf"]
    xchunk, sqscr, stats = sh["xchunk"], sh["sqscr"], sh["stats"]
    xr4, outv, outp = sh["xr4"], sh["outv"], sh["outp"]
    S.inc("dX", n=4); S.setmark(("init",), "dX")
    S.inc("gD", n=16); S.setmark(("winit",), "gD")
    S.inc("vS", ("onesbf",))
    for ti in range(NTI):
        for tc in range(4):
            G = ti * 4 + tc
            S.inc("dX", ("a", "xload", G))
            ln_prepass(S, "a", G)
        for oc in range(24):
            S.inc("tS", ("mmQ", ti, oc))
            S.inc("vS", ("qev", ti, oc))
        for si in range(3):
            for tc in range(4):
                S.inc("tS", ("trQ", ti, si, tc))
                S.inc("aS", ("tevQ", ti, si, tc))
        for tc in range(4):
            S.inc("pS", ("qkp", ti, tc))
            S.inc("vS", ("nm8", ti, tc))
            S.inc("aS", ("exp", ti, tc))
            S.inc("pS", ("pvp", ti, tc))
            S.inc("vS", ("otok", ti, tc))
        for tc in range(4):
            S.inc("tS", ("trO", ti, tc))
            S.inc("aS", ("tevO", ti, tc))
        for tc in range(4):
            idx = ti * 4 + tc
            S.inc("tS", ("mmW", ti, tc))
            S.inc("dX", ("wxr", idx))
            S.inc("vS", ("oadd", idx))
            S.inc("dS", ("store", idx))

    with ExitStack() as st:
        sb = lambda n, shape, dt: st.enter_context(nc.sbuf_tensor("sb_" + S.pfx + n, shape, dt))
        xln_tok = [sb("xlntok0", [128, 1024], BF16), sb("xlntok1", [128, 1024], BF16)]
        xlnT = sb("xlnT", [128, 4096], BF16)
        wqkv_sb = sb("wqkv", [128, 8 * 3072], BF16)
        wout_sb = sb("wout", [128, 8 * 1024], BF16)
        bqkv = sb("bqkv", [128, 24], F32)
        boutb = sb("boutb", [128, 1024], F32)
        pos2 = sb("pos2", [128, 2048], F32)
        qT = sb("qT", [128, 4096], BF16)
        kT = sb("kT", [128, 4096], BF16)
        vT = sb("vT", [128, 4096], BF16)
        q_tok = sb("q_tok", [128, 4096], BF16)
        k_tok = sb("k_tok", [128, 4096], BF16)
        v_tok = sb("v_tok", [128, 4096], BF16)
        prod = sb("prod", [128, 1024], BF16)
        prod2 = sb("prod2", [128, 1024], BF16)
        prodP = sb("prodP", [128, 1024], F32)
        prodP2 = sb("prodP2", [128, 1024], F32)
        Sm = sb("Sm", [128, 256], F32)
        Ee = sb("Ee", [128, 256], BF16)
        esum = sb("esum", [128, 16], F32)
        rec = sb("rec", [128, 16], F32)
        nm8 = sb("nm8", [128, 16], F32)
        Oraw = sb("Oraw", [128, 1024], F32)
        O_bf = sb("O_bf", [128, 4096], BF16)
        OT = sb("OT", [128, 4096], BF16)
        ones1 = sb("ones1", [1, 128], F32)
        onesbf = sb("onesbf", [1, 128], BF16)
        bout_bf = sb("bout_bf", [1, 1024], BF16)
        pQ2 = st.enter_context(nc.psum_tensor(S.pfx + "pQ2", [128, 1024], F32))
        pTR = st.enter_context(nc.psum_tensor(S.pfx + "pTR", [128, 1024], F32))
        pW = [st.enter_context(nc.psum_tensor(S.pfx + "pW0", [128, 1024], F32)),
              st.enter_context(nc.psum_tensor(S.pfx + "pW1", [128, 1024], F32))]

        pTRb = pTR[:, :].bitcast(BF16)
        pTR_slots = [pTRb[:, r * 128:(r + 1) * 128] for r in range(8)]

        def xlnT_dst3d(tc):
            return xlnT[:, :].rearrange("p (r t) -> p r t", t=512)[:, :, tc * 128:(tc + 1) * 128]

        def pTR_src3d():
            return pTRb.rearrange("p (r t) -> p r t", t=128)[:, 0:8, :]

        toks = None

        @block.sync
        def _(sync):
            S.barrier(sync, snap)
            for dst, srcdram in ((bqkv, dbqkv), (boutb, dbout), (pos2, dpos),
                                 (ones1, dones)):
                sync.dma_start(out=dst[:, :], in_=srcdram[:, :]).then_inc(S.sem["dX"], 16)
            for ti in range(NTI):
                for tc in range(4):
                    G = ti * 4 + tc
                    if G >= 2:
                        S.wait(sync, ("a", "apply", G - 2))
                    sync.dma_start(out=xchunk[G % 2][:, :],
                                   in_=dxin[G * 128:(G + 1) * 128, :]).then_inc(S.sem["dX"], 16)
                for tc in range(4):
                    idx = ti * 4 + tc
                    if idx >= 1:
                        S.wait(sync, ("oadd", idx - 1))
                    row0 = idx * 128
                    sync.dma_start(out=xr4[:, :],
                                   in_=dxin[row0:row0 + 128, :]).then_inc(S.sem["dX"], 16)
                    S.wait(sync, ("oadd", idx))
                    sync.dma_start(out=dxout[row0:row0 + 128, :],
                                   in_=outp[:, :]).then_inc(S.sem["dS"], 16)
            sync.wait_ge(S.sem["dS"], 16 * S.final("dS"))

        @block.gpsimd
        def _(g):
            S.barrier(g, snap)
            for r in range(8):
                g.dma_start(out=wqkv_sb[:, r * 3072:(r + 1) * 3072],
                            in_=dwqkv[r][:, :]).then_inc(S.sem["gD"], 16)
            for r in range(8):
                g.dma_start(out=wout_sb[:, r * 1024:(r + 1) * 1024],
                            in_=dwout[r][:, :]).then_inc(S.sem["gD"], 16)
            for ti in range(NTI):
                for tc in range(4):
                    waits = []
                    if tc == 0:
                        waits += [("tevQ", ti, 0, 3), ("tevQ", ti, 1, 3)]
                    if ti > 0 or tc > 0:
                        pt, pc = (ti, tc - 1) if tc > 0 else (ti - 1, 3)
                        waits.append(("exp", pt, pc))
                    if waits:
                        S.wait_max(g, waits)
                    k3 = k_tok[:, tc * 1024:(tc + 1) * 1024].rearrange("p (g d) -> p g d", d=64)
                    p3 = prodP[:, :].rearrange("p (g d) -> p g d", d=64)
                    for h in range(HV, 16):
                        q1 = q_tok[:, tc * 1024 + h * 64: tc * 1024 + (h + 1) * 64]
                        g.tensor_tensor(p3, q1.unsqueeze(1).broadcast_to([128, 16, 64]),
                                        k3, ALU.mult)
                        g.drain()
                        w = 32
                        while w >= 2:
                            g.tensor_add(p3[:, :, 0:w], p3[:, :, 0:w], p3[:, :, w:2 * w])
                            g.drain()
                            w //= 2
                        g.tensor_add(Sm[:, h * 16:(h + 1) * 16].unsqueeze(2),
                                     p3[:, :, 0:1], p3[:, :, 1:2])
                        g.drain()
                    g.engine_nop().then_inc(S.sem["pS"], 1)
                    waits = [("exp", ti, tc)]
                    if tc == 0:
                        waits.append(("tevQ", ti, 2, 3))
                    if ti > 0 or tc > 0:
                        pt, pc = (ti, tc - 1) if tc > 0 else (ti - 1, 3)
                        waits.append(("otok", pt, pc))
                    S.wait_max(g, waits)
                    v3 = v_tok[:, tc * 1024:(tc + 1) * 1024].rearrange("p (d g) -> p d g", g=16)
                    p32 = prodP2[:, :].rearrange("p (d g) -> p d g", g=16)
                    for h in range(HV, 16):
                        e1 = Ee[:, h * 16:(h + 1) * 16]
                        g.tensor_tensor(p32, e1.unsqueeze(1).broadcast_to([128, 64, 16]),
                                        v3, ALU.mult)
                        g.drain()
                        w = 8
                        while w >= 2:
                            g.tensor_add(p32[:, :, 0:w], p32[:, :, 0:w], p32[:, :, w:2 * w])
                            g.drain()
                            w //= 2
                        g.tensor_add(Oraw[:, h * 64:(h + 1) * 64].unsqueeze(2),
                                     p32[:, :, 0:1], p32[:, :, 1:2])
                        g.drain()
                    g.engine_nop().then_inc(S.sem["pS"], 1)

        @block.tensor
        def _(t):
            S.barrier(t, snap)
            ptr_prev = []
            for ti in range(NTI):
                for tc in range(4):
                    G = ti * 4 + tc
                    waits = list(ptr_prev)
                    ln_emit_tensor(t, S, "a", G, xln_tok[G % 2], identbf, pTR_slots, waits)
                    ptr_prev = [("a", "xevac", G)]
                for oc in range(24):
                    waits = []
                    if oc == 0:
                        waits += [("a", "xevac", ti * 4 + 3), ("winit",)]
                    if oc >= 2:
                        waits.append(("qev", ti, oc - 2))
                    if waits:
                        S.wait_max(t, waits)
                    ins = None
                    for r in range(8):
                        ins = t.matmul(pQ2[:, (oc % 2) * 512:(oc % 2) * 512 + 512],
                                       wqkv_sb[:, r * 3072 + oc * 128: r * 3072 + (oc + 1) * 128],
                                       xlnT[:, r * 512:(r + 1) * 512],
                                       start=(r == 0), stop=(r == 7))
                    ins.then_inc(S.sem["tS"], 1)
                for si, srcT in enumerate((qT, kT, vT)):
                    for tc in range(4):
                        waits = list(ptr_prev)
                        if tc == 0:
                            waits.append(("qev", ti, si * 8 + 7))
                        S.wait_max(t, waits)
                        ins = None
                        for j in range(8):
                            ins = t.transpose(pTR_slots[j],
                                              srcT[:, j * 512 + tc * 128: j * 512 + (tc + 1) * 128],
                                              identbf[:, :])
                        ins.then_inc(S.sem["tS"], 1)
                        ptr_prev = [("tevQ", ti, si, tc)]
                for tc in range(4):
                    waits = list(ptr_prev) + [("otok", ti, tc)]
                    S.wait_max(t, waits)
                    ins = None
                    for j in range(8):
                        ins = t.transpose(pTR_slots[j],
                                          O_bf[:, tc * 1024 + j * 128: tc * 1024 + (j + 1) * 128],
                                          identbf[:, :])
                    ins.then_inc(S.sem["tS"], 1)
                    ptr_prev = [("tevO", ti, tc)]
                for tc in range(4):
                    idx = ti * 4 + tc
                    waits = [("tevO", ti, tc)]
                    if idx == 0:
                        waits.append(("onesbf",))
                    if idx >= 2:
                        waits.append(("oadd", idx - 2))
                    S.wait_max(t, waits)
                    ins = None
                    for n in range(2):
                        ins = t.matmul(pW[tc % 2][:, n * 512:(n + 1) * 512],
                                       onesbf[0:1, :],
                                       bout_bf[0:1, n * 512:(n + 1) * 512],
                                       start=True, stop=False)
                    for r in range(8):
                        for n in range(2):
                            ins = t.matmul(pW[tc % 2][:, n * 512:(n + 1) * 512],
                                           OT[:, r * 512 + tc * 128: r * 512 + (tc + 1) * 128],
                                           wout_sb[:, r * 1024 + n * 512: r * 1024 + (n + 1) * 512],
                                           start=False, stop=(r == 7))
                    ins.then_inc(S.sem["tS"], 1)

        @block.vector
        def _(v):
            S.barrier(v, snap)
            S.wait(v, ("init",))
            v.tensor_copy(onesbf[0:1, :], ones1[0:1, :])
            v.drain()
            v.tensor_copy(bout_bf[0:1, :], boutb[0:1, :]).then_inc(S.sem["vS"], 1)
            for ti in range(NTI):
                for tc in range(4):
                    G = ti * 4 + tc
                    extra = [("a", "tr", G - 2)] if G >= 2 else []
                    ln_emit_vector(v, S, "a", G, xchunk[G % 2], stats, consts,
                                   pTR_src3d(), xlnT_dst3d(tc), xln_tok[G % 2], extra)
                for oc in range(24):
                    S.wait(v, ("mmQ", ti, oc))
                    ps = pQ2[:, (oc % 2) * 512:(oc % 2) * 512 + 512]
                    if oc < 8:
                        v.tensor_scalar_add(qT[:, oc * 512:(oc + 1) * 512], ps,
                                            bqkv[:, oc:oc + 1]).then_inc(S.sem["vS"], 1)
                    elif oc < 16:
                        j = oc - 8
                        v.scalar_tensor_tensor(kT[:, j * 512:(j + 1) * 512], ps,
                                               bqkv[:, oc:oc + 1],
                                               pos2[:, (ti % 4) * 512:(ti % 4 + 1) * 512],
                                               ALU.add, ALU.add).then_inc(S.sem["vS"], 1)
                    else:
                        j = oc - 16
                        v.tensor_scalar_add(vT[:, j * 512:(j + 1) * 512], ps,
                                            bqkv[:, oc:oc + 1]).then_inc(S.sem["vS"], 1)
                for tc in range(4):
                    if tc == 0:
                        S.wait_max(v, [("tevQ", ti, 0, 3), ("tevQ", ti, 1, 3)])
                    k3 = k_tok[:, tc * 1024:(tc + 1) * 1024].rearrange("p (g d) -> p g d", d=64)
                    for h in range(HV):
                        q1 = q_tok[:, tc * 1024 + h * 64: tc * 1024 + (h + 1) * 64]
                        v.tensor_tensor(prod[:, :].rearrange("p (g d) -> p g d", d=64),
                                        q1.unsqueeze(1).broadcast_to([128, 16, 64]), k3, ALU.mult)
                        v.drain()
                        v.tensor_reduce(Sm[:, h * 16:(h + 1) * 16],
                                        prod[:, :].rearrange("p (g d) -> p g d", d=64),
                                        axis=AX.X, op=ALU.add)
                        v.drain()
                    S.wait(v, ("qkp", ti, tc))
                    v.tensor_reduce(nm8[:, :], Sm[:, :].rearrange("p (h g) -> p h g", g=16),
                                    axis=AX.X, op=ALU.max, negate=True)
                    v.drain()
                    v.tensor_scalar_mul(nm8[:, :], nm8[:, :], consts[:, C_QK8:C_QK8 + 1])
                    v.drain()
                    v.engine_nop().then_inc(S.sem["vS"], 1)
                    S.wait(v, ("exp", ti, tc))
                    v.reciprocal(rec[:, :], esum[:, :])
                    v.drain()
                    v3 = v_tok[:, tc * 1024:(tc + 1) * 1024].rearrange("p (d g) -> p d g", g=16)
                    if tc == 0:
                        S.wait(v, ("tevQ", ti, 2, 3))
                    for h in range(HV):
                        e1 = Ee[:, h * 16:(h + 1) * 16]
                        v.tensor_tensor(prod2[:, :].rearrange("p (d g) -> p d g", g=16),
                                        e1.unsqueeze(1).broadcast_to([128, 64, 16]), v3, ALU.mult)
                        v.drain()
                        v.tensor_reduce(Oraw[:, h * 64:(h + 1) * 64],
                                        prod2[:, :].rearrange("p (d g) -> p d g", g=16),
                                        axis=AX.X, op=ALU.add)
                        v.drain()
                    S.wait(v, ("pvp", ti, tc))
                    if ti >= 1:
                        S.wait(v, ("trO", ti - 1, tc))
                    v.tensor_tensor(O_bf[:, tc * 1024:(tc + 1) * 1024].rearrange("p (h d) -> p h d", d=64),
                                    Oraw[:, :].rearrange("p (h d) -> p h d", d=64),
                                    rec[:, :].unsqueeze(2).broadcast_to([128, 16, 64]),
                                    ALU.mult).then_inc(S.sem["vS"], 1)
                for tc in range(4):
                    idx = ti * 4 + tc
                    waits = [("mmW", ti, tc), ("wxr", idx)]
                    if idx >= 1:
                        waits.append(("store", idx - 1))
                    S.wait_max(v, waits)
                    v.tensor_add(outp[:, :], pW[tc % 2][:, :], xr4[:, :]).then_inc(S.sem["vS"], 1)

        @block.scalar
        def _(s):
            S.barrier(s, snap)
            for ti in range(NTI):
                for tc in range(4):
                    G = ti * 4 + tc
                    ln_emit_scalar(s, S, "a", G, xchunk[G % 2], stats, consts, sqscr)
                for si, dtok in enumerate((q_tok, k_tok, v_tok)):
                    for tc in range(4):
                        S.wait(s, ("trQ", ti, si, tc))
                        if si < 2:
                            dst3 = dtok[:, tc * 1024:(tc + 1) * 1024].rearrange("p (j k) -> p j k", k=128)
                            s.copy(dst3, pTR_src3d()).then_inc(S.sem["aS"], 1)
                        else:
                            dst3 = dtok[:, tc * 1024:(tc + 1) * 1024].rearrange(
                                "p (d g) -> p g d", g=16)
                            src3 = pTRb.rearrange("p (j d) -> p j d", d=64)[:, 0:16, :]
                            s.copy(dst3, src3).then_inc(S.sem["aS"], 1)
                for tc in range(4):
                    waits = [("nm8", ti, tc)]
                    if ti > 0 or tc > 0:
                        pt, pc = (ti, tc - 1) if tc > 0 else (ti - 1, 3)
                        waits.append(("pvp", pt, pc))
                    S.wait_max(s, waits)
                    ins = None
                    for h in range(16):
                        ins = s.activation(Ee[:, h * 16:(h + 1) * 16], Sm[:, h * 16:(h + 1) * 16],
                                           AF.Exp, scale=consts[:, C_QK8:C_QK8 + 1],
                                           bias=nm8[:, h:h + 1], accum_out=esum[:, h:h + 1])
                    ins.then_inc(S.sem["aS"], 1)
                for tc in range(4):
                    S.wait(s, ("trO", ti, tc))
                    dst3 = OT[:, :].rearrange("p (j t) -> p j t", t=512)[:, :, tc * 128:(tc + 1) * 128]
                    s.copy(dst3, pTR_src3d()).then_inc(S.sem["aS"], 1)


def attn0_inputs(w, xcore):
    return {"xin": xcore, "wqkvb": w["wqkvb"], "bqkv": w["bqkv"], "woutb": w["woutb"],
            "boutb": w["boutb"], "pos2": w["pos2"], "consts": w["consts"], "ident": w["ident"]}



def emit_attn1(nc, block, S, sh, dr, snap):
    NSEQ = TPC // SEQ
    NCH = SEQ // 128
    dxin, dxout = dr["xin"], dr["xout"]
    dwq, dwk, dwv = dr["wqb"], dr["wkb"], dr["wvb"]
    dbq, dbk, dbv, dwo, dbo = dr["bqc"], dr["bkc"], dr["bvb"], dr["wob"], dr["bob"]
    consts, ident32, identbf = sh["consts"], sh["ident32"], sh["identbf"]
    xchunk, sqscr, stats = sh["xchunk"], sh["sqscr"], sh["stats"]
    xr4, outv, outp = sh["xr4"], sh["outv"], sh["outp"]
    S.inc("dX", n=4); S.setmark(("init",), "dX")
    for q in range(NSEQ):
        for cc in range(NCH):
            G = q * NCH + cc
            S.inc("dX", ("l", "xload", G))
            ln_prepass(S, "l", G)
        for stg in range(4):
            S.inc("gD", n=8); S.setmark(("wld", q, stg), "gD")
        for stg in range(2):
            for oc in range(8):
                S.inc("tS", ("mmP", q, stg, oc))
                S.inc("vS", ("pev", q, stg, oc))
        for tc in range(16):
            S.inc("tS", ("mmV", q, tc))
            S.inc("vS", ("vev", q, tc))
        first, last = q * 256, q * 256 + 255
        S.inc("tS", ("mmS", first))
        for ai in range(first, last + 1):
            S.inc("vS", ("nms", ai))
            S.inc("aS", ("exp", ai))
            if ai > first:
                S.inc("tS", ("trB", ai - 1))
                S.inc("aS", ("tevB", ai - 1))
            S.inc("tS", ("trP", ai))
            S.inc("aS", ("tevPa", ai))
            S.inc("vS", ("tevPd", ai))
            if ai < last:
                S.inc("tS", ("mmS", ai + 1))
            S.inc("tS", ("mmO", ai))
            S.inc("vS", ("oev", ai))
        S.inc("tS", ("trB", last))
        S.inc("aS", ("tevB", last))
        for sc in range(16):
            oi = q * 16 + sc
            S.inc("tS", ("mmF", oi))
            S.inc("dX", ("oxr", oi))
            S.inc("vS", ("oev1", oi))
            S.inc("pS", ("oev2", oi))
            S.inc("dS", ("store", oi))

    with ExitStack() as st:
        sb = lambda n, shape, dt: st.enter_context(nc.sbuf_tensor("sb_" + S.pfx + n, shape, dt))
        xln_tok = [sb("xlntok0", [128, 1024], BF16), sb("xlntok1", [128, 1024], BF16)]
        xlnT = sb("xlnT", [128, 16384], BF16)
        attT = xlnT
        wsh = sb("wsh", [128, 8192], BF16)
        bqc = sb("bqc", [128, 8], F32)
        bkc = sb("bkc", [128, 8], F32)
        bvb = sb("bvb", [128, 1024], F32)
        bob = sb("bob", [128, 1024], F32)
        qT = sb("qT", [128, 16384], BF16)
        kT = sb("kT", [128, 16384], BF16)
        v_tok = sb("v_tok", [128, 16384], BF16)
        P_sb = [sb("P_sb0", [128, 2048], BF16), sb("P_sb1", [128, 2048], BF16)]
        PT = sb("PT", [128, 2048], BF16)
        O_s = [sb("O_s0", [128, 64], BF16), sb("O_s1", [128, 64], BF16)]
        sstat = sb("sstat", [128, 8], F32)
        pS = st.enter_context(nc.psum_tensor(S.pfx + "pS", [128, 2048], F32))
        pTr = st.enter_context(nc.psum_tensor(S.pfx + "pTr", [128, 1024], F32))
        pO = st.enter_context(nc.psum_tensor(S.pfx + "pO", [128, 128], F32))

        pTrb = pTr[:, :].bitcast(BF16)
        pTR_slots = [pTrb[:, r * 128:(r + 1) * 128] for r in range(16)]

        def xlnT_dst3d(cc):
            return xlnT[:, :].rearrange("p (r t) -> p r t", t=2048)[:, :, cc * 128:(cc + 1) * 128]

        def pTR_src3d(n=8):
            return pTrb.rearrange("p (r t) -> p r t", t=128)[:, 0:n, :]

        qpb = lambda h: (h % 2) * 64

        @block.sync
        def _(sync):
            S.barrier(sync, snap)
            for dst, srcdram in ((bqc, dbq), (bkc, dbk), (bvb, dbv), (bob, dbo)):
                sync.dma_start(out=dst[:, :], in_=srcdram[:, :]).then_inc(S.sem["dX"], 16)
            for q in range(NSEQ):
                for cc in range(NCH):
                    G = q * NCH + cc
                    if G >= 2:
                        S.wait(sync, ("l", "apply", G - 2))
                    sync.dma_start(out=xchunk[G % 2][:, :],
                                   in_=dxin[G * 128:(G + 1) * 128, :]).then_inc(S.sem["dX"], 16)
                for sc in range(16):
                    oi = q * 16 + sc
                    if oi >= 1:
                        S.wait(sync, ("oev2", oi - 1))
                    row0 = oi * 128
                    sync.dma_start(out=xr4[:, :],
                                   in_=dxin[row0:row0 + 128, :]).then_inc(S.sem["dX"], 16)
                    S.wait(sync, ("oev2", oi))
                    sync.dma_start(out=dxout[row0:row0 + 128, :],
                                   in_=outp[:, :]).then_inc(S.sem["dS"], 16)
            sync.wait_ge(S.sem["dS"], 16 * S.final("dS"))

        @block.gpsimd
        def _(g):
            S.barrier(g, snap)
            for q in range(NSEQ):
                for stg, dw in enumerate((dwq, dwk, dwv, dwo)):
                    if stg == 0 and q >= 1:
                        S.wait(g, ("mmF", q * 16 - 1))
                    elif stg == 1:
                        S.wait(g, ("mmP", q, 0, 7))
                    elif stg == 2:
                        S.wait(g, ("mmP", q, 1, 7))
                    elif stg == 3:
                        S.wait(g, ("mmV", q, 15))
                    for r in range(8):
                        g.dma_start(out=wsh[:, r * 1024:(r + 1) * 1024],
                                    in_=dw[r][:, :]).then_inc(S.sem["gD"], 16)
                for sc in range(16):
                    oi = q * 16 + sc
                    S.wait_max(g, [("oev1", oi), ("oxr", oi)] +
                               ([("store", oi - 1)] if oi >= 1 else []))
                    g.tensor_add(outp[:, :], outv[:, :], xr4[:, :]).then_inc(S.sem["pS"], 1)

        @block.tensor
        def _(t):
            S.barrier(t, snap)
            ptr_prev = []
            for q in range(NSEQ):
                for cc in range(NCH):
                    G = q * NCH + cc
                    waits = list(ptr_prev)
                    ln_emit_tensor(t, S, "l", G, xln_tok[G % 2], identbf,
                                   pTR_slots[0:8], waits)
                    ptr_prev = [("l", "xevac", G)]
                for stg in range(2):
                    for oc in range(8):
                        waits = [("wld", q, stg)]
                        if stg == 0 and oc == 0:
                            waits.append(("l", "xevac", q * NCH + NCH - 1))
                        prev = ("pev", q, stg, oc - 1) if oc >= 1 else \
                               (("pev", q, 0, 7) if stg == 1 else None)
                        if prev is None and q >= 1 and stg == 0:
                            prev = ("mmF", q * 16 - 1)
                            prev = ("oev1", q * 16 - 1)
                        if prev is not None:
                            waits.append(prev)
                        S.wait_max(t, waits)
                        ins = None
                        for n in range(4):
                            for r in range(8):
                                ins = t.matmul(pS[:, n * 512:(n + 1) * 512],
                                               wsh[:, r * 1024 + oc * 128: r * 1024 + (oc + 1) * 128],
                                               xlnT[:, r * 2048 + n * 512: r * 2048 + (n + 1) * 512],
                                               start=(r == 0), stop=(r == 7))
                        ins.then_inc(S.sem["tS"], 1)
                for tc in range(16):
                    waits = [("wld", q, 2)]
                    if tc < 2:
                        waits.append(("pev", q, 1, 7))
                    else:
                        waits.append(("vev", q, tc - 2))
                    S.wait_max(t, waits)
                    half = (tc % 2) * 1024
                    ins = None
                    for r in range(8):
                        for n in range(2):
                            ins = t.matmul(pS[:, half + n * 512: half + (n + 1) * 512],
                                           xlnT[:, r * 2048 + tc * 128: r * 2048 + (tc + 1) * 128],
                                           wsh[:, r * 1024 + n * 512: r * 1024 + (n + 1) * 512],
                                           start=(r == 0), stop=(r == 7))
                    ins.then_inc(S.sem["tS"], 1)
                def emit_mmS(ai2):
                    h2, sc2 = (ai2 // 16) % 16, ai2 % 16
                    ins = None
                    for n in range(4):
                        ins = t.matmul(pS[:, n * 512:(n + 1) * 512],
                                       qT[qpb(h2):qpb(h2) + 64,
                                          (h2 // 2) * 2048 + sc2 * 128: (h2 // 2) * 2048 + (sc2 + 1) * 128],
                                       kT[qpb(h2):qpb(h2) + 64,
                                          (h2 // 2) * 2048 + n * 512: (h2 // 2) * 2048 + (n + 1) * 512],
                                       start=True, stop=True)
                    ins.then_inc(S.sem["tS"], 1)

                first, last = q * 256, q * 256 + 255
                waits = [("vev", q, 15)]
                if q >= 1:
                    waits.append(("oev1", q * 16 - 1))
                S.wait_max(t, waits)
                emit_mmS(first)
                for ai in range(first, last + 1):
                    h, pi = (ai // 16) % 16, ai % 2
                    if ai > first:
                        S.wait(t, ("oev", ai - 1))
                        t.transpose(pTrb[0:64, 0:128], O_s[(ai - 1) % 2][:, :],
                                    identbf[:, :]).then_inc(S.sem["tS"], 1)
                    waits = [("exp", ai)]
                    if ai >= 1:
                        waits.append(("tevB", ai - 1))
                    if ai == first:
                        waits.append(("l", "xevac", q * NCH + NCH - 1))
                    S.wait_max(t, waits)
                    ins = None
                    for j in range(16):
                        ins = t.transpose(pTR_slots[j],
                                          P_sb[pi][:, j * 128:(j + 1) * 128], identbf[:, :])
                    ins.then_inc(S.sem["tS"], 1)
                    if ai < last:
                        emit_mmS(ai + 1)
                    waits = [("tevPa", ai), ("tevPd", ai)]
                    if ai >= 2:
                        waits.append(("oev", ai - 2))
                    S.wait_max(t, waits)
                    ins = None
                    for j in range(16):
                        ins = t.matmul(pO[:, pi * 64:(pi + 1) * 64],
                                       PT[:, j * 128:(j + 1) * 128],
                                       v_tok[:, j * 1024 + h * 64: j * 1024 + (h + 1) * 64],
                                       start=(j == 0), stop=(j == 15))
                    ins.then_inc(S.sem["tS"], 1)
                S.wait(t, ("oev", last))
                t.transpose(pTrb[0:64, 0:128], O_s[last % 2][:, :],
                            identbf[:, :]).then_inc(S.sem["tS"], 1)
                ptr_prev = [("tevB", last)]
                for sc in range(16):
                    oi = q * 16 + sc
                    waits = []
                    if sc == 0:
                        waits += [("tevB", (q * 16 + 15) * 16 + 15), ("exp", (q * 16 + 15) * 16 + 15),
                                  ("wld", q, 3)]
                    if oi >= 2:
                        waits.append(("oev1", oi - 2))
                    if waits:
                        S.wait_max(t, waits)
                    half = (sc % 2) * 1024
                    ins = None
                    for r in range(8):
                        for n in range(2):
                            ins = t.matmul(pS[:, half + n * 512: half + (n + 1) * 512],
                                           attT[:, r * 2048 + sc * 128: r * 2048 + (sc + 1) * 128],
                                           wsh[:, r * 1024 + n * 512: r * 1024 + (n + 1) * 512],
                                           start=(r == 0), stop=(r == 7))
                    ins.then_inc(S.sem["tS"], 1)

        @block.vector
        def _(v):
            S.barrier(v, snap)
            S.wait(v, ("init",))
            for q in range(NSEQ):
                for cc in range(NCH):
                    G = q * NCH + cc
                    extra = [("l", "tr", G - 2)] if G >= 2 else []
                    ln_emit_vector(v, S, "l", G, xchunk[G % 2], stats, consts,
                                   pTR_src3d(8), xlnT_dst3d(cc), xln_tok[G % 2], extra)
                for stg in range(2):
                    bc = bqc if stg == 0 else bkc
                    dst = qT if stg == 0 else kT
                    for oc in range(8):
                        S.wait(v, ("mmP", q, stg, oc))
                        if stg == 0:
                            v.tensor_scalar(dst[:, oc * 2048:(oc + 1) * 2048], pS[:, :],
                                            bc[:, oc:oc + 1], consts[:, C_QK8:C_QK8 + 1],
                                            ALU.add, ALU.mult).then_inc(S.sem["vS"], 1)
                        else:
                            v.tensor_scalar_add(dst[:, oc * 2048:(oc + 1) * 2048], pS[:, :],
                                                bc[:, oc:oc + 1]).then_inc(S.sem["vS"], 1)
                for tc in range(16):
                    S.wait(v, ("mmV", q, tc))
                    half = (tc % 2) * 1024
                    v.tensor_add(v_tok[:, tc * 1024:(tc + 1) * 1024],
                                 pS[:, half:half + 1024], bvb[:, :]).then_inc(S.sem["vS"], 1)
                first, last = q * 256, q * 256 + 255
                for ai in range(first, last + 1):
                    pi = ai % 2
                    S.wait(v, ("mmS", ai))
                    v.reduce_max(sstat[:, pi:pi + 1], pS[:, :], axis=AX.X, negate=True)
                    v.drain()
                    v.engine_nop().then_inc(S.sem["vS"], 1)
                    S.wait(v, ("trP", ai))
                    v.tensor_copy(PT[:, 1280:2048],
                                  pTrb[:, 1280:2048]).then_inc(S.sem["vS"], 1)
                    waits = [("mmO", ai)]
                    if ai >= 2:
                        waits.append(("trB", ai - 2))
                    S.wait_max(v, waits)
                    v.reciprocal(sstat[:, 4 + pi:5 + pi], sstat[:, 2 + pi:3 + pi])
                    v.drain()
                    v.tensor_scalar_mul(O_s[pi][:, :], pO[:, pi * 64:(pi + 1) * 64],
                                        sstat[:, 4 + pi:5 + pi]).then_inc(S.sem["vS"], 1)
                for sc in range(16):
                    oi = q * 16 + sc
                    waits = [("mmF", oi)]
                    if oi >= 1:
                        waits.append(("oev2", oi - 1))
                    S.wait_max(v, waits)
                    half = (sc % 2) * 1024
                    v.tensor_add(outv[:, :], pS[:, half:half + 1024],
                                 bob[:, :]).then_inc(S.sem["vS"], 1)

        @block.scalar
        def _(s):
            S.barrier(s, snap)
            for q in range(NSEQ):
                for cc in range(NCH):
                    G = q * NCH + cc
                    ln_emit_scalar(s, S, "l", G, xchunk[G % 2], stats, consts, sqscr)
                first, last = q * 256, q * 256 + 255
                for ai in range(first, last + 1):
                    h, sc, pi = (ai // 16) % 16, ai % 16, ai % 2
                    waits = [("nms", ai)]
                    if ai >= 2:
                        waits.append(("trP", ai - 2))
                    S.wait_max(s, waits)
                    s.activation(P_sb[pi][:, :], pS[:, :], AF.Exp,
                                 bias=sstat[:, pi:pi + 1],
                                 accum_out=sstat[:, 2 + pi:3 + pi]).then_inc(S.sem["aS"], 1)
                    if ai > first:
                        h1, sc1 = ((ai - 1) // 16) % 16, (ai - 1) % 16
                        S.wait(s, ("trB", ai - 1))
                        s.copy(attT[qpb(h1):qpb(h1) + 64,
                                    (h1 // 2) * 2048 + sc1 * 128:(h1 // 2) * 2048 + (sc1 + 1) * 128],
                               pTrb[0:64, 0:128]).then_inc(S.sem["aS"], 1)
                    S.wait(s, ("trP", ai))
                    s.copy(PT[:, 0:1280], pTrb[:, 0:1280]).then_inc(S.sem["aS"], 1)
                h1, sc1 = 15, 15
                S.wait(s, ("trB", last))
                s.copy(attT[qpb(h1):qpb(h1) + 64,
                            (h1 // 2) * 2048 + sc1 * 128:(h1 // 2) * 2048 + (sc1 + 1) * 128],
                       pTrb[0:64, 0:128]).then_inc(S.sem["aS"], 1)


def attn1_inputs(w):
    return {"wqb": w["wqb"], "wkb": w["wkb"], "wvb": w["wvb"], "bqc": w["bqc"],
            "bkc": w["bkc"], "bvb": w["bvb"], "wob": w["wob"], "bob": w["bob"]}


def attn0_inputs(w):
    return {"wqkvb": w["wqkvb"], "bqkv": w["bqkv"], "woutb": w["woutb"],
            "boutb": w["boutb"], "pos2": w["pos2"], "onesb": w["onesb"]}


def ffn_inputs(w, L):
    keys = ("w1b", "w2b", "h1b", "h2b", "w3b", "sc1", "sc2", "sch1", "sch2", "s3b")
    return {k + str(L): w[k + str(L)] for k in keys}



def _dram_attn0(nc):
    return {
        "wqkvb": nc.dram_tensor("wqkvb", [8, 128, 3072], F32, kind="ExternalInput"),
        "bqkv": nc.dram_tensor("bqkv", [128, 24], F32, kind="ExternalInput"),
        "woutb": nc.dram_tensor("woutb", [8, 128, 1024], F32, kind="ExternalInput"),
        "boutb": nc.dram_tensor("boutb", [128, 1024], F32, kind="ExternalInput"),
        "pos2": nc.dram_tensor("pos2", [128, 2048], F32, kind="ExternalInput"),
        "onesb": nc.dram_tensor("onesb", [1, 128], F32, kind="ExternalInput"),
    }


def _dram_attn1(nc):
    return {
        "wqb": nc.dram_tensor("wqb", [8, 128, 1024], F32, kind="ExternalInput"),
        "wkb": nc.dram_tensor("wkb", [8, 128, 1024], F32, kind="ExternalInput"),
        "wvb": nc.dram_tensor("wvb", [8, 128, 1024], F32, kind="ExternalInput"),
        "bqc": nc.dram_tensor("bqc", [128, 8], F32, kind="ExternalInput"),
        "bkc": nc.dram_tensor("bkc", [128, 8], F32, kind="ExternalInput"),
        "bvb": nc.dram_tensor("bvb", [128, 1024], F32, kind="ExternalInput"),
        "wob": nc.dram_tensor("wob", [8, 128, 1024], F32, kind="ExternalInput"),
        "bob": nc.dram_tensor("bob", [128, 1024], F32, kind="ExternalInput"),
    }


def _dram_ffn(nc, L):
    s = str(L)
    return {
        "w1b": nc.dram_tensor("w1b" + s, [32, 128, 8, 128], F32, kind="ExternalInput"),
        "w2b": nc.dram_tensor("w2b" + s, [32, 128, 8, 128], F32, kind="ExternalInput"),
        "h1b": nc.dram_tensor("h1b" + s, [32, 128, 32, 128], F32, kind="ExternalInput"),
        "h2b": nc.dram_tensor("h2b" + s, [32, 128, 32, 128], F32, kind="ExternalInput"),
        "w3b": nc.dram_tensor("w3b" + s, [32, 128, 1024], F32, kind="ExternalInput"),
        "sc1": nc.dram_tensor("sc1" + s, [128, 32], F32, kind="ExternalInput"),
        "sc2": nc.dram_tensor("sc2" + s, [128, 32], F32, kind="ExternalInput"),
        "sch1": nc.dram_tensor("sch1" + s, [128, 32], F32, kind="ExternalInput"),
        "sch2": nc.dram_tensor("sch2" + s, [128, 32], F32, kind="ExternalInput"),
        "s3b": nc.dram_tensor("s3b" + s, [128, 1024], F32, kind="ExternalInput"),
    }



def _shared_and_init(nc, st, S, block, dcon, dide):
    sb = lambda n, shape, dt: st.enter_context(nc.sbuf_tensor("sh_" + n, shape, dt))
    sh = {
        "xchunk": [sb("xchunk0", [128, 1024], F32), sb("xchunk1", [128, 1024], F32)],
        "sqscr": sb("sqscr", [128, 1024], F32),
        "stats": sb("stats", [128, 16], F32),
        "consts": sb("consts", [128, 8], F32),
        "ident32": sb("ident32", [128, 128], F32),
        "identbf": sb("identbf", [128, 128], BF16),
        "xr4": sb("xr4", [128, 1024], F32),
        "outv": sb("outv", [128, 1024], F32),
        "outp": sb("outp", [128, 1024], F32),
    }
    S.pfx = "g:"
    S.inc("dX", n=2); S.setmark(("init",), "dX")
    S.inc("vS", ("identbf",))

    @block.sync
    def _(sync):
        sync.dma_start(out=sh["consts"][:, :], in_=dcon[:, :]).then_inc(S.sem["dX"], 16)
        sync.dma_start(out=sh["ident32"][:, :], in_=dide[:, :]).then_inc(S.sem["dX"], 16)

    @block.vector
    def _(v):
        S.wait(v, ("init",))
        v.tensor_copy(sh["identbf"][:, :], sh["ident32"][:, :]).then_inc(S.sem["vS"], 1)

    return sh


def build_fused_nc():
    nc = bass.Bass("TRN2", target_bir_lowering=False)
    dxin = nc.dram_tensor("xin", [TPC, D], F32, kind="ExternalInput")
    dx1 = nc.dram_tensor("x1", [TPC, D], F32)
    dx2 = nc.dram_tensor("x2", [TPC, D], F32)
    dx3 = nc.dram_tensor("x3", [TPC, D], F32)
    dxout = nc.dram_tensor("xout", [TPC, D], F32, kind="ExternalOutput")
    dcon = nc.dram_tensor("consts", [128, 8], F32, kind="ExternalInput")
    dide = nc.dram_tensor("ident", [128, 128], F32, kind="ExternalInput")
    dr_a0 = _dram_attn0(nc); dr_a0.update(xin=dxin, xout=dx1)
    dr_f0 = _dram_ffn(nc, 0); dr_f0.update(xin=dx1, xout=dx2)
    dr_l1 = _dram_attn1(nc); dr_l1.update(xin=dx2, xout=dx3)
    dr_f1 = _dram_ffn(nc, 1); dr_f1.update(xin=dx3, xout=dxout)

    S = Sched()
    with ExitStack() as st:
        for name in SEMS:
            S.sem[name] = st.enter_context(nc.semaphore(name))
        block = st.enter_context(nc.Block())
        sh = _shared_and_init(nc, st, S, block, dcon, dide)
        snap = S.snapshot()
        S.pfx = "a0:"
        emit_attn0(nc, block, S, sh, dr_a0, snap)
        snap = S.snapshot()
        S.pfx = "f0:"
        emit_ffn(nc, block, S, sh, dr_f0, snap)
        snap = S.snapshot()
        S.pfx = "l1:"
        emit_attn1(nc, block, S, sh, dr_l1, snap)
        snap = S.snapshot()
        S.pfx = "f1:"
        emit_ffn(nc, block, S, sh, dr_f1, snap)
    return nc



def _build_single(which, L=0):
    nc = bass.Bass("TRN2", target_bir_lowering=False)
    dxin = nc.dram_tensor("xin", [TPC, D], F32, kind="ExternalInput")
    dxout = nc.dram_tensor("xout", [TPC, D], F32, kind="ExternalOutput")
    dcon = nc.dram_tensor("consts", [128, 8], F32, kind="ExternalInput")
    dide = nc.dram_tensor("ident", [128, 128], F32, kind="ExternalInput")
    if which == "a0":
        dr = _dram_attn0(nc)
    elif which == "l1":
        dr = _dram_attn1(nc)
    else:
        dr = _dram_ffn(nc, L)
    dr.update(xin=dxin, xout=dxout)
    S = Sched()
    with ExitStack() as st:
        for name in SEMS:
            S.sem[name] = st.enter_context(nc.semaphore(name))
        block = st.enter_context(nc.Block())
        sh = _shared_and_init(nc, st, S, block, dcon, dide)
        snap = S.snapshot()
        S.pfx = which + ":"
        if which == "a0":
            emit_attn0(nc, block, S, sh, dr, snap)
        elif which == "l1":
            emit_attn1(nc, block, S, sh, dr, snap)
        else:
            emit_ffn(nc, block, S, sh, dr, snap)
    return nc



def kernel(**inputs):
    w = prep_shared(inputs)
    x = np.ascontiguousarray(np.asarray(inputs["L0_attn_x"], np.float32).reshape(NCORES * TPC, D))
    shared_w = {"consts": w["consts"], "ident": w["ident"]}
    shared_w.update(attn0_inputs(w))
    shared_w.update(ffn_inputs(w, 0))
    shared_w.update(attn1_inputs(w))
    shared_w.update(ffn_inputs(w, 1))

    nc = build_fused_nc()
    in_maps = []
    for c in range(NCORES):
        m = dict(shared_w)
        m["xin"] = np.ascontiguousarray(x[c * TPC:(c + 1) * TPC])
        in_maps.append(m)
    res = run_bass_kernel_spmd(nc, in_maps, core_ids=list(range(NCORES)))
    out = np.concatenate([res.results[c]["xout"] for c in range(NCORES)], axis=0)
    return out.reshape(16, 2048, D).astype(np.float32)


# version 06
# speedup vs baseline: 1.1170x; 1.0777x over previous
import math
import numpy as np
from contextlib import ExitStack

import concourse.bass as bass
import concourse.mybir as mybir
from concourse.bass_utils import run_bass_kernel_spmd

F32 = mybir.dt.float32
BF16 = mybir.dt.bfloat16
AF = mybir.ActivationFunctionType
ALU = mybir.AluOpType
AX = mybir.AxisListType

NCORES = 8
D = 1024
SEQ = 2048
TPC = 4096
HV = 16

C_INV_D, C_EPS, C_NEG1, C_QK8 = 0, 1, 2, 3



def _pos_encoding_T2():
    L, dim = SEQ, 64
    positions = np.arange(L, dtype=np.float32)[:, None]
    div_term = np.exp(np.arange(0, dim, 2, dtype=np.float32) * -(np.log(10000.0) / dim)).astype(np.float32)
    ang = (positions * div_term).astype(np.float32)
    pe = np.zeros((L, dim), dtype=np.float32)
    pe[:, 0::2] = np.sin(ang)
    pe[:, 1::2] = np.cos(ang)
    peT = np.ascontiguousarray(pe.T)
    return np.ascontiguousarray(np.concatenate([peT, peT], axis=0))


def _blk_lhsT(W):
    O, I = W.shape
    return np.ascontiguousarray(W.reshape(O // 128, 128, I // 128, 128).transpose(0, 3, 2, 1))


def _col_chunks(v):
    return np.ascontiguousarray(np.asarray(v, np.float32).reshape(-1, 128).T)


def _rep(v, n=128):
    v = np.asarray(v, np.float32).reshape(-1)
    return np.ascontiguousarray(np.broadcast_to(v.reshape(1, -1), (n, v.size)))


def prep_shared(inputs):
    f = lambda k: np.ascontiguousarray(np.asarray(inputs[k], np.float32))
    w = {}
    perm = np.empty(3072, np.int64)
    for part in range(3):
        for h in range(16):
            base = part * 1024 + h * 64
            perm[base:base + 64] = h * 192 + part * 64 + np.arange(64)
    wqkv = f("L0_attn_w_qkv")[:, perm]
    w["wqkvb"] = np.ascontiguousarray(wqkv.reshape(8, 128, 3072))
    w["bqkv"] = _col_chunks(f("L0_attn_b_qkv")[perm])
    w["woutb"] = np.ascontiguousarray(f("L0_attn_w_out").reshape(8, 128, 1024))
    w["boutb"] = _rep(f("L0_attn_b_out"))
    w["pos2"] = _pos_encoding_T2()
    w["onesb"] = np.ones((1, 128), np.float32)
    for nm, key in (("wq", "L1_attn_wq"), ("wk", "L1_attn_wk"), ("wv", "L1_attn_wv")):
        m = np.transpose(f(key), (1, 0, 2)).reshape(1024, 1024)
        w[nm + "b"] = np.ascontiguousarray(m.reshape(8, 128, 1024))
    w["bqc"] = _col_chunks(f("L1_attn_bq").reshape(-1))
    w["bkc"] = _col_chunks(f("L1_attn_bk").reshape(-1))
    w["bvb"] = _rep(f("L1_attn_bv").reshape(-1))
    w["wob"] = np.ascontiguousarray(f("L1_attn_wo").T.reshape(8, 128, 1024))
    w["onesl"] = np.ones((128, 128), np.float32)
    w["bob"] = _rep(f("L1_attn_bo"))
    for L in (0, 1):
        p = f"L{L}_ffn_"
        w[f"w1b{L}"] = _blk_lhsT(f(p + "w1_w"))
        w[f"w2b{L}"] = _blk_lhsT(f(p + "w2_w"))
        w[f"h1b{L}"] = _blk_lhsT(f(p + "h1_w"))
        w[f"h2b{L}"] = _blk_lhsT(f(p + "h2_w"))
        w[f"w3b{L}"] = np.ascontiguousarray(f(p + "w3_w").T.reshape(32, 128, 1024))
        w[f"sc1{L}"] = _col_chunks(f(p + "w1_s"))
        w[f"sc2{L}"] = _col_chunks(f(p + "w2_s"))
        w[f"sch1{L}"] = _col_chunks(f(p + "h1_s"))
        w[f"sch2{L}"] = _col_chunks(f(p + "h2_s"))
        w[f"s3b{L}"] = _rep(f(p + "w3_s"))
    consts = np.zeros((128, 8), np.float32)
    consts[:, C_INV_D] = 1.0 / D
    consts[:, C_EPS] = 1e-5
    consts[:, C_NEG1] = -1.0
    consts[:, C_QK8] = 0.125
    w["consts"] = consts
    w["ident"] = np.eye(128, dtype=np.float32)
    return w



SEMS = ("dX", "dS", "gD", "tS", "aS", "vS", "pS")


DMA_SEMS = ("dX", "dS", "gD")


class Sched:
    def __init__(self):
        self.c = {n: 0 for n in SEMS}
        self.marks = {}
        self.sem = {}
        self.pfx = ""

    def _k(self, key):
        return (self.pfx,) + tuple(key)

    def inc(self, name, key=None, n=1):
        self.c[name] += n
        if key is not None:
            self.marks[self._k(key)] = (name, self.c[name])
        return self.c[name]

    def setmark(self, key, name):
        self.marks[self._k(key)] = (name, self.c[name])

    def alias(self, key, srckey):
        self.marks[self._k(key)] = self.marks[self._k(srckey)]

    def wait(self, eng, key):
        name, cnt = self.marks[self._k(key)]
        if name in DMA_SEMS:
            cnt *= 16
        eng.wait_ge(self.sem[name], cnt)

    def wait_max(self, eng, keys):
        best = {}
        for k in keys:
            name, cnt = self.marks[self._k(k)]
            best[name] = max(best.get(name, 0), cnt)
        for name, cnt in best.items():
            if name in DMA_SEMS:
                cnt *= 16
            eng.wait_ge(self.sem[name], cnt)

    def snapshot(self):
        return dict(self.c)

    def barrier(self, eng, snap):
        for name, cnt in snap.items():
            if cnt <= 0:
                continue
            eng.wait_ge(self.sem[name], cnt * 16 if name in DMA_SEMS else cnt)

    def final(self, name):
        return self.c[name]



def ln_prepass(S, pfx, G):
    S.inc("aS", (pfx, "sq", G))
    S.inc("vS", (pfx, "var", G))
    S.inc("aS", (pfx, "sqrt", G))
    S.inc("vS", (pfx, "rstdb", G))
    S.inc("vS", (pfx, "apply", G))
    S.inc("tS", (pfx, "tr", G))
    S.inc("vS", (pfx, "xevac", G))


def ln_emit_vector(v, S, pfx, G, xchunk, stats, consts, pTRb, xlnT_dst3d, xln_tok,
                   extra_apply_waits=()):
    b = 8 * (G % 2)
    S.wait(v, (pfx, "xload", G))
    v.reduce_sum(stats[:, b + 0:b + 1], xchunk[:, :], axis=AX.X)
    S.wait(v, (pfx, "sq", G))
    v.drain()
    v.tensor_scalar_mul(stats[:, b + 2:b + 3], stats[:, b + 0:b + 1], consts[:, C_INV_D:C_INV_D + 1])
    v.drain()
    v.tensor_mul(stats[:, b + 3:b + 4], stats[:, b + 2:b + 3], stats[:, b + 2:b + 3])
    v.drain()
    v.scalar_tensor_tensor(stats[:, b + 4:b + 5], stats[:, b + 1:b + 2],
                           consts[:, C_INV_D:C_INV_D + 1], stats[:, b + 3:b + 4],
                           ALU.mult, ALU.subtract).then_inc(S.sem["vS"], 1)
    S.wait(v, (pfx, "sqrt", G))
    v.reciprocal(stats[:, b + 6:b + 7], stats[:, b + 5:b + 6])
    v.drain()
    v.tensor_scalar(stats[:, b + 7:b + 8], stats[:, b + 2:b + 3], stats[:, b + 6:b + 7],
                    consts[:, C_NEG1:C_NEG1 + 1], ALU.mult, ALU.mult).then_inc(S.sem["vS"], 1)
    v.drain()
    if extra_apply_waits:
        S.wait_max(v, extra_apply_waits)
    v.tensor_scalar(xln_tok[:, :], xchunk[:, :], stats[:, b + 6:b + 7],
                    stats[:, b + 7:b + 8], ALU.mult, ALU.add).then_inc(S.sem["vS"], 1)
    S.wait(v, (pfx, "tr", G))
    v.tensor_copy(xlnT_dst3d, pTRb).then_inc(S.sem["vS"], 1)


def ln_emit_scalar(s, S, pfx, G, xchunk, stats, consts, sqscr):
    b = 8 * (G % 2)
    S.wait(s, (pfx, "xload", G))
    s.activation(sqscr[:, :], xchunk[:, :], AF.Square,
                 accum_out=stats[:, b + 1:b + 2]).then_inc(S.sem["aS"], 1)
    S.wait(s, (pfx, "var", G))
    s.activation(stats[:, b + 5:b + 6], stats[:, b + 4:b + 5], AF.Sqrt,
                 bias=consts[:, C_EPS:C_EPS + 1]).then_inc(S.sem["aS"], 1)



def ln_emit_tensor(t, S, pfx, G, xln_tok, identbf, pTRb_slots, wait_keys):
    S.wait_max(t, wait_keys + [(pfx, "apply", G)])
    ins = None
    for r in range(8):
        ins = t.transpose(pTRb_slots[r], xln_tok[:, r * 128:(r + 1) * 128], identbf[:, :])
    ins.then_inc(S.sem["tS"], 1)



def emit_ffn(nc, block, S, sh, dr, snap):
    FT = 1024
    NT = TPC // FT
    dxin, dxout = dr["xin"], dr["xout"]
    dw1, dw2, dh1, dh2, dw3 = dr["w1b"], dr["w2b"], dr["h1b"], dr["h2b"], dr["w3b"]
    dsc1, dsc2, dsh1, dsh2, ds3 = dr["sc1"], dr["sc2"], dr["sch1"], dr["sch2"], dr["s3b"]
    consts, ident32, identbf = sh["consts"], sh["ident32"], sh["identbf"]
    xchunk, sqscr, stats = sh["xchunk"], sh["sqscr"], sh["stats"]
    xr4, outv, outp = sh["xr4"], sh["outv"], sh["outp"]
    S.inc("dX", n=5); S.setmark(("init",), "dX")
    for i in range(NT):
        for tc in range(8):
            G = i * 8 + tc
            S.inc("dX", ("f", "xload", G))
            ln_prepass(S, "f", G)
        for c in range(32):
            bi = i * 32 + c
            S.inc("gD", n=2); S.setmark(("wB", bi), "gD")
            S.inc("tS", ("mmB", bi))
            S.inc("aS", ("sil", bi))
            S.inc("vS", ("g1", bi))
        for st_ in range(2):
            for c in range(32):
                gi = i * 64 + st_ * 32 + c
                S.inc("gD", ("wCDh", 2 * gi))
                S.inc("gD", ("wCDh", 2 * gi + 1))
                S.inc("tS", ("mmCDh", 2 * gi))
                S.inc("tS", ("mmCD", gi))
                S.alias(("mmCDh", 2 * gi + 1), ("mmCD", gi))
                S.inc("aS", ("gelu", gi))
        for half in range(2):
            for r in range(32):
                ei = (i * 2 + half) * 32 + r
                S.inc("gD", ("wE", ei))
                S.inc("tS", ("mmE", ei))
            for t4 in range(4):
                idx = i * 8 + half * 4 + t4
                S.inc("dX", ("xr", idx))
                S.inc("vS", ("ev1", idx))
                S.inc("pS", ("ev2", idx))
                S.inc("dS", ("store", idx))

    with ExitStack() as st:
        sb = lambda n, shape, dt: st.enter_context(nc.sbuf_tensor("sb_" + S.pfx + n, shape, dt))
        xln_tok = [sb("xlntok0", [128, 1024], BF16), sb("xlntok1", [128, 1024], BF16)]
        xlnT = sb("xlnT", [128, 8 * 1024], BF16)
        g_a = sb("g_a", [128, 32 * 1024], BF16)
        g_b = sb("g_b", [128, 32 * 1024], BF16)
        wb1 = [sb("wb1_0", [128, 1024], BF16), sb("wb1_1", [128, 1024], BF16)]
        wb2 = [sb("wb2_0", [128, 1024], BF16), sb("wb2_1", [128, 1024], BF16)]
        wbig = [sb("wbig0", [128, 2048], BF16), sb("wbig1", [128, 2048], BF16)]
        tsil = [sb("tsil0", [128, 1024], BF16), sb("tsil1", [128, 1024], BF16)]
        sc1 = sb("sc1", [128, 32], F32)
        sc2 = sb("sc2", [128, 32], F32)
        sch1 = sb("sch1", [128, 32], F32)
        sch2 = sb("sch2", [128, 32], F32)
        s3b = sb("s3b", [128, 1024], F32)
        pA = st.enter_context(nc.psum_tensor(S.pfx + "pA", [128, 1024], F32))
        pB = st.enter_context(nc.psum_tensor(S.pfx + "pB", [128, 1024], F32))
        pC = st.enter_context(nc.psum_tensor(S.pfx + "pC", [128, 1024], F32))
        pE = st.enter_context(nc.psum_tensor(S.pfx + "pE", [128, 1024], F32))
        psums = [pA, pB, pC, pE]

        pTRb = pE[:, :].bitcast(BF16)
        pTR_slots = [pTRb[:, r * 128:(r + 1) * 128] for r in range(8)]

        def xlnT_dst3d(tc):
            return xlnT[:, :].rearrange("p (r t) -> p r t", t=1024)[:, :, tc * 128:(tc + 1) * 128]

        def pTR_src3d():
            return pTRb.rearrange("p (r t) -> p r t", t=128)[:, 0:8, :]

        @block.sync
        def _(sync):
            S.barrier(sync, snap)
            for dst, srcdram in ((sc1, dsc1), (sc2, dsc2),
                                 (sch1, dsh1), (sch2, dsh2), (s3b, ds3)):
                sync.dma_start(out=dst[:, :], in_=srcdram[:, :]).then_inc(S.sem["dX"], 16)
            for i in range(NT):
                for tc in range(8):
                    G = i * 8 + tc
                    if G >= 2:
                        S.wait(sync, ("f", "apply", G - 2))
                    sync.dma_start(out=xchunk[G % 2][:, :],
                                   in_=dxin[G * 128:(G + 1) * 128, :]).then_inc(S.sem["dX"], 16)
                for half in range(2):
                    for t4 in range(4):
                        idx = i * 8 + half * 4 + t4
                        if idx >= 1:
                            S.wait(sync, ("ev2", idx - 1))
                        row0 = i * 1024 + (half * 4 + t4) * 128
                        sync.dma_start(out=xr4[:, :],
                                       in_=dxin[row0:row0 + 128, :]).then_inc(S.sem["dX"], 16)
                        S.wait(sync, ("ev2", idx))
                        sync.dma_start(out=dxout[row0:row0 + 128, :],
                                       in_=outp[:, :]).then_inc(S.sem["dS"], 16)
            sync.wait_ge(S.sem["dS"], 16 * S.final("dS"))

        @block.gpsimd
        def _(g):
            S.barrier(g, snap)
            for i in range(NT):
                for c in range(32):
                    bi = i * 32 + c
                    if c >= 2:
                        S.wait(g, ("mmB", bi - 2))
                    elif i >= 1:
                        S.wait(g, ("mmE", (i - 1) * 64 + 62 + c))
                    g.dma_start(out=wb1[c % 2][:, :],
                                in_=dw1[c].rearrange("p r j -> p (r j)")).then_inc(S.sem["gD"], 16)
                    g.dma_start(out=wb2[c % 2][:, :],
                                in_=dw2[c].rearrange("p r j -> p (r j)")).then_inc(S.sem["gD"], 16)
                for st_ in range(2):
                    dh = dh1 if st_ == 0 else dh2
                    for c in range(32):
                        gi = i * 64 + st_ * 32 + c
                        for hp in range(2):
                            hi = 2 * gi + hp
                            if hi >= 2:
                                S.wait(g, ("mmCDh", hi - 2))
                            g.dma_start(out=wbig[hi % 2][:, :],
                                        in_=dh[c].rearrange("p r j -> p (r j)")[:, hp * 2048:(hp + 1) * 2048]
                                        ).then_inc(S.sem["gD"], 16)
                for half in range(2):
                    for r in range(32):
                        ei = (i * 2 + half) * 32 + r
                        if half == 0 and r < 2:
                            S.wait(g, ("mmB", i * 32 + 30 + r))
                        else:
                            S.wait(g, ("mmE", ei - 2))
                        g.dma_start(out=wb1[ei % 2][:, :],
                                    in_=dw3[r][:, :]).then_inc(S.sem["gD"], 16)
                    for t4 in range(4):
                        idx = i * 8 + half * 4 + t4
                        S.wait_max(g, [("ev1", idx), ("xr", idx)] +
                                   ([("store", idx - 1)] if idx >= 1 else []))
                        g.tensor_add(outp[:, :], outv[:, :],
                                     xr4[:, :]).then_inc(S.sem["pS"], 1)

        @block.tensor
        def _(t):
            S.barrier(t, snap)
            for i in range(NT):
                for tc in range(8):
                    G = i * 8 + tc
                    waits = []
                    if G >= 1:
                        waits.append(("f", "xevac", G - 1))
                    if i >= 1:
                        waits.append(("ev1", i * 8 - 1))
                    ln_emit_tensor(t, S, "f", G, xln_tok[G % 2], identbf, pTR_slots, waits)
                for c in range(32):
                    bi = i * 32 + c
                    pX, pY = (pA, pB) if c % 2 == 0 else (pC, pE)
                    waits = [("wB", bi)]
                    if c == 0:
                        waits.append(("f", "xevac", i * 8 + 7))
                    if bi >= 2:
                        waits += [("sil", bi - 2), ("g1", bi - 2)]
                    if c < 2 and i >= 1:
                        waits.append(("ev1", i * 8 - 1))
                    S.wait_max(t, waits)
                    ins = None
                    for r in range(8):
                        for n in range(2):
                            ins = t.matmul(pX[:, n * 512:(n + 1) * 512],
                                           wb1[c % 2][:, r * 128:(r + 1) * 128],
                                           xlnT[:, r * 1024 + n * 512: r * 1024 + (n + 1) * 512],
                                           start=(r == 0), stop=(r == 7))
                    for r in range(8):
                        for n in range(2):
                            ins = t.matmul(pY[:, n * 512:(n + 1) * 512],
                                           wb2[c % 2][:, r * 128:(r + 1) * 128],
                                           xlnT[:, r * 1024 + n * 512: r * 1024 + (n + 1) * 512],
                                           start=(r == 0), stop=(r == 7))
                    ins.then_inc(S.sem["tS"], 1)
                for st_ in range(2):
                    gsrc = g_a if st_ == 0 else g_b
                    for c in range(32):
                        gi = i * 64 + st_ * 32 + c
                        pX = pA if c % 2 == 0 else pC
                        waits = [("wCDh", 2 * gi)]
                        if st_ == 0 and c == 0:
                            waits.append(("g1", i * 32 + 31))
                        if st_ == 1 and c == 0:
                            waits.append(("gelu", i * 64 + 31))
                        if gi >= 2:
                            waits.append(("gelu", gi - 2))
                        if c < 2 and st_ == 0:
                            waits += [("sil", i * 32 + 30 + c)]
                        S.wait_max(t, waits)
                        ins = None
                        for r in range(32):
                            if r == 16:
                                ins.then_inc(S.sem["tS"], 1)
                                S.wait(t, ("wCDh", 2 * gi + 1))
                            for n in range(2):
                                ins = t.matmul(pX[:, n * 512:(n + 1) * 512],
                                               wbig[(2 * gi + r // 16) % 2][:, (r % 16) * 128:(r % 16 + 1) * 128],
                                               gsrc[:, r * 1024 + n * 512: r * 1024 + (n + 1) * 512],
                                               start=(r == 0), stop=(r == 31))
                        ins.then_inc(S.sem["tS"], 1)
                for half in range(2):
                    if half == 1:
                        S.wait_max(t, [("ev1", i * 8 + 3)])
                    for r in range(32):
                        ei = (i * 2 + half) * 32 + r
                        waits = [("wE", ei), ("gelu", i * 64 + 32 + r)]
                        if half == 0 and r == 0:
                            waits += [("gelu", i * 64 + 63), ("g1", i * 32 + 31)]
                        S.wait_max(t, waits)
                        ins = None
                        for t4 in range(4):
                            tloc = half * 4 + t4
                            for n in range(2):
                                ins = t.matmul(psums[t4][:, n * 512:(n + 1) * 512],
                                               g_a[:, r * 1024 + tloc * 128: r * 1024 + (tloc + 1) * 128],
                                               wb1[ei % 2][:, n * 512:(n + 1) * 512],
                                               start=(r == 0), stop=(r == 31))
                        ins.then_inc(S.sem["tS"], 1)

        @block.vector
        def _(v):
            S.barrier(v, snap)
            S.wait(v, ("init",))
            for i in range(NT):
                for tc in range(8):
                    G = i * 8 + tc
                    extra = [("f", "tr", G - 2)] if G >= 2 else []
                    ln_emit_vector(v, S, "f", G, xchunk[G % 2], stats, consts,
                                   pTR_src3d(), xlnT_dst3d(tc), xln_tok[G % 2], extra)
                for c in range(32):
                    bi = i * 32 + c
                    pY = pB if c % 2 == 0 else pE
                    S.wait(v, ("sil", bi))
                    v.scalar_tensor_tensor(g_a[:, c * 1024:(c + 1) * 1024], pY[:, :],
                                           sc2[:, c:c + 1], tsil[c % 2][:, :],
                                           ALU.mult, ALU.mult).then_inc(S.sem["vS"], 1)
                for half in range(2):
                    for t4 in range(4):
                        idx = i * 8 + half * 4 + t4
                        waits = [("mmE", (i * 2 + half) * 32 + 31)]
                        if idx >= 1:
                            waits.append(("ev2", idx - 1))
                        S.wait_max(v, waits)
                        v.tensor_mul(outv[:, :],
                                     psums[t4][:, :], s3b[:, :]).then_inc(S.sem["vS"], 1)

        @block.scalar
        def _(s):
            S.barrier(s, snap)
            S.wait(s, ("init",))
            for i in range(NT):
                for tc in range(8):
                    G = i * 8 + tc
                    ln_emit_scalar(s, S, "f", G, xchunk[G % 2], stats, consts, sqscr)
                for c in range(32):
                    bi = i * 32 + c
                    pX = pA if c % 2 == 0 else pC
                    waits = [("mmB", bi)]
                    if bi >= 2:
                        waits.append(("g1", bi - 2))
                    S.wait_max(s, waits)
                    s.activation(tsil[c % 2][:, :], pX[:, :], AF.Silu,
                                 scale=sc1[:, c:c + 1]).then_inc(S.sem["aS"], 1)
                for st_ in range(2):
                    gdst = g_b if st_ == 0 else g_a
                    sch = sch1 if st_ == 0 else sch2
                    for c in range(32):
                        gi = i * 64 + st_ * 32 + c
                        pX = pA if c % 2 == 0 else pC
                        S.wait(s, ("mmCD", gi))
                        s.activation(gdst[:, c * 1024:(c + 1) * 1024], pX[:, :], AF.Gelu,
                                     scale=sch[:, c:c + 1]).then_inc(S.sem["aS"], 1)



def emit_attn0(nc, block, S, sh, dr, snap):
    AT = 512
    NTI = TPC // AT
    dxin, dxout = dr["xin"], dr["xout"]
    dwqkv, dbqkv, dwout, dbout, dpos = dr["wqkvb"], dr["bqkv"], dr["woutb"], dr["boutb"], dr["pos2"]
    dones = dr["onesb"]
    consts, ident32, identbf = sh["consts"], sh["ident32"], sh["identbf"]
    xchunk, sqscr, stats = sh["xchunk"], sh["sqscr"], sh["stats"]
    xr4, outv, outp = sh["xr4"], sh["outv"], sh["outp"]
    S.inc("dX", n=4); S.setmark(("init",), "dX")
    S.inc("gD", n=16); S.setmark(("winit",), "gD")
    S.inc("vS", ("onesbf",))
    for ti in range(NTI):
        for tc in range(4):
            G = ti * 4 + tc
            S.inc("dX", ("a", "xload", G))
            ln_prepass(S, "a", G)
        for oc in range(24):
            S.inc("tS", ("mmQ", ti, oc))
            S.inc("vS", ("qev", ti, oc))
        for si in range(3):
            for tc in range(4):
                S.inc("tS", ("trQ", ti, si, tc))
                S.inc("aS", ("tevQ", ti, si, tc))
        for tc in range(4):
            S.inc("pS", ("qkp", ti, tc))
            S.inc("vS", ("nm8", ti, tc))
            S.inc("aS", ("exp", ti, tc))
            S.inc("pS", ("pvp", ti, tc))
            S.inc("vS", ("otok", ti, tc))
        for tc in range(4):
            S.inc("tS", ("trO", ti, tc))
            S.inc("aS", ("tevO", ti, tc))
        for tc in range(4):
            idx = ti * 4 + tc
            S.inc("tS", ("mmW", ti, tc))
            S.inc("dX", ("wxr", idx))
            S.inc("vS", ("oadd", idx))
            S.inc("dS", ("store", idx))

    with ExitStack() as st:
        sb = lambda n, shape, dt: st.enter_context(nc.sbuf_tensor("sb_" + S.pfx + n, shape, dt))
        xln_tok = [sb("xlntok0", [128, 1024], BF16), sb("xlntok1", [128, 1024], BF16)]
        xlnT = sb("xlnT", [128, 4096], BF16)
        wqkv_sb = sb("wqkv", [128, 8 * 3072], BF16)
        wout_sb = sb("wout", [128, 8 * 1024], BF16)
        bqkv = sb("bqkv", [128, 24], F32)
        boutb = sb("boutb", [128, 1024], F32)
        pos2 = sb("pos2", [128, 2048], F32)
        qT = sb("qT", [128, 4096], BF16)
        kT = sb("kT", [128, 4096], BF16)
        vT = sb("vT", [128, 4096], BF16)
        q_tok = sb("q_tok", [128, 4096], BF16)
        k_tok = sb("k_tok", [128, 4096], BF16)
        v_tok = sb("v_tok", [128, 4096], BF16)
        prod = sb("prod", [128, 1024], BF16)
        prod2 = sb("prod2", [128, 1024], BF16)
        prodP = sb("prodP", [128, 1024], F32)
        prodP2 = sb("prodP2", [128, 1024], F32)
        Sm = sb("Sm", [128, 256], F32)
        Ee = sb("Ee", [128, 256], BF16)
        esum = sb("esum", [128, 16], F32)
        rec = sb("rec", [128, 16], F32)
        nm8 = sb("nm8", [128, 16], F32)
        Oraw = sb("Oraw", [128, 1024], F32)
        O_bf = sb("O_bf", [128, 4096], BF16)
        OT = sb("OT", [128, 4096], BF16)
        ones1 = sb("ones1", [1, 128], F32)
        onesbf = sb("onesbf", [1, 128], BF16)
        bout_bf = sb("bout_bf", [1, 1024], BF16)
        pQ2 = st.enter_context(nc.psum_tensor(S.pfx + "pQ2", [128, 1024], F32))
        pTR = st.enter_context(nc.psum_tensor(S.pfx + "pTR", [128, 1024], F32))
        pW = [st.enter_context(nc.psum_tensor(S.pfx + "pW0", [128, 1024], F32)),
              st.enter_context(nc.psum_tensor(S.pfx + "pW1", [128, 1024], F32))]

        pTRb = pTR[:, :].bitcast(BF16)
        pTR_slots = [pTRb[:, r * 128:(r + 1) * 128] for r in range(8)]

        def xlnT_dst3d(tc):
            return xlnT[:, :].rearrange("p (r t) -> p r t", t=512)[:, :, tc * 128:(tc + 1) * 128]

        def pTR_src3d():
            return pTRb.rearrange("p (r t) -> p r t", t=128)[:, 0:8, :]

        toks = None

        @block.sync
        def _(sync):
            S.barrier(sync, snap)
            for dst, srcdram in ((bqkv, dbqkv), (boutb, dbout), (pos2, dpos),
                                 (ones1, dones)):
                sync.dma_start(out=dst[:, :], in_=srcdram[:, :]).then_inc(S.sem["dX"], 16)
            for ti in range(NTI):
                for tc in range(4):
                    G = ti * 4 + tc
                    if G >= 2:
                        S.wait(sync, ("a", "apply", G - 2))
                    sync.dma_start(out=xchunk[G % 2][:, :],
                                   in_=dxin[G * 128:(G + 1) * 128, :]).then_inc(S.sem["dX"], 16)
                for tc in range(4):
                    idx = ti * 4 + tc
                    if idx >= 1:
                        S.wait(sync, ("oadd", idx - 1))
                    row0 = idx * 128
                    sync.dma_start(out=xr4[:, :],
                                   in_=dxin[row0:row0 + 128, :]).then_inc(S.sem["dX"], 16)
                    S.wait(sync, ("oadd", idx))
                    sync.dma_start(out=dxout[row0:row0 + 128, :],
                                   in_=outp[:, :]).then_inc(S.sem["dS"], 16)
            sync.wait_ge(S.sem["dS"], 16 * S.final("dS"))

        @block.gpsimd
        def _(g):
            S.barrier(g, snap)
            for r in range(8):
                g.dma_start(out=wqkv_sb[:, r * 3072:(r + 1) * 3072],
                            in_=dwqkv[r][:, :]).then_inc(S.sem["gD"], 16)
            for r in range(8):
                g.dma_start(out=wout_sb[:, r * 1024:(r + 1) * 1024],
                            in_=dwout[r][:, :]).then_inc(S.sem["gD"], 16)
            for ti in range(NTI):
                for tc in range(4):
                    waits = []
                    if tc == 0:
                        waits += [("tevQ", ti, 0, 3), ("tevQ", ti, 1, 3)]
                    if ti > 0 or tc > 0:
                        pt, pc = (ti, tc - 1) if tc > 0 else (ti - 1, 3)
                        waits.append(("exp", pt, pc))
                    if waits:
                        S.wait_max(g, waits)
                    k3 = k_tok[:, tc * 1024:(tc + 1) * 1024].rearrange("p (g d) -> p g d", d=64)
                    p3 = prodP[:, :].rearrange("p (g d) -> p g d", d=64)
                    for h in range(HV, 16):
                        q1 = q_tok[:, tc * 1024 + h * 64: tc * 1024 + (h + 1) * 64]
                        g.tensor_tensor(p3, q1.unsqueeze(1).broadcast_to([128, 16, 64]),
                                        k3, ALU.mult)
                        g.drain()
                        w = 32
                        while w >= 2:
                            g.tensor_add(p3[:, :, 0:w], p3[:, :, 0:w], p3[:, :, w:2 * w])
                            g.drain()
                            w //= 2
                        g.tensor_add(Sm[:, h * 16:(h + 1) * 16].unsqueeze(2),
                                     p3[:, :, 0:1], p3[:, :, 1:2])
                        g.drain()
                    g.engine_nop().then_inc(S.sem["pS"], 1)
                    waits = [("exp", ti, tc)]
                    if tc == 0:
                        waits.append(("tevQ", ti, 2, 3))
                    if ti > 0 or tc > 0:
                        pt, pc = (ti, tc - 1) if tc > 0 else (ti - 1, 3)
                        waits.append(("otok", pt, pc))
                    S.wait_max(g, waits)
                    v3 = v_tok[:, tc * 1024:(tc + 1) * 1024].rearrange("p (d g) -> p d g", g=16)
                    p32 = prodP2[:, :].rearrange("p (d g) -> p d g", g=16)
                    for h in range(HV, 16):
                        e1 = Ee[:, h * 16:(h + 1) * 16]
                        g.tensor_tensor(p32, e1.unsqueeze(1).broadcast_to([128, 64, 16]),
                                        v3, ALU.mult)
                        g.drain()
                        w = 8
                        while w >= 2:
                            g.tensor_add(p32[:, :, 0:w], p32[:, :, 0:w], p32[:, :, w:2 * w])
                            g.drain()
                            w //= 2
                        g.tensor_add(Oraw[:, h * 64:(h + 1) * 64].unsqueeze(2),
                                     p32[:, :, 0:1], p32[:, :, 1:2])
                        g.drain()
                    g.engine_nop().then_inc(S.sem["pS"], 1)

        @block.tensor
        def _(t):
            S.barrier(t, snap)
            ptr_prev = []
            for ti in range(NTI):
                for tc in range(4):
                    G = ti * 4 + tc
                    waits = list(ptr_prev)
                    ln_emit_tensor(t, S, "a", G, xln_tok[G % 2], identbf, pTR_slots, waits)
                    ptr_prev = [("a", "xevac", G)]
                for oc in range(24):
                    waits = []
                    if oc == 0:
                        waits += [("a", "xevac", ti * 4 + 3), ("winit",)]
                    if oc >= 2:
                        waits.append(("qev", ti, oc - 2))
                    if waits:
                        S.wait_max(t, waits)
                    ins = None
                    for r in range(8):
                        ins = t.matmul(pQ2[:, (oc % 2) * 512:(oc % 2) * 512 + 512],
                                       wqkv_sb[:, r * 3072 + oc * 128: r * 3072 + (oc + 1) * 128],
                                       xlnT[:, r * 512:(r + 1) * 512],
                                       start=(r == 0), stop=(r == 7))
                    ins.then_inc(S.sem["tS"], 1)
                for si, srcT in enumerate((qT, kT, vT)):
                    for tc in range(4):
                        waits = list(ptr_prev)
                        if tc == 0:
                            waits.append(("qev", ti, si * 8 + 7))
                        S.wait_max(t, waits)
                        ins = None
                        for j in range(8):
                            ins = t.transpose(pTR_slots[j],
                                              srcT[:, j * 512 + tc * 128: j * 512 + (tc + 1) * 128],
                                              identbf[:, :])
                        ins.then_inc(S.sem["tS"], 1)
                        ptr_prev = [("tevQ", ti, si, tc)]
                for tc in range(4):
                    waits = list(ptr_prev) + [("otok", ti, tc)]
                    S.wait_max(t, waits)
                    ins = None
                    for j in range(8):
                        ins = t.transpose(pTR_slots[j],
                                          O_bf[:, tc * 1024 + j * 128: tc * 1024 + (j + 1) * 128],
                                          identbf[:, :])
                    ins.then_inc(S.sem["tS"], 1)
                    ptr_prev = [("tevO", ti, tc)]
                for tc in range(4):
                    idx = ti * 4 + tc
                    waits = [("tevO", ti, tc)]
                    if idx == 0:
                        waits.append(("onesbf",))
                    if idx >= 2:
                        waits.append(("oadd", idx - 2))
                    S.wait_max(t, waits)
                    ins = None
                    for n in range(2):
                        ins = t.matmul(pW[tc % 2][:, n * 512:(n + 1) * 512],
                                       onesbf[0:1, :],
                                       bout_bf[0:1, n * 512:(n + 1) * 512],
                                       start=True, stop=False)
                    for r in range(8):
                        for n in range(2):
                            ins = t.matmul(pW[tc % 2][:, n * 512:(n + 1) * 512],
                                           OT[:, r * 512 + tc * 128: r * 512 + (tc + 1) * 128],
                                           wout_sb[:, r * 1024 + n * 512: r * 1024 + (n + 1) * 512],
                                           start=False, stop=(r == 7))
                    ins.then_inc(S.sem["tS"], 1)

        @block.vector
        def _(v):
            S.barrier(v, snap)
            S.wait(v, ("init",))
            v.tensor_copy(onesbf[0:1, :], ones1[0:1, :])
            v.drain()
            v.tensor_copy(bout_bf[0:1, :], boutb[0:1, :]).then_inc(S.sem["vS"], 1)
            for ti in range(NTI):
                for tc in range(4):
                    G = ti * 4 + tc
                    extra = [("a", "tr", G - 2)] if G >= 2 else []
                    ln_emit_vector(v, S, "a", G, xchunk[G % 2], stats, consts,
                                   pTR_src3d(), xlnT_dst3d(tc), xln_tok[G % 2], extra)
                for oc in range(24):
                    S.wait(v, ("mmQ", ti, oc))
                    ps = pQ2[:, (oc % 2) * 512:(oc % 2) * 512 + 512]
                    if oc < 8:
                        v.tensor_scalar_add(qT[:, oc * 512:(oc + 1) * 512], ps,
                                            bqkv[:, oc:oc + 1]).then_inc(S.sem["vS"], 1)
                    elif oc < 16:
                        j = oc - 8
                        v.scalar_tensor_tensor(kT[:, j * 512:(j + 1) * 512], ps,
                                               bqkv[:, oc:oc + 1],
                                               pos2[:, (ti % 4) * 512:(ti % 4 + 1) * 512],
                                               ALU.add, ALU.add).then_inc(S.sem["vS"], 1)
                    else:
                        j = oc - 16
                        v.tensor_scalar_add(vT[:, j * 512:(j + 1) * 512], ps,
                                            bqkv[:, oc:oc + 1]).then_inc(S.sem["vS"], 1)
                for tc in range(4):
                    if tc == 0:
                        S.wait_max(v, [("tevQ", ti, 0, 3), ("tevQ", ti, 1, 3)])
                    k3 = k_tok[:, tc * 1024:(tc + 1) * 1024].rearrange("p (g d) -> p g d", d=64)
                    for h in range(HV):
                        q1 = q_tok[:, tc * 1024 + h * 64: tc * 1024 + (h + 1) * 64]
                        v.tensor_tensor(prod[:, :].rearrange("p (g d) -> p g d", d=64),
                                        q1.unsqueeze(1).broadcast_to([128, 16, 64]), k3, ALU.mult)
                        v.drain()
                        v.tensor_reduce(Sm[:, h * 16:(h + 1) * 16],
                                        prod[:, :].rearrange("p (g d) -> p g d", d=64),
                                        axis=AX.X, op=ALU.add)
                        v.drain()
                    S.wait(v, ("qkp", ti, tc))
                    v.tensor_reduce(nm8[:, :], Sm[:, :].rearrange("p (h g) -> p h g", g=16),
                                    axis=AX.X, op=ALU.max, negate=True)
                    v.drain()
                    v.tensor_scalar_mul(nm8[:, :], nm8[:, :], consts[:, C_QK8:C_QK8 + 1])
                    v.drain()
                    v.engine_nop().then_inc(S.sem["vS"], 1)
                    S.wait(v, ("exp", ti, tc))
                    v.reciprocal(rec[:, :], esum[:, :])
                    v.drain()
                    v3 = v_tok[:, tc * 1024:(tc + 1) * 1024].rearrange("p (d g) -> p d g", g=16)
                    if tc == 0:
                        S.wait(v, ("tevQ", ti, 2, 3))
                    for h in range(HV):
                        e1 = Ee[:, h * 16:(h + 1) * 16]
                        v.tensor_tensor(prod2[:, :].rearrange("p (d g) -> p d g", g=16),
                                        e1.unsqueeze(1).broadcast_to([128, 64, 16]), v3, ALU.mult)
                        v.drain()
                        v.tensor_reduce(Oraw[:, h * 64:(h + 1) * 64],
                                        prod2[:, :].rearrange("p (d g) -> p d g", g=16),
                                        axis=AX.X, op=ALU.add)
                        v.drain()
                    S.wait(v, ("pvp", ti, tc))
                    if ti >= 1:
                        S.wait(v, ("trO", ti - 1, tc))
                    v.tensor_tensor(O_bf[:, tc * 1024:(tc + 1) * 1024].rearrange("p (h d) -> p h d", d=64),
                                    Oraw[:, :].rearrange("p (h d) -> p h d", d=64),
                                    rec[:, :].unsqueeze(2).broadcast_to([128, 16, 64]),
                                    ALU.mult).then_inc(S.sem["vS"], 1)
                for tc in range(4):
                    idx = ti * 4 + tc
                    waits = [("mmW", ti, tc), ("wxr", idx)]
                    if idx >= 1:
                        waits.append(("store", idx - 1))
                    S.wait_max(v, waits)
                    v.tensor_add(outp[:, :], pW[tc % 2][:, :], xr4[:, :]).then_inc(S.sem["vS"], 1)

        @block.scalar
        def _(s):
            S.barrier(s, snap)
            for ti in range(NTI):
                for tc in range(4):
                    G = ti * 4 + tc
                    ln_emit_scalar(s, S, "a", G, xchunk[G % 2], stats, consts, sqscr)
                for si, dtok in enumerate((q_tok, k_tok, v_tok)):
                    for tc in range(4):
                        S.wait(s, ("trQ", ti, si, tc))
                        if si < 2:
                            dst3 = dtok[:, tc * 1024:(tc + 1) * 1024].rearrange("p (j k) -> p j k", k=128)
                            s.copy(dst3, pTR_src3d()).then_inc(S.sem["aS"], 1)
                        else:
                            dst3 = dtok[:, tc * 1024:(tc + 1) * 1024].rearrange(
                                "p (d g) -> p g d", g=16)
                            src3 = pTRb.rearrange("p (j d) -> p j d", d=64)[:, 0:16, :]
                            s.copy(dst3, src3).then_inc(S.sem["aS"], 1)
                for tc in range(4):
                    waits = [("nm8", ti, tc)]
                    if ti > 0 or tc > 0:
                        pt, pc = (ti, tc - 1) if tc > 0 else (ti - 1, 3)
                        waits.append(("pvp", pt, pc))
                    S.wait_max(s, waits)
                    ins = None
                    for h in range(16):
                        ins = s.activation(Ee[:, h * 16:(h + 1) * 16], Sm[:, h * 16:(h + 1) * 16],
                                           AF.Exp, scale=consts[:, C_QK8:C_QK8 + 1],
                                           bias=nm8[:, h:h + 1], accum_out=esum[:, h:h + 1])
                    ins.then_inc(S.sem["aS"], 1)
                for tc in range(4):
                    S.wait(s, ("trO", ti, tc))
                    dst3 = OT[:, :].rearrange("p (j t) -> p j t", t=512)[:, :, tc * 128:(tc + 1) * 128]
                    s.copy(dst3, pTR_src3d()).then_inc(S.sem["aS"], 1)


def attn0_inputs(w, xcore):
    return {"xin": xcore, "wqkvb": w["wqkvb"], "bqkv": w["bqkv"], "woutb": w["woutb"],
            "boutb": w["boutb"], "pos2": w["pos2"], "consts": w["consts"], "ident": w["ident"]}



def emit_attn1(nc, block, S, sh, dr, snap):
    NSEQ = TPC // SEQ
    NCH = SEQ // 128
    dxin, dxout = dr["xin"], dr["xout"]
    dwq, dwk, dwv = dr["wqb"], dr["wkb"], dr["wvb"]
    dbq, dbk, dbv, dwo, dbo = dr["bqc"], dr["bkc"], dr["bvb"], dr["wob"], dr["bob"]
    dones = dr["onesl"]
    consts, ident32, identbf = sh["consts"], sh["ident32"], sh["identbf"]
    xchunk, sqscr, stats = sh["xchunk"], sh["sqscr"], sh["stats"]
    xr4, outv, outp = sh["xr4"], sh["outv"], sh["outp"]
    S.inc("dX", n=5); S.setmark(("init",), "dX")
    S.inc("vS", ("ones1",))
    for q in range(NSEQ):
        for cc in range(NCH):
            G = q * NCH + cc
            S.inc("dX", ("l", "xload", G))
            ln_prepass(S, "l", G)
        for stg in range(4):
            S.inc("gD", n=8); S.setmark(("wld", q, stg), "gD")
        for stg in range(2):
            for oc in range(8):
                S.inc("tS", ("mmP", q, stg, oc))
                S.inc("vS", ("pev", q, stg, oc))
        for tc in range(16):
            S.inc("tS", ("mmV", q, tc))
            S.inc("vS", ("vev", q, tc))
        for h in range(16):
            S.inc("vS", ("csq", q, h, 0))
            S.inc("tS", ("cmm", q, h, 0))
            S.inc("vS", ("crd", q, h, 0))
            S.inc("vS", ("csq", q, h, 1))
            S.inc("tS", ("cmm", q, h, 1))
            S.inc("vS", ("crd", q, h, 1))
            S.inc("vS", ("cc2", q, h))
            S.inc("aS", ("cexp", q, h))
            S.inc("vS", ("cneg", q, h))
            S.inc("tS", ("cbc", q, h))
            S.inc("vS", ("cbe", q, h))
        first, last = q * 256, q * 256 + 255
        S.inc("tS", ("mmS", first))
        for ai in range(first, last + 1):
            S.inc("aS", ("exp", ai))
            if ai > first:
                S.inc("tS", ("trB", ai - 1))
                S.inc("aS", ("tevB", ai - 1))
            S.inc("tS", ("trP", ai))
            S.inc("aS", ("tevPa", ai))
            S.inc("vS", ("tevPd", ai))
            if ai < last:
                S.inc("tS", ("mmS", ai + 1))
            S.inc("tS", ("mmO", ai))
            S.inc("vS", ("oev", ai))
        S.inc("tS", ("trB", last))
        S.inc("aS", ("tevB", last))
        for sc in range(16):
            oi = q * 16 + sc
            S.inc("tS", ("mmF", oi))
            S.inc("dX", ("oxr", oi))
            S.inc("vS", ("oev1", oi))
            S.inc("pS", ("oev2", oi))
            S.inc("dS", ("store", oi))

    with ExitStack() as st:
        sb = lambda n, shape, dt: st.enter_context(nc.sbuf_tensor("sb_" + S.pfx + n, shape, dt))
        xln_tok = [sb("xlntok0", [128, 1024], BF16), sb("xlntok1", [128, 1024], BF16)]
        xlnT = sb("xlnT", [128, 16384], BF16)
        attT = xlnT
        wsh = sb("wsh", [128, 8192], BF16)
        bqc = sb("bqc", [128, 8], F32)
        bkc = sb("bkc", [128, 8], F32)
        bvb = sb("bvb", [128, 1024], F32)
        bob = sb("bob", [128, 1024], F32)
        qT = sb("qT", [128, 16384], BF16)
        kT = sb("kT", [128, 16384], BF16)
        v_tok = sb("v_tok", [128, 16384], BF16)
        P_sb = [sb("P_sb0", [128, 2048], BF16), sb("P_sb1", [128, 2048], BF16)]
        PT = sb("PT", [128, 2048], BF16)
        O_s = [sb("O_s0", [128, 64], BF16), sb("O_s1", [128, 64], BF16)]
        sstat = sb("sstat", [128, 8], F32)
        ones_f = sb("ones_f", [128, 128], F32)
        ones_bf = sb("ones_bf", [128, 128], BF16)
        csc = sb("csc", [1, 8], F32)
        cneg_bf = sb("cneg_bf", [1, 1], BF16)
        cb = sb("cb", [128, 32], F32)
        pS = st.enter_context(nc.psum_tensor(S.pfx + "pS", [128, 2048], F32))
        pTr = st.enter_context(nc.psum_tensor(S.pfx + "pTr", [128, 1024], F32))
        pO = st.enter_context(nc.psum_tensor(S.pfx + "pO", [128, 128], F32))

        pTrb = pTr[:, :].bitcast(BF16)
        pTR_slots = [pTrb[:, r * 128:(r + 1) * 128] for r in range(16)]

        def xlnT_dst3d(cc):
            return xlnT[:, :].rearrange("p (r t) -> p r t", t=2048)[:, :, cc * 128:(cc + 1) * 128]

        def pTR_src3d(n=8):
            return pTrb.rearrange("p (r t) -> p r t", t=128)[:, 0:n, :]

        qpb = lambda h: (h % 2) * 64

        @block.sync
        def _(sync):
            S.barrier(sync, snap)
            for dst, srcdram in ((bqc, dbq), (bkc, dbk), (bvb, dbv), (bob, dbo),
                                 (ones_f, dones)):
                sync.dma_start(out=dst[:, :], in_=srcdram[:, :]).then_inc(S.sem["dX"], 16)
            for q in range(NSEQ):
                for cc in range(NCH):
                    G = q * NCH + cc
                    if G >= 2:
                        S.wait(sync, ("l", "apply", G - 2))
                    sync.dma_start(out=xchunk[G % 2][:, :],
                                   in_=dxin[G * 128:(G + 1) * 128, :]).then_inc(S.sem["dX"], 16)
                for sc in range(16):
                    oi = q * 16 + sc
                    if oi >= 1:
                        S.wait(sync, ("oev2", oi - 1))
                    row0 = oi * 128
                    sync.dma_start(out=xr4[:, :],
                                   in_=dxin[row0:row0 + 128, :]).then_inc(S.sem["dX"], 16)
                    S.wait(sync, ("oev2", oi))
                    sync.dma_start(out=dxout[row0:row0 + 128, :],
                                   in_=outp[:, :]).then_inc(S.sem["dS"], 16)
            sync.wait_ge(S.sem["dS"], 16 * S.final("dS"))

        @block.gpsimd
        def _(g):
            S.barrier(g, snap)
            for q in range(NSEQ):
                for stg, dw in enumerate((dwq, dwk, dwv, dwo)):
                    if stg == 0 and q >= 1:
                        S.wait(g, ("mmF", q * 16 - 1))
                    elif stg == 1:
                        S.wait(g, ("mmP", q, 0, 7))
                    elif stg == 2:
                        S.wait(g, ("mmP", q, 1, 7))
                    elif stg == 3:
                        S.wait(g, ("mmV", q, 15))
                    for r in range(8):
                        g.dma_start(out=wsh[:, r * 1024:(r + 1) * 1024],
                                    in_=dw[r][:, :]).then_inc(S.sem["gD"], 16)
                for sc in range(16):
                    oi = q * 16 + sc
                    S.wait_max(g, [("oev1", oi), ("oxr", oi)] +
                               ([("store", oi - 1)] if oi >= 1 else []))
                    g.tensor_add(outp[:, :], outv[:, :], xr4[:, :]).then_inc(S.sem["pS"], 1)

        @block.tensor
        def _(t):
            S.barrier(t, snap)
            ptr_prev = []
            for q in range(NSEQ):
                for cc in range(NCH):
                    G = q * NCH + cc
                    waits = list(ptr_prev)
                    ln_emit_tensor(t, S, "l", G, xln_tok[G % 2], identbf,
                                   pTR_slots[0:8], waits)
                    ptr_prev = [("l", "xevac", G)]
                for stg in range(2):
                    for oc in range(8):
                        waits = [("wld", q, stg)]
                        if stg == 0 and oc == 0:
                            waits.append(("l", "xevac", q * NCH + NCH - 1))
                        prev = ("pev", q, stg, oc - 1) if oc >= 1 else \
                               (("pev", q, 0, 7) if stg == 1 else None)
                        if prev is None and q >= 1 and stg == 0:
                            prev = ("mmF", q * 16 - 1)
                            prev = ("oev1", q * 16 - 1)
                        if prev is not None:
                            waits.append(prev)
                        S.wait_max(t, waits)
                        ins = None
                        for n in range(4):
                            for r in range(8):
                                ins = t.matmul(pS[:, n * 512:(n + 1) * 512],
                                               wsh[:, r * 1024 + oc * 128: r * 1024 + (oc + 1) * 128],
                                               xlnT[:, r * 2048 + n * 512: r * 2048 + (n + 1) * 512],
                                               start=(r == 0), stop=(r == 7))
                        ins.then_inc(S.sem["tS"], 1)
                for tc in range(16):
                    waits = [("wld", q, 2)]
                    if tc < 2:
                        waits.append(("pev", q, 1, 7))
                    else:
                        waits.append(("vev", q, tc - 2))
                    S.wait_max(t, waits)
                    half = (tc % 2) * 1024
                    ins = None
                    for r in range(8):
                        for n in range(2):
                            ins = t.matmul(pS[:, half + n * 512: half + (n + 1) * 512],
                                           xlnT[:, r * 2048 + tc * 128: r * 2048 + (tc + 1) * 128],
                                           wsh[:, r * 1024 + n * 512: r * 1024 + (n + 1) * 512],
                                           start=(r == 0), stop=(r == 7))
                    ins.then_inc(S.sem["tS"], 1)
                def emit_mmS(ai2):
                    h2, sc2 = (ai2 // 16) % 16, ai2 % 16
                    ins = None
                    for n in range(4):
                        ins = t.matmul(pS[:, n * 512:(n + 1) * 512],
                                       qT[qpb(h2):qpb(h2) + 64,
                                          (h2 // 2) * 2048 + sc2 * 128: (h2 // 2) * 2048 + (sc2 + 1) * 128],
                                       kT[qpb(h2):qpb(h2) + 64,
                                          (h2 // 2) * 2048 + n * 512: (h2 // 2) * 2048 + (n + 1) * 512],
                                       start=True, stop=True)
                    ins.then_inc(S.sem["tS"], 1)

                first, last = q * 256, q * 256 + 255
                for h2 in range(16):
                    b = qpb(h2)
                    waits = [("csq", q, h2, 0)]
                    if h2 == 0:
                        waits.append(("vev", q, 15))
                        if q >= 1:
                            waits.append(("oev1", q * 16 - 1))
                    S.wait_max(t, waits)
                    ins = None
                    for n in range(4):
                        ins = t.matmul(pS[0:1, n * 512:(n + 1) * 512],
                                       ones_bf[b:b + 64, 0:1],
                                       P_sb[0][b:b + 64, n * 512:(n + 1) * 512],
                                       start=True, stop=True)
                    ins.then_inc(S.sem["tS"], 1)
                    S.wait_max(t, [("csq", q, h2, 1), ("crd", q, h2, 0)])
                    ins = None
                    for n in range(4):
                        ins = t.matmul(pS[0:1, n * 512:(n + 1) * 512],
                                       ones_bf[b:b + 64, 0:1],
                                       P_sb[0][b:b + 64, n * 512:(n + 1) * 512],
                                       start=True, stop=True)
                    ins.then_inc(S.sem["tS"], 1)
                    S.wait(t, ("cneg", q, h2))
                    t.matmul(pO[:, 0:1], ones_bf[0:1, :], cneg_bf[0:1, 0:1],
                             start=True, stop=True).then_inc(S.sem["tS"], 1)
                S.wait(t, ("crd", q, 15, 1))
                emit_mmS(first)
                for ai in range(first, last + 1):
                    h, pi = (ai // 16) % 16, ai % 2
                    if ai > first:
                        S.wait(t, ("oev", ai - 1))
                        t.transpose(pTrb[0:64, 0:128], O_s[(ai - 1) % 2][:, :],
                                    identbf[:, :]).then_inc(S.sem["tS"], 1)
                    waits = [("exp", ai)]
                    if ai >= 1:
                        waits.append(("tevB", ai - 1))
                    if ai == first:
                        waits.append(("l", "xevac", q * NCH + NCH - 1))
                    S.wait_max(t, waits)
                    ins = None
                    for j in range(16):
                        ins = t.transpose(pTR_slots[j],
                                          P_sb[pi][:, j * 128:(j + 1) * 128], identbf[:, :])
                    ins.then_inc(S.sem["tS"], 1)
                    if ai < last:
                        emit_mmS(ai + 1)
                    waits = [("tevPa", ai), ("tevPd", ai)]
                    if ai >= 2:
                        waits.append(("oev", ai - 2))
                    S.wait_max(t, waits)
                    ins = None
                    for j in range(16):
                        ins = t.matmul(pO[:, pi * 64:(pi + 1) * 64],
                                       PT[:, j * 128:(j + 1) * 128],
                                       v_tok[:, j * 1024 + h * 64: j * 1024 + (h + 1) * 64],
                                       start=(j == 0), stop=(j == 15))
                    ins.then_inc(S.sem["tS"], 1)
                S.wait(t, ("oev", last))
                t.transpose(pTrb[0:64, 0:128], O_s[last % 2][:, :],
                            identbf[:, :]).then_inc(S.sem["tS"], 1)
                ptr_prev = [("tevB", last)]
                for sc in range(16):
                    oi = q * 16 + sc
                    waits = []
                    if sc == 0:
                        waits += [("tevB", (q * 16 + 15) * 16 + 15), ("exp", (q * 16 + 15) * 16 + 15),
                                  ("wld", q, 3)]
                    if oi >= 2:
                        waits.append(("oev1", oi - 2))
                    if waits:
                        S.wait_max(t, waits)
                    half = (sc % 2) * 1024
                    ins = None
                    for r in range(8):
                        for n in range(2):
                            ins = t.matmul(pS[:, half + n * 512: half + (n + 1) * 512],
                                           attT[:, r * 2048 + sc * 128: r * 2048 + (sc + 1) * 128],
                                           wsh[:, r * 1024 + n * 512: r * 1024 + (n + 1) * 512],
                                           start=(r == 0), stop=(r == 7))
                    ins.then_inc(S.sem["tS"], 1)

        @block.vector
        def _(v):
            S.barrier(v, snap)
            S.wait(v, ("init",))
            v.tensor_copy(ones_bf[:, :], ones_f[:, :]).then_inc(S.sem["vS"], 1)
            for q in range(NSEQ):
                for cc in range(NCH):
                    G = q * NCH + cc
                    extra = [("l", "tr", G - 2)] if G >= 2 else []
                    ln_emit_vector(v, S, "l", G, xchunk[G % 2], stats, consts,
                                   pTR_src3d(8), xlnT_dst3d(cc), xln_tok[G % 2], extra)
                for stg in range(2):
                    bc = bqc if stg == 0 else bkc
                    dst = qT if stg == 0 else kT
                    for oc in range(8):
                        S.wait(v, ("mmP", q, stg, oc))
                        if stg == 0:
                            v.tensor_scalar(dst[:, oc * 2048:(oc + 1) * 2048], pS[:, :],
                                            bc[:, oc:oc + 1], consts[:, C_QK8:C_QK8 + 1],
                                            ALU.add, ALU.mult).then_inc(S.sem["vS"], 1)
                        else:
                            v.tensor_scalar_add(dst[:, oc * 2048:(oc + 1) * 2048], pS[:, :],
                                                bc[:, oc:oc + 1]).then_inc(S.sem["vS"], 1)
                for tc in range(16):
                    S.wait(v, ("mmV", q, tc))
                    half = (tc % 2) * 1024
                    v.tensor_add(v_tok[:, tc * 1024:(tc + 1) * 1024],
                                 pS[:, half:half + 1024], bvb[:, :]).then_inc(S.sem["vS"], 1)
                first, last = q * 256, q * 256 + 255
                for h2 in range(16):
                    b = qpb(h2)
                    ksl = kT[b:b + 64, (h2 // 2) * 2048:(h2 // 2 + 1) * 2048]
                    qsl = qT[b:b + 64, (h2 // 2) * 2048:(h2 // 2 + 1) * 2048]
                    waits = []
                    if h2 == 0 and q >= 1:
                        waits.append(("trP", q * 256 - 2))
                    if h2 == 0:
                        waits.append(("pev", q, 1, 7))
                    if h2 >= 1:
                        waits.append(("cmm", q, h2 - 1, 1))
                    if waits:
                        S.wait_max(v, waits)
                    v.tensor_mul(P_sb[0][b:b + 64, :], ksl, ksl).then_inc(S.sem["vS"], 1)
                    S.wait(v, ("cmm", q, h2, 0))
                    v.reduce_max(csc[0:1, 0:1], pS[0:1, :], axis=AX.X)
                    v.drain()
                    v.engine_nop().then_inc(S.sem["vS"], 1)
                    v.tensor_mul(P_sb[0][b:b + 64, :], qsl, qsl).then_inc(S.sem["vS"], 1)
                    S.wait(v, ("cmm", q, h2, 1))
                    v.reduce_max(csc[0:1, 1:2], pS[0:1, :], axis=AX.X)
                    v.drain()
                    v.engine_nop().then_inc(S.sem["vS"], 1)
                    v.tensor_mul(csc[0:1, 2:3], csc[0:1, 0:1],
                                 csc[0:1, 1:2]).then_inc(S.sem["vS"], 1)
                    S.wait(v, ("cexp", q, h2))
                    v.tensor_scalar_mul(cneg_bf[0:1, 0:1], csc[0:1, 4:5],
                                        consts[0:1, C_NEG1:C_NEG1 + 1]).then_inc(S.sem["vS"], 1)
                    S.wait(v, ("cbc", q, h2))
                    v.tensor_copy(cb[:, q * 16 + h2:q * 16 + h2 + 1],
                                  pO[:, 0:1]).then_inc(S.sem["vS"], 1)
                for ai in range(first, last + 1):
                    pi = ai % 2
                    S.wait(v, ("trP", ai))
                    v.tensor_copy(PT[:, 1280:2048],
                                  pTrb[:, 1280:2048]).then_inc(S.sem["vS"], 1)
                    waits = [("mmO", ai)]
                    if ai >= 2:
                        waits.append(("trB", ai - 2))
                    S.wait_max(v, waits)
                    v.reciprocal(sstat[:, 4 + pi:5 + pi], sstat[:, 2 + pi:3 + pi])
                    v.drain()
                    v.tensor_scalar_mul(O_s[pi][:, :], pO[:, pi * 64:(pi + 1) * 64],
                                        sstat[:, 4 + pi:5 + pi]).then_inc(S.sem["vS"], 1)
                for sc in range(16):
                    oi = q * 16 + sc
                    waits = [("mmF", oi)]
                    if oi >= 1:
                        waits.append(("oev2", oi - 1))
                    S.wait_max(v, waits)
                    half = (sc % 2) * 1024
                    v.tensor_add(outv[:, :], pS[:, half:half + 1024],
                                 bob[:, :]).then_inc(S.sem["vS"], 1)

        @block.scalar
        def _(s):
            S.barrier(s, snap)
            for q in range(NSEQ):
                for cc in range(NCH):
                    G = q * NCH + cc
                    ln_emit_scalar(s, S, "l", G, xchunk[G % 2], stats, consts, sqscr)
                first, last = q * 256, q * 256 + 255
                for h2 in range(16):
                    S.wait(s, ("cc2", q, h2))
                    s.activation(csc[0:1, 3:4], csc[0:1, 2:3], AF.Ln)
                    s.drain()
                    s.activation(csc[0:1, 4:5], csc[0:1, 3:4], AF.Exp,
                                 scale=0.5).then_inc(S.sem["aS"], 1)
                for ai in range(first, last + 1):
                    h, sc, pi = (ai // 16) % 16, ai % 16, ai % 2
                    waits = [("mmS", ai)]
                    if sc == 0:
                        waits.append(("cbe", q, h))
                    if ai >= 2:
                        waits.append(("trP", ai - 2))
                    S.wait_max(s, waits)
                    s.activation(P_sb[pi][:, :], pS[:, :], AF.Exp,
                                 bias=cb[:, q * 16 + h:q * 16 + h + 1],
                                 accum_out=sstat[:, 2 + pi:3 + pi]).then_inc(S.sem["aS"], 1)
                    if ai > first:
                        h1, sc1 = ((ai - 1) // 16) % 16, (ai - 1) % 16
                        S.wait(s, ("trB", ai - 1))
                        s.copy(attT[qpb(h1):qpb(h1) + 64,
                                    (h1 // 2) * 2048 + sc1 * 128:(h1 // 2) * 2048 + (sc1 + 1) * 128],
                               pTrb[0:64, 0:128]).then_inc(S.sem["aS"], 1)
                    S.wait(s, ("trP", ai))
                    s.copy(PT[:, 0:1280], pTrb[:, 0:1280]).then_inc(S.sem["aS"], 1)
                h1, sc1 = 15, 15
                S.wait(s, ("trB", last))
                s.copy(attT[qpb(h1):qpb(h1) + 64,
                            (h1 // 2) * 2048 + sc1 * 128:(h1 // 2) * 2048 + (sc1 + 1) * 128],
                       pTrb[0:64, 0:128]).then_inc(S.sem["aS"], 1)


def attn1_inputs(w):
    return {"wqb": w["wqb"], "wkb": w["wkb"], "wvb": w["wvb"], "bqc": w["bqc"],
            "bkc": w["bkc"], "bvb": w["bvb"], "wob": w["wob"], "bob": w["bob"],
            "onesl": w["onesl"]}


def attn0_inputs(w):
    return {"wqkvb": w["wqkvb"], "bqkv": w["bqkv"], "woutb": w["woutb"],
            "boutb": w["boutb"], "pos2": w["pos2"], "onesb": w["onesb"]}


def ffn_inputs(w, L):
    keys = ("w1b", "w2b", "h1b", "h2b", "w3b", "sc1", "sc2", "sch1", "sch2", "s3b")
    return {k + str(L): w[k + str(L)] for k in keys}



def _dram_attn0(nc):
    return {
        "wqkvb": nc.dram_tensor("wqkvb", [8, 128, 3072], F32, kind="ExternalInput"),
        "bqkv": nc.dram_tensor("bqkv", [128, 24], F32, kind="ExternalInput"),
        "woutb": nc.dram_tensor("woutb", [8, 128, 1024], F32, kind="ExternalInput"),
        "boutb": nc.dram_tensor("boutb", [128, 1024], F32, kind="ExternalInput"),
        "pos2": nc.dram_tensor("pos2", [128, 2048], F32, kind="ExternalInput"),
        "onesb": nc.dram_tensor("onesb", [1, 128], F32, kind="ExternalInput"),
    }


def _dram_attn1(nc):
    return {
        "wqb": nc.dram_tensor("wqb", [8, 128, 1024], F32, kind="ExternalInput"),
        "wkb": nc.dram_tensor("wkb", [8, 128, 1024], F32, kind="ExternalInput"),
        "wvb": nc.dram_tensor("wvb", [8, 128, 1024], F32, kind="ExternalInput"),
        "bqc": nc.dram_tensor("bqc", [128, 8], F32, kind="ExternalInput"),
        "bkc": nc.dram_tensor("bkc", [128, 8], F32, kind="ExternalInput"),
        "bvb": nc.dram_tensor("bvb", [128, 1024], F32, kind="ExternalInput"),
        "wob": nc.dram_tensor("wob", [8, 128, 1024], F32, kind="ExternalInput"),
        "onesl": nc.dram_tensor("onesl", [128, 128], F32, kind="ExternalInput"),
        "bob": nc.dram_tensor("bob", [128, 1024], F32, kind="ExternalInput"),
    }


def _dram_ffn(nc, L):
    s = str(L)
    return {
        "w1b": nc.dram_tensor("w1b" + s, [32, 128, 8, 128], F32, kind="ExternalInput"),
        "w2b": nc.dram_tensor("w2b" + s, [32, 128, 8, 128], F32, kind="ExternalInput"),
        "h1b": nc.dram_tensor("h1b" + s, [32, 128, 32, 128], F32, kind="ExternalInput"),
        "h2b": nc.dram_tensor("h2b" + s, [32, 128, 32, 128], F32, kind="ExternalInput"),
        "w3b": nc.dram_tensor("w3b" + s, [32, 128, 1024], F32, kind="ExternalInput"),
        "sc1": nc.dram_tensor("sc1" + s, [128, 32], F32, kind="ExternalInput"),
        "sc2": nc.dram_tensor("sc2" + s, [128, 32], F32, kind="ExternalInput"),
        "sch1": nc.dram_tensor("sch1" + s, [128, 32], F32, kind="ExternalInput"),
        "sch2": nc.dram_tensor("sch2" + s, [128, 32], F32, kind="ExternalInput"),
        "s3b": nc.dram_tensor("s3b" + s, [128, 1024], F32, kind="ExternalInput"),
    }



def _shared_and_init(nc, st, S, block, dcon, dide):
    sb = lambda n, shape, dt: st.enter_context(nc.sbuf_tensor("sh_" + n, shape, dt))
    sh = {
        "xchunk": [sb("xchunk0", [128, 1024], F32), sb("xchunk1", [128, 1024], F32)],
        "sqscr": sb("sqscr", [128, 1024], F32),
        "stats": sb("stats", [128, 16], F32),
        "consts": sb("consts", [128, 8], F32),
        "ident32": sb("ident32", [128, 128], F32),
        "identbf": sb("identbf", [128, 128], BF16),
        "xr4": sb("xr4", [128, 1024], F32),
        "outv": sb("outv", [128, 1024], F32),
        "outp": sb("outp", [128, 1024], F32),
    }
    S.pfx = "g:"
    S.inc("dX", n=2); S.setmark(("init",), "dX")
    S.inc("vS", ("identbf",))

    @block.sync
    def _(sync):
        sync.dma_start(out=sh["consts"][:, :], in_=dcon[:, :]).then_inc(S.sem["dX"], 16)
        sync.dma_start(out=sh["ident32"][:, :], in_=dide[:, :]).then_inc(S.sem["dX"], 16)

    @block.vector
    def _(v):
        S.wait(v, ("init",))
        v.tensor_copy(sh["identbf"][:, :], sh["ident32"][:, :]).then_inc(S.sem["vS"], 1)

    return sh


def build_fused_nc():
    nc = bass.Bass("TRN2", target_bir_lowering=False)
    dxin = nc.dram_tensor("xin", [TPC, D], F32, kind="ExternalInput")
    dx1 = nc.dram_tensor("x1", [TPC, D], F32)
    dx2 = nc.dram_tensor("x2", [TPC, D], F32)
    dx3 = nc.dram_tensor("x3", [TPC, D], F32)
    dxout = nc.dram_tensor("xout", [TPC, D], F32, kind="ExternalOutput")
    dcon = nc.dram_tensor("consts", [128, 8], F32, kind="ExternalInput")
    dide = nc.dram_tensor("ident", [128, 128], F32, kind="ExternalInput")
    dr_a0 = _dram_attn0(nc); dr_a0.update(xin=dxin, xout=dx1)
    dr_f0 = _dram_ffn(nc, 0); dr_f0.update(xin=dx1, xout=dx2)
    dr_l1 = _dram_attn1(nc); dr_l1.update(xin=dx2, xout=dx3)
    dr_f1 = _dram_ffn(nc, 1); dr_f1.update(xin=dx3, xout=dxout)

    S = Sched()
    with ExitStack() as st:
        for name in SEMS:
            S.sem[name] = st.enter_context(nc.semaphore(name))
        block = st.enter_context(nc.Block())
        sh = _shared_and_init(nc, st, S, block, dcon, dide)
        snap = S.snapshot()
        S.pfx = "a0:"
        emit_attn0(nc, block, S, sh, dr_a0, snap)
        snap = S.snapshot()
        S.pfx = "f0:"
        emit_ffn(nc, block, S, sh, dr_f0, snap)
        snap = S.snapshot()
        S.pfx = "l1:"
        emit_attn1(nc, block, S, sh, dr_l1, snap)
        snap = S.snapshot()
        S.pfx = "f1:"
        emit_ffn(nc, block, S, sh, dr_f1, snap)
    return nc



def _build_single(which, L=0):
    nc = bass.Bass("TRN2", target_bir_lowering=False)
    dxin = nc.dram_tensor("xin", [TPC, D], F32, kind="ExternalInput")
    dxout = nc.dram_tensor("xout", [TPC, D], F32, kind="ExternalOutput")
    dcon = nc.dram_tensor("consts", [128, 8], F32, kind="ExternalInput")
    dide = nc.dram_tensor("ident", [128, 128], F32, kind="ExternalInput")
    if which == "a0":
        dr = _dram_attn0(nc)
    elif which == "l1":
        dr = _dram_attn1(nc)
    else:
        dr = _dram_ffn(nc, L)
    dr.update(xin=dxin, xout=dxout)
    S = Sched()
    with ExitStack() as st:
        for name in SEMS:
            S.sem[name] = st.enter_context(nc.semaphore(name))
        block = st.enter_context(nc.Block())
        sh = _shared_and_init(nc, st, S, block, dcon, dide)
        snap = S.snapshot()
        S.pfx = which + ":"
        if which == "a0":
            emit_attn0(nc, block, S, sh, dr, snap)
        elif which == "l1":
            emit_attn1(nc, block, S, sh, dr, snap)
        else:
            emit_ffn(nc, block, S, sh, dr, snap)
    return nc



def kernel(**inputs):
    w = prep_shared(inputs)
    x = np.ascontiguousarray(np.asarray(inputs["L0_attn_x"], np.float32).reshape(NCORES * TPC, D))
    shared_w = {"consts": w["consts"], "ident": w["ident"]}
    shared_w.update(attn0_inputs(w))
    shared_w.update(ffn_inputs(w, 0))
    shared_w.update(attn1_inputs(w))
    shared_w.update(ffn_inputs(w, 1))

    nc = build_fused_nc()
    in_maps = []
    for c in range(NCORES):
        m = dict(shared_w)
        m["xin"] = np.ascontiguousarray(x[c * TPC:(c + 1) * TPC])
        in_maps.append(m)
    res = run_bass_kernel_spmd(nc, in_maps, core_ids=list(range(NCORES)))
    out = np.concatenate([res.results[c]["xout"] for c in range(NCORES)], axis=0)
    return out.reshape(16, 2048, D).astype(np.float32)


# version 11 (final)
# speedup vs baseline: 1.1474x; 1.0271x over previous
import math
import numpy as np

from contextlib import ExitStack

import concourse.bass as bass
import concourse.mybir as mybir
from concourse.bass_utils import run_bass_kernel_spmd

F32 = mybir.dt.float32
BF16 = mybir.dt.bfloat16
AF = mybir.ActivationFunctionType
ALU = mybir.AluOpType
AX = mybir.AxisListType

NCORES = 8
D = 1024
SEQ = 2048
TPC = 4096
HV = 16

C_INV_D, C_EPS, C_NEG1, C_QK8 = 0, 1, 2, 3



def _pos_encoding_T2():
    L, dim = SEQ, 64
    positions = np.arange(L, dtype=np.float32)[:, None]
    div_term = np.exp(np.arange(0, dim, 2, dtype=np.float32) * -(np.log(10000.0) / dim)).astype(np.float32)
    ang = (positions * div_term).astype(np.float32)
    pe = np.zeros((L, dim), dtype=np.float32)
    pe[:, 0::2] = np.sin(ang)
    pe[:, 1::2] = np.cos(ang)
    peT = np.ascontiguousarray(pe.T)
    return np.ascontiguousarray(np.concatenate([peT, peT], axis=0))


def _blk_lhsT(W):
    O, I = W.shape
    return np.ascontiguousarray(W.reshape(O // 128, 128, I // 128, 128).transpose(0, 3, 2, 1))


def _col_chunks(v):
    return np.ascontiguousarray(np.asarray(v, np.float32).reshape(-1, 128).T)


def _rep(v, n=128):
    v = np.asarray(v, np.float32).reshape(-1)
    return np.ascontiguousarray(np.broadcast_to(v.reshape(1, -1), (n, v.size)))


def prep_shared(inputs):
    f = lambda k: np.ascontiguousarray(np.asarray(inputs[k], np.float32))
    w = {}
    perm = np.empty(3072, np.int64)
    for part in range(3):
        for h in range(16):
            base = part * 1024 + h * 64
            perm[base:base + 64] = h * 192 + part * 64 + np.arange(64)
    wqkv = f("L0_attn_w_qkv")[:, perm]
    w["wqkvb"] = np.ascontiguousarray(wqkv.reshape(8, 128, 3072))
    w["bqkv"] = _col_chunks(f("L0_attn_b_qkv")[perm])
    w["woutb"] = np.ascontiguousarray(f("L0_attn_w_out").reshape(8, 128, 1024))
    w["boutb"] = _rep(f("L0_attn_b_out"))
    w["pos2"] = _pos_encoding_T2()
    w["onesb"] = np.ones((1, 128), np.float32)
    for nm, key in (("wq", "L1_attn_wq"), ("wk", "L1_attn_wk"), ("wv", "L1_attn_wv")):
        m = np.transpose(f(key), (1, 0, 2)).reshape(1024, 1024)
        w[nm + "b"] = np.ascontiguousarray(m.reshape(8, 128, 1024))
    w["bqc"] = _col_chunks(f("L1_attn_bq").reshape(-1))
    w["bkc"] = _col_chunks(f("L1_attn_bk").reshape(-1))
    w["bvb"] = _rep(f("L1_attn_bv").reshape(-1))
    w["wob"] = np.ascontiguousarray(f("L1_attn_wo").T.reshape(8, 128, 1024))
    w["onesl"] = np.ones((128, 128), np.float32)
    w["bob"] = _rep(f("L1_attn_bo"))
    for L in (0, 1):
        p = f"L{L}_ffn_"
        w[f"w1b{L}"] = _blk_lhsT(f(p + "w1_w"))
        w[f"w2b{L}"] = _blk_lhsT(f(p + "w2_w"))
        w[f"h1b{L}"] = _blk_lhsT(f(p + "h1_w"))
        w[f"h2b{L}"] = _blk_lhsT(f(p + "h2_w"))
        w[f"w3b{L}"] = np.ascontiguousarray(f(p + "w3_w").T.reshape(32, 128, 1024))
        w[f"sc1{L}"] = _col_chunks(f(p + "w1_s"))
        w[f"sc2{L}"] = _col_chunks(f(p + "w2_s"))
        w[f"sch1{L}"] = _col_chunks(f(p + "h1_s"))
        w[f"sch2{L}"] = _col_chunks(f(p + "h2_s"))
        w[f"s3b{L}"] = _rep(f(p + "w3_s"))
    consts = np.zeros((128, 8), np.float32)
    consts[:, C_INV_D] = 1.0 / D
    consts[:, C_EPS] = 1e-5
    consts[:, C_NEG1] = -1.0
    consts[:, C_QK8] = 0.125
    w["consts"] = consts
    w["ident"] = np.eye(128, dtype=np.float32)
    return w



SEMS = ("dX", "dS", "gD", "tS", "aS", "vS", "pS")


DMA_SEMS = ("dX", "dS", "gD")


class Sched:
    def __init__(self):
        self.c = {n: 0 for n in SEMS}
        self.marks = {}
        self.sem = {}
        self.pfx = ""

    def _k(self, key):
        return (self.pfx,) + tuple(key)

    def inc(self, name, key=None, n=1):
        self.c[name] += n
        if key is not None:
            self.marks[self._k(key)] = (name, self.c[name])
        return self.c[name]

    def setmark(self, key, name):
        self.marks[self._k(key)] = (name, self.c[name])

    def alias(self, key, srckey):
        self.marks[self._k(key)] = self.marks[self._k(srckey)]

    def wait(self, eng, key):
        name, cnt = self.marks[self._k(key)]
        if name in DMA_SEMS:
            cnt *= 16
        eng.wait_ge(self.sem[name], cnt)

    def wait_max(self, eng, keys):
        best = {}
        for k in keys:
            name, cnt = self.marks[self._k(k)]
            best[name] = max(best.get(name, 0), cnt)
        for name, cnt in best.items():
            if name in DMA_SEMS:
                cnt *= 16
            eng.wait_ge(self.sem[name], cnt)

    def snapshot(self):
        return dict(self.c)

    def barrier(self, eng, snap):
        for name, cnt in snap.items():
            if cnt <= 0:
                continue
            eng.wait_ge(self.sem[name], cnt * 16 if name in DMA_SEMS else cnt)

    def final(self, name):
        return self.c[name]



def ln_prepass(S, pfx, G):
    S.inc("aS", (pfx, "sq", G))
    S.inc("vS", (pfx, "var", G))
    S.inc("aS", (pfx, "sqrt", G))
    S.inc("vS", (pfx, "rstdb", G))
    S.inc("vS", (pfx, "apply", G))
    S.inc("tS", (pfx, "tr", G))
    S.inc("vS", (pfx, "xevac", G))


def ln_emit_vector(v, S, pfx, G, xchunk, stats, consts, pTRb, xlnT_dst3d, xln_tok,
                   extra_apply_waits=()):
    b = 8 * (G % 2)
    S.wait(v, (pfx, "xload", G))
    v.reduce_sum(stats[:, b + 0:b + 1], xchunk[:, :], axis=AX.X)
    S.wait(v, (pfx, "sq", G))
    v.drain()
    v.tensor_scalar_mul(stats[:, b + 2:b + 3], stats[:, b + 0:b + 1], consts[:, C_INV_D:C_INV_D + 1])
    v.drain()
    v.tensor_mul(stats[:, b + 3:b + 4], stats[:, b + 2:b + 3], stats[:, b + 2:b + 3])
    v.drain()
    v.scalar_tensor_tensor(stats[:, b + 4:b + 5], stats[:, b + 1:b + 2],
                           consts[:, C_INV_D:C_INV_D + 1], stats[:, b + 3:b + 4],
                           ALU.mult, ALU.subtract).then_inc(S.sem["vS"], 1)
    S.wait(v, (pfx, "sqrt", G))
    v.reciprocal(stats[:, b + 6:b + 7], stats[:, b + 5:b + 6])
    v.drain()
    v.tensor_scalar(stats[:, b + 7:b + 8], stats[:, b + 2:b + 3], stats[:, b + 6:b + 7],
                    consts[:, C_NEG1:C_NEG1 + 1], ALU.mult, ALU.mult).then_inc(S.sem["vS"], 1)
    v.drain()
    if extra_apply_waits:
        S.wait_max(v, extra_apply_waits)
    v.tensor_scalar(xln_tok[:, :], xchunk[:, :], stats[:, b + 6:b + 7],
                    stats[:, b + 7:b + 8], ALU.mult, ALU.add).then_inc(S.sem["vS"], 1)
    S.wait(v, (pfx, "tr", G))
    v.tensor_copy(xlnT_dst3d, pTRb).then_inc(S.sem["vS"], 1)


def ln_emit_scalar(s, S, pfx, G, xchunk, stats, consts, sqscr):
    b = 8 * (G % 2)
    S.wait(s, (pfx, "xload", G))
    s.activation(sqscr[:, :], xchunk[:, :], AF.Square,
                 accum_out=stats[:, b + 1:b + 2]).then_inc(S.sem["aS"], 1)
    S.wait(s, (pfx, "var", G))
    s.activation(stats[:, b + 5:b + 6], stats[:, b + 4:b + 5], AF.Sqrt,
                 bias=consts[:, C_EPS:C_EPS + 1]).then_inc(S.sem["aS"], 1)



def ln_emit_tensor(t, S, pfx, G, xln_tok, identbf, pTRb_slots, wait_keys):
    S.wait_max(t, wait_keys + [(pfx, "apply", G)])
    ins = None
    for r in range(8):
        ins = t.transpose(pTRb_slots[r], xln_tok[:, r * 128:(r + 1) * 128], identbf[:, :])
    ins.then_inc(S.sem["tS"], 1)



def emit_ffn(nc, block, S, sh, dr, snap):
    FT = 1024
    NT = TPC // FT
    dxin, dxout = dr["xin"], dr["xout"]
    dw1, dw2, dh1, dh2, dw3 = dr["w1c"], dr["w2c"], dr["h1c"], dr["h2c"], dr["w3c"]
    dsc1, dsc2, dsh1, dsh2, ds3 = dr["sc1"], dr["sc2"], dr["sch1"], dr["sch2"], dr["s3b"]
    consts, ident32, identbf = sh["consts"], sh["ident32"], sh["identbf"]
    xchunk, sqscr, stats = sh["xchunk"], sh["sqscr"], sh["stats"]
    xr4, outv, outp = sh["xr4"], sh["outv"], sh["outp"]
    S.inc("dX", n=5); S.setmark(("init",), "dX")
    for i in range(NT):
        for tc in range(8):
            G = i * 8 + tc
            S.inc("dX", ("f", "xload", G))
            ln_prepass(S, "f", G)
        for c in range(32):
            bi = i * 32 + c
            S.inc("dX", ("wB1", bi))
            S.inc("gD", ("wB2", bi))
            S.inc("tS", ("mmB", bi))
            S.inc("aS", ("sil", bi))
            S.inc("vS", ("g1", bi))
        for st_ in range(2):
            for c in range(32):
                gi = i * 64 + st_ * 32 + c
                S.inc("dX", ("wCDh", 2 * gi))
                S.inc("gD", ("wCDh", 2 * gi + 1))
                S.inc("tS", ("mmCDh", 2 * gi))
                S.inc("tS", ("mmCD", gi))
                S.alias(("mmCDh", 2 * gi + 1), ("mmCD", gi))
                S.inc("aS", ("gelu", gi))
        for half in range(2):
            for r in range(32):
                ei = (i * 2 + half) * 32 + r
                S.inc("dX" if r % 2 == 0 else "gD", ("wE", ei))
                S.inc("tS", ("mmE", ei))
            for t4 in range(4):
                idx = i * 8 + half * 4 + t4
                S.inc("dX", ("xr", idx))
                S.inc("vS", ("ev1", idx))
                S.inc("pS", ("ev2", idx))
                S.inc("dS", ("store", idx))

    with ExitStack() as st:
        sb = lambda n, shape, dt: st.enter_context(nc.sbuf_tensor("sb_" + S.pfx + n, shape, dt))
        xln_tok = [sb("xlntok0", [128, 1024], BF16), sb("xlntok1", [128, 1024], BF16)]
        xlnT = sb("xlnT", [128, 8 * 1024], BF16)
        g_a = sb("g_a", [128, 32 * 1024], BF16)
        g_b = sb("g_b", [128, 32 * 1024], BF16)
        wb1 = [sb("wb1_0", [128, 1024], BF16), sb("wb1_1", [128, 1024], BF16)]
        wb2 = [sb("wb2_0", [128, 1024], BF16), sb("wb2_1", [128, 1024], BF16)]
        wbig = [sb("wbig%d" % k, [128, 2048], BF16) for k in range(3)]
        wE4 = [sb("wE%d" % k, [128, 1024], BF16) for k in range(4)]
        tsil = [sb("tsil0", [128, 1024], BF16), sb("tsil1", [128, 1024], BF16)]
        sc1 = sb("sc1", [128, 32], F32)
        sc2 = sb("sc2", [128, 32], F32)
        sch1 = sb("sch1", [128, 32], F32)
        sch2 = sb("sch2", [128, 32], F32)
        s3b = sb("s3b", [128, 1024], F32)
        pA = st.enter_context(nc.psum_tensor(S.pfx + "pA", [128, 1024], F32))
        pB = st.enter_context(nc.psum_tensor(S.pfx + "pB", [128, 1024], F32))
        pC = st.enter_context(nc.psum_tensor(S.pfx + "pC", [128, 1024], F32))
        pE = st.enter_context(nc.psum_tensor(S.pfx + "pE", [128, 1024], F32))
        psums = [pA, pB, pC, pE]

        pTRb = pE[:, :].bitcast(BF16)
        pTR_slots = [pTRb[:, r * 128:(r + 1) * 128] for r in range(8)]

        def xlnT_dst3d(tc):
            return xlnT[:, :].rearrange("p (r t) -> p r t", t=1024)[:, :, tc * 128:(tc + 1) * 128]

        def pTR_src3d():
            return pTRb.rearrange("p (r t) -> p r t", t=128)[:, 0:8, :]

        @block.sync
        def _(sync):
            S.barrier(sync, snap)
            for dst, srcdram in ((sc1, dsc1), (sc2, dsc2),
                                 (sch1, dsh1), (sch2, dsh2), (s3b, ds3)):
                sync.dma_start(out=dst[:, :], in_=srcdram[:, :]).then_inc(S.sem["dX"], 16)
            for i in range(NT):
                for tc in range(8):
                    G = i * 8 + tc
                    if G >= 2:
                        S.wait(sync, ("f", "apply", G - 2))
                    sync.dma_start(out=xchunk[G % 2][:, :],
                                   in_=dxin[G * 128:(G + 1) * 128, :]).then_inc(S.sem["dX"], 16)
                for c in range(32):
                    bi = i * 32 + c
                    if bi >= 2:
                        S.wait(sync, ("mmB", bi - 2))
                    sync.dma_start(out=wb1[c % 2][:, :],
                                   in_=dw1[c][:, :]).then_inc(S.sem["dX"], 16)
                for st_ in range(2):
                    dh = dh1 if st_ == 0 else dh2
                    for c in range(32):
                        gi = i * 64 + st_ * 32 + c
                        hi = 2 * gi
                        if hi >= 3:
                            S.wait(sync, ("mmCDh", hi - 3))
                        sync.dma_start(out=wbig[hi % 3][:, :],
                                       in_=dh[c][:, 0:2048]).then_inc(S.sem["dX"], 16)
                for half in range(2):
                    for r in range(0, 32, 2):
                        ei = (i * 2 + half) * 32 + r
                        if ei >= 4:
                            S.wait(sync, ("mmE", ei - 4))
                        sync.dma_start(out=wE4[ei % 4][:, :],
                                       in_=dw3[r][:, :]).then_inc(S.sem["dX"], 16)
                    for t4 in range(4):
                        idx = i * 8 + half * 4 + t4
                        if idx >= 1:
                            S.wait(sync, ("ev2", idx - 1))
                        row0 = i * 1024 + (half * 4 + t4) * 128
                        sync.dma_start(out=xr4[:, :],
                                       in_=dxin[row0:row0 + 128, :]).then_inc(S.sem["dX"], 16)
                        S.wait(sync, ("ev2", idx))
                        sync.dma_start(out=dxout[row0:row0 + 128, :],
                                       in_=outp[:, :]).then_inc(S.sem["dS"], 16)
            sync.wait_ge(S.sem["dS"], 16 * S.final("dS"))

        @block.gpsimd
        def _(g):
            S.barrier(g, snap)
            for i in range(NT):
                for c in range(32):
                    bi = i * 32 + c
                    if c >= 2:
                        S.wait(g, ("mmB", bi - 2))
                    g.dma_start(out=wb2[c % 2][:, :],
                                in_=dw2[c][:, :]).then_inc(S.sem["gD"], 16)
                for st_ in range(2):
                    dh = dh1 if st_ == 0 else dh2
                    for c in range(32):
                        gi = i * 64 + st_ * 32 + c
                        hi = 2 * gi + 1
                        if hi >= 3:
                            S.wait(g, ("mmCDh", hi - 3))
                        g.dma_start(out=wbig[hi % 3][:, :],
                                    in_=dh[c][:, 2048:4096]).then_inc(S.sem["gD"], 16)
                for half in range(2):
                    for r in range(1, 32, 2):
                        ei = (i * 2 + half) * 32 + r
                        if ei >= 4:
                            S.wait(g, ("mmE", ei - 4))
                        g.dma_start(out=wE4[ei % 4][:, :],
                                    in_=dw3[r][:, :]).then_inc(S.sem["gD"], 16)
                    for t4 in range(4):
                        idx = i * 8 + half * 4 + t4
                        S.wait_max(g, [("ev1", idx), ("xr", idx)] +
                                   ([("store", idx - 1)] if idx >= 1 else []))
                        g.tensor_add(outp[:, :], outv[:, :], xr4[:, :])
                        g.drain()
                        g.engine_nop().then_inc(S.sem["pS"], 1)

        @block.tensor
        def _(t):
            S.barrier(t, snap)
            for i in range(NT):
                for tc in range(8):
                    G = i * 8 + tc
                    waits = []
                    if G >= 1:
                        waits.append(("f", "xevac", G - 1))
                    if i >= 1:
                        waits.append(("ev1", i * 8 - 1))
                    ln_emit_tensor(t, S, "f", G, xln_tok[G % 2], identbf, pTR_slots, waits)
                for c in range(32):
                    bi = i * 32 + c
                    pX, pY = (pA, pB) if c % 2 == 0 else (pC, pE)
                    waits = [("wB1", bi), ("wB2", bi)]
                    if c == 0:
                        waits.append(("f", "xevac", i * 8 + 7))
                    if bi >= 2:
                        waits += [("sil", bi - 2), ("g1", bi - 2)]
                    if c < 2 and i >= 1:
                        waits.append(("ev1", i * 8 - 1))
                    S.wait_max(t, waits)
                    ins = None
                    for r in range(8):
                        for n in range(2):
                            ins = t.matmul(pX[:, n * 512:(n + 1) * 512],
                                           wb1[c % 2][:, r * 128:(r + 1) * 128],
                                           xlnT[:, r * 1024 + n * 512: r * 1024 + (n + 1) * 512],
                                           start=(r == 0), stop=(r == 7))
                    for r in range(8):
                        for n in range(2):
                            ins = t.matmul(pY[:, n * 512:(n + 1) * 512],
                                           wb2[c % 2][:, r * 128:(r + 1) * 128],
                                           xlnT[:, r * 1024 + n * 512: r * 1024 + (n + 1) * 512],
                                           start=(r == 0), stop=(r == 7))
                    ins.then_inc(S.sem["tS"], 1)
                for st_ in range(2):
                    gsrc = g_a if st_ == 0 else g_b
                    for c in range(32):
                        gi = i * 64 + st_ * 32 + c
                        pX = pA if c % 2 == 0 else pC
                        waits = [("wCDh", 2 * gi)]
                        if st_ == 0 and c == 0:
                            waits.append(("g1", i * 32 + 31))
                        if st_ == 1 and c == 0:
                            waits.append(("gelu", i * 64 + 31))
                        if gi >= 2:
                            waits.append(("gelu", gi - 2))
                        if c < 2 and st_ == 0:
                            waits += [("sil", i * 32 + 30 + c)]
                        S.wait_max(t, waits)
                        ins = None
                        for r in range(32):
                            if r == 16:
                                ins.then_inc(S.sem["tS"], 1)
                                S.wait(t, ("wCDh", 2 * gi + 1))
                            for n in range(2):
                                ins = t.matmul(pX[:, n * 512:(n + 1) * 512],
                                               wbig[(2 * gi + r // 16) % 3][:, (r % 16) * 128:(r % 16 + 1) * 128],
                                               gsrc[:, r * 1024 + n * 512: r * 1024 + (n + 1) * 512],
                                               start=(r == 0), stop=(r == 31))
                        ins.then_inc(S.sem["tS"], 1)
                for half in range(2):
                    if half == 1:
                        S.wait_max(t, [("ev1", i * 8 + 3)])
                    for r in range(32):
                        ei = (i * 2 + half) * 32 + r
                        waits = [("wE", ei), ("gelu", i * 64 + 32 + r)]
                        if half == 0 and r == 0:
                            waits += [("gelu", i * 64 + 63), ("g1", i * 32 + 31)]
                        S.wait_max(t, waits)
                        ins = None
                        for t4 in range(4):
                            tloc = half * 4 + t4
                            for n in range(2):
                                ins = t.matmul(psums[t4][:, n * 512:(n + 1) * 512],
                                               g_a[:, r * 1024 + tloc * 128: r * 1024 + (tloc + 1) * 128],
                                               wE4[ei % 4][:, n * 512:(n + 1) * 512],
                                               start=(r == 0), stop=(r == 31))
                        ins.then_inc(S.sem["tS"], 1)

        @block.vector
        def _(v):
            S.barrier(v, snap)
            S.wait(v, ("init",))
            for i in range(NT):
                for tc in range(8):
                    G = i * 8 + tc
                    extra = [("f", "tr", G - 2)] if G >= 2 else []
                    ln_emit_vector(v, S, "f", G, xchunk[G % 2], stats, consts,
                                   pTR_src3d(), xlnT_dst3d(tc), xln_tok[G % 2], extra)
                for c in range(32):
                    bi = i * 32 + c
                    pY = pB if c % 2 == 0 else pE
                    S.wait(v, ("sil", bi))
                    v.scalar_tensor_tensor(g_a[:, c * 1024:(c + 1) * 1024], pY[:, :],
                                           sc2[:, c:c + 1], tsil[c % 2][:, :],
                                           ALU.mult, ALU.mult).then_inc(S.sem["vS"], 1)
                for half in range(2):
                    for t4 in range(4):
                        idx = i * 8 + half * 4 + t4
                        waits = [("mmE", (i * 2 + half) * 32 + 31)]
                        if idx >= 1:
                            waits.append(("ev2", idx - 1))
                        S.wait_max(v, waits)
                        v.tensor_mul(outv[:, :], psums[t4][:, :], s3b[:, :])
                        v.drain()
                        v.engine_nop().then_inc(S.sem["vS"], 1)

        @block.scalar
        def _(s):
            S.barrier(s, snap)
            S.wait(s, ("init",))
            for i in range(NT):
                for tc in range(8):
                    G = i * 8 + tc
                    ln_emit_scalar(s, S, "f", G, xchunk[G % 2], stats, consts, sqscr)
                for c in range(32):
                    bi = i * 32 + c
                    pX = pA if c % 2 == 0 else pC
                    waits = [("mmB", bi)]
                    if bi >= 2:
                        waits.append(("g1", bi - 2))
                    S.wait_max(s, waits)
                    s.activation(tsil[c % 2][:, :], pX[:, :], AF.Silu,
                                 scale=sc1[:, c:c + 1]).then_inc(S.sem["aS"], 1)
                for st_ in range(2):
                    gdst = g_b if st_ == 0 else g_a
                    sch = sch1 if st_ == 0 else sch2
                    for c in range(32):
                        gi = i * 64 + st_ * 32 + c
                        pX = pA if c % 2 == 0 else pC
                        S.wait(s, ("mmCD", gi))
                        s.activation(gdst[:, c * 1024:(c + 1) * 1024], pX[:, :], AF.Gelu,
                                     scale=sch[:, c:c + 1]).then_inc(S.sem["aS"], 1)



def emit_attn0(nc, block, S, sh, dr, snap):
    AT = 512
    NTI = TPC // AT
    dxin, dxout = dr["xin"], dr["xout"]
    dwqkv, dbqkv, dwout, dbout, dpos = dr["wqkvb"], dr["bqkv"], dr["woutb"], dr["boutb"], dr["pos2"]
    dones = dr["onesb"]
    consts, ident32, identbf = sh["consts"], sh["ident32"], sh["identbf"]
    xchunk, sqscr, stats = sh["xchunk"], sh["sqscr"], sh["stats"]
    xr4, outv, outp = sh["xr4"], sh["outv"], sh["outp"]
    S.inc("dX", n=4); S.setmark(("init",), "dX")
    S.inc("gD", n=16); S.setmark(("winit",), "gD")
    S.inc("vS", ("onesbf",))
    pc_chunks = []
    for srcd, dstd, grp in dr.get("precast", []):
        for c0 in range(0, 32, grp):
            pc_chunks.append((srcd, dstd, c0, grp))
    npc = len(pc_chunks)
    for k in range(npc):
        if k >= 2:
            S.inc("gD", ("pco", k - 2))
        S.inc("gD", ("pci", k))
    for k in (npc - 2, npc - 1):
        if k >= 0:
            S.inc("gD", ("pco", k))
    for ti in range(NTI):
        for tc in range(4):
            G = ti * 4 + tc
            S.inc("dX", ("a", "xload", G))
            ln_prepass(S, "a", G)
        for oc in range(24):
            S.inc("tS", ("mmQ", ti, oc))
            S.inc("vS", ("qev", ti, oc))
        for si in range(3):
            for tc in range(4):
                S.inc("tS", ("trQ", ti, si, tc))
                S.inc("aS", ("tevQ", ti, si, tc))
        for tc in range(4):
            S.inc("vS", ("nm8", ti, tc))
            S.inc("aS", ("exp", ti, tc))
            S.inc("vS", ("otok", ti, tc))
        for tc in range(4):
            S.inc("tS", ("trO", ti, tc))
            S.inc("aS", ("tevO", ti, tc))
        for tc in range(4):
            idx = ti * 4 + tc
            S.inc("tS", ("mmW", ti, tc))
            S.inc("dX", ("wxr", idx))
            S.inc("vS", ("oadd", idx))
            S.inc("dS", ("store", idx))

    with ExitStack() as st:
        sb = lambda n, shape, dt: st.enter_context(nc.sbuf_tensor("sb_" + S.pfx + n, shape, dt))
        xln_tok = [sb("xlntok0", [128, 1024], BF16), sb("xlntok1", [128, 1024], BF16)]
        xlnT = sb("xlnT", [128, 4096], BF16)
        wqkv_sb = sb("wqkv", [128, 8 * 3072], BF16)
        wout_sb = sb("wout", [128, 8 * 1024], BF16)
        bqkv = sb("bqkv", [128, 24], F32)
        boutb = sb("boutb", [128, 1024], F32)
        pos2 = sb("pos2", [128, 2048], F32)
        qT = sb("qT", [128, 4096], BF16)
        kT = sb("kT", [128, 4096], BF16)
        vT = sb("vT", [128, 4096], BF16)
        q_tok = sb("q_tok", [128, 4096], BF16)
        k_tok = sb("k_tok", [128, 4096], BF16)
        v_tok = sb("v_tok", [128, 4096], BF16)
        prod = sb("prod", [128, 4096], BF16)
        prod2 = prod
        Sm = sb("Sm", [128, 256], F32)
        Ee = sb("Ee", [128, 256], BF16)
        esum = sb("esum", [128, 16], F32)
        rec = sb("rec", [128, 16], F32)
        nm8 = sb("nm8", [128, 16], F32)
        Oraw = sb("Oraw", [128, 1024], F32)
        O_bf = sb("O_bf", [128, 4096], BF16)
        OT = sb("OT", [128, 4096], BF16)
        ones1 = sb("ones1", [1, 128], F32)
        onesbf = sb("onesbf", [1, 128], BF16)
        bout_bf = sb("bout_bf", [1, 1024], BF16)
        pcb = [sb("pcb0", [128, 4096], BF16), sb("pcb1", [128, 4096], BF16)]
        pQ2 = st.enter_context(nc.psum_tensor(S.pfx + "pQ2", [128, 1024], F32))
        pTR = st.enter_context(nc.psum_tensor(S.pfx + "pTR", [128, 1024], F32))
        pW = [st.enter_context(nc.psum_tensor(S.pfx + "pW0", [128, 1024], F32)),
              st.enter_context(nc.psum_tensor(S.pfx + "pW1", [128, 1024], F32))]

        pTRb = pTR[:, :].bitcast(BF16)
        pTR_slots = [pTRb[:, r * 128:(r + 1) * 128] for r in range(8)]

        def xlnT_dst3d(tc):
            return xlnT[:, :].rearrange("p (r t) -> p r t", t=512)[:, :, tc * 128:(tc + 1) * 128]

        def pTR_src3d():
            return pTRb.rearrange("p (r t) -> p r t", t=128)[:, 0:8, :]

        toks = None

        @block.sync
        def _(sync):
            S.barrier(sync, snap)
            for dst, srcdram in ((bqkv, dbqkv), (boutb, dbout), (pos2, dpos),
                                 (ones1, dones)):
                sync.dma_start(out=dst[:, :], in_=srcdram[:, :]).then_inc(S.sem["dX"], 16)
            for ti in range(NTI):
                for tc in range(4):
                    G = ti * 4 + tc
                    if G >= 2:
                        S.wait(sync, ("a", "apply", G - 2))
                    sync.dma_start(out=xchunk[G % 2][:, :],
                                   in_=dxin[G * 128:(G + 1) * 128, :]).then_inc(S.sem["dX"], 16)
                for tc in range(4):
                    idx = ti * 4 + tc
                    if idx >= 1:
                        S.wait(sync, ("oadd", idx - 1))
                    row0 = idx * 128
                    sync.dma_start(out=xr4[:, :],
                                   in_=dxin[row0:row0 + 128, :]).then_inc(S.sem["dX"], 16)
                    S.wait(sync, ("oadd", idx))
                    sync.dma_start(out=dxout[row0:row0 + 128, :],
                                   in_=outp[:, :]).then_inc(S.sem["dS"], 16)
            sync.wait_ge(S.sem["dS"], 16 * S.final("dS"))

        @block.gpsimd
        def _(g):
            S.barrier(g, snap)
            for r in range(8):
                g.dma_start(out=wqkv_sb[:, r * 3072:(r + 1) * 3072],
                            in_=dwqkv[r][:, :]).then_inc(S.sem["gD"], 16)
            for r in range(8):
                g.dma_start(out=wout_sb[:, r * 1024:(r + 1) * 1024],
                            in_=dwout[r][:, :]).then_inc(S.sem["gD"], 16)

            def pc_view(t, c0, grp):
                sl = t[c0:c0 + grp]
                if len(sl.shape) == 4:
                    return sl.rearrange("c p r j -> p c (r j)")
                return sl.rearrange("c p n -> p c n")

            def pc_in(k):
                srcd, dstd, c0, grp = pc_chunks[k]
                g.dma_start(out=pcb[k % 2][:, :].rearrange("p (c n) -> p c n", c=grp),
                            in_=pc_view(srcd, c0, grp)).then_inc(S.sem["gD"], 16)

            def pc_out(k):
                srcd, dstd, c0, grp = pc_chunks[k]
                S.wait(g, ("pci", k))
                g.dma_start(out=pc_view(dstd, c0, grp),
                            in_=pcb[k % 2][:, :].rearrange("p (c n) -> p c n", c=grp)
                            ).then_inc(S.sem["gD"], 16)

            for k in range(npc):
                if k < 2:
                    pc_in(k)
                else:
                    pc_out(k - 2)
                    S.wait(g, ("pco", k - 2))
                    pc_in(k)
            for k in (npc - 2, npc - 1):
                if k >= 0:
                    pc_out(k)
        @block.tensor
        def _(t):
            S.barrier(t, snap)
            ptr_prev = []
            for ti in range(NTI):
                for tc in range(4):
                    G = ti * 4 + tc
                    waits = list(ptr_prev)
                    ln_emit_tensor(t, S, "a", G, xln_tok[G % 2], identbf, pTR_slots, waits)
                    ptr_prev = [("a", "xevac", G)]
                for oc in range(24):
                    waits = []
                    if oc == 0:
                        waits += [("a", "xevac", ti * 4 + 3), ("winit",)]
                    if oc >= 2:
                        waits.append(("qev", ti, oc - 2))
                    if waits:
                        S.wait_max(t, waits)
                    ins = None
                    for r in range(8):
                        ins = t.matmul(pQ2[:, (oc % 2) * 512:(oc % 2) * 512 + 512],
                                       wqkv_sb[:, r * 3072 + oc * 128: r * 3072 + (oc + 1) * 128],
                                       xlnT[:, r * 512:(r + 1) * 512],
                                       start=(r == 0), stop=(r == 7))
                    ins.then_inc(S.sem["tS"], 1)
                for si, srcT in enumerate((qT, kT, vT)):
                    for tc in range(4):
                        waits = list(ptr_prev)
                        if tc == 0:
                            waits.append(("qev", ti, si * 8 + 7))
                        S.wait_max(t, waits)
                        ins = None
                        for j in range(8):
                            ins = t.transpose(pTR_slots[j],
                                              srcT[:, j * 512 + tc * 128: j * 512 + (tc + 1) * 128],
                                              identbf[:, :])
                        ins.then_inc(S.sem["tS"], 1)
                        ptr_prev = [("tevQ", ti, si, tc)]
                for tc in range(4):
                    waits = list(ptr_prev) + [("otok", ti, tc)]
                    S.wait_max(t, waits)
                    ins = None
                    for j in range(8):
                        ins = t.transpose(pTR_slots[j],
                                          O_bf[:, tc * 1024 + j * 128: tc * 1024 + (j + 1) * 128],
                                          identbf[:, :])
                    ins.then_inc(S.sem["tS"], 1)
                    ptr_prev = [("tevO", ti, tc)]
                for tc in range(4):
                    idx = ti * 4 + tc
                    waits = [("tevO", ti, tc)]
                    if idx == 0:
                        waits.append(("onesbf",))
                    if idx >= 2:
                        waits.append(("oadd", idx - 2))
                    S.wait_max(t, waits)
                    ins = None
                    for n in range(2):
                        ins = t.matmul(pW[tc % 2][:, n * 512:(n + 1) * 512],
                                       onesbf[0:1, :],
                                       bout_bf[0:1, n * 512:(n + 1) * 512],
                                       start=True, stop=False)
                    for r in range(8):
                        for n in range(2):
                            ins = t.matmul(pW[tc % 2][:, n * 512:(n + 1) * 512],
                                           OT[:, r * 512 + tc * 128: r * 512 + (tc + 1) * 128],
                                           wout_sb[:, r * 1024 + n * 512: r * 1024 + (n + 1) * 512],
                                           start=False, stop=(r == 7))
                    ins.then_inc(S.sem["tS"], 1)

        @block.vector
        def _(v):
            S.barrier(v, snap)
            S.wait(v, ("init",))
            v.tensor_copy(onesbf[0:1, :], ones1[0:1, :])
            v.drain()
            v.tensor_copy(bout_bf[0:1, :], boutb[0:1, :]).then_inc(S.sem["vS"], 1)
            for ti in range(NTI):
                for tc in range(4):
                    G = ti * 4 + tc
                    extra = [("a", "tr", G - 2)] if G >= 2 else []
                    ln_emit_vector(v, S, "a", G, xchunk[G % 2], stats, consts,
                                   pTR_src3d(), xlnT_dst3d(tc), xln_tok[G % 2], extra)
                for oc in range(24):
                    S.wait(v, ("mmQ", ti, oc))
                    ps = pQ2[:, (oc % 2) * 512:(oc % 2) * 512 + 512]
                    if oc < 8:
                        v.tensor_scalar_add(qT[:, oc * 512:(oc + 1) * 512], ps,
                                            bqkv[:, oc:oc + 1]).then_inc(S.sem["vS"], 1)
                    elif oc < 16:
                        j = oc - 8
                        v.scalar_tensor_tensor(kT[:, j * 512:(j + 1) * 512], ps,
                                               bqkv[:, oc:oc + 1],
                                               pos2[:, (ti % 4) * 512:(ti % 4 + 1) * 512],
                                               ALU.add, ALU.add).then_inc(S.sem["vS"], 1)
                    else:
                        j = oc - 16
                        v.tensor_scalar_add(vT[:, j * 512:(j + 1) * 512], ps,
                                            bqkv[:, oc:oc + 1]).then_inc(S.sem["vS"], 1)
                for tc in range(4):
                    if tc == 0:
                        S.wait_max(v, [("tevQ", ti, 0, 3), ("tevQ", ti, 1, 3)])
                    k3b = k_tok[:, tc * 1024:(tc + 1) * 1024].rearrange(
                        "p (g d) -> p g d", d=64).unsqueeze(1).broadcast_to([128, 4, 16, 64])
                    p4 = prod[:, :].rearrange("p (h g d) -> p h g d", h=4, g=16)
                    for hg in range(4):
                        q4 = q_tok[:, tc * 1024 + hg * 256: tc * 1024 + (hg + 1) * 256].rearrange(
                            "p (h d) -> p h d", d=64).unsqueeze(2).broadcast_to([128, 4, 16, 64])
                        v.tensor_tensor(p4, q4, k3b, ALU.mult)
                        v.drain()
                        v.tensor_reduce(Sm[:, hg * 64:(hg + 1) * 64].rearrange("p (h g) -> p h g", g=16),
                                        p4, axis=AX.X, op=ALU.add)
                        v.drain()
                    v.tensor_reduce(nm8[:, :], Sm[:, :].rearrange("p (h g) -> p h g", g=16),
                                    axis=AX.X, op=ALU.max, negate=True)
                    v.drain()
                    v.tensor_scalar_mul(nm8[:, :], nm8[:, :], consts[:, C_QK8:C_QK8 + 1])
                    v.drain()
                    v.engine_nop().then_inc(S.sem["vS"], 1)
                    S.wait(v, ("exp", ti, tc))
                    v.reciprocal(rec[:, :], esum[:, :])
                    v.drain()
                    v3b = v_tok[:, tc * 1024:(tc + 1) * 1024].rearrange(
                        "p (d g) -> p d g", g=16).unsqueeze(1).broadcast_to([128, 4, 64, 16])
                    p42 = prod2[:, :].rearrange("p (h d g) -> p h d g", h=4, g=16)
                    if tc == 0:
                        S.wait(v, ("tevQ", ti, 2, 3))
                    for hg in range(4):
                        e4 = Ee[:, hg * 64:(hg + 1) * 64].rearrange(
                            "p (h g) -> p h g", g=16).unsqueeze(2).broadcast_to([128, 4, 64, 16])
                        v.tensor_tensor(p42, e4, v3b, ALU.mult)
                        v.drain()
                        v.tensor_reduce(Oraw[:, hg * 256:(hg + 1) * 256].rearrange("p (h d) -> p h d", d=64),
                                        p42, axis=AX.X, op=ALU.add)
                        v.drain()
                    if ti >= 1:
                        S.wait(v, ("trO", ti - 1, tc))
                    v.tensor_tensor(O_bf[:, tc * 1024:(tc + 1) * 1024].rearrange("p (h d) -> p h d", d=64),
                                    Oraw[:, :].rearrange("p (h d) -> p h d", d=64),
                                    rec[:, :].unsqueeze(2).broadcast_to([128, 16, 64]),
                                    ALU.mult).then_inc(S.sem["vS"], 1)
                for tc in range(4):
                    idx = ti * 4 + tc
                    waits = [("mmW", ti, tc), ("wxr", idx)]
                    if idx >= 1:
                        waits.append(("store", idx - 1))
                    S.wait_max(v, waits)
                    v.tensor_add(outp[:, :], pW[tc % 2][:, :], xr4[:, :])
                    v.drain()
                    v.engine_nop().then_inc(S.sem["vS"], 1)

        @block.scalar
        def _(s):
            S.barrier(s, snap)
            for ti in range(NTI):
                for tc in range(4):
                    G = ti * 4 + tc
                    ln_emit_scalar(s, S, "a", G, xchunk[G % 2], stats, consts, sqscr)
                for si, dtok in enumerate((q_tok, k_tok, v_tok)):
                    for tc in range(4):
                        S.wait(s, ("trQ", ti, si, tc))
                        if si < 2:
                            dst3 = dtok[:, tc * 1024:(tc + 1) * 1024].rearrange("p (j k) -> p j k", k=128)
                            s.copy(dst3, pTR_src3d()).then_inc(S.sem["aS"], 1)
                        else:
                            dst3 = dtok[:, tc * 1024:(tc + 1) * 1024].rearrange(
                                "p (d g) -> p g d", g=16)
                            src3 = pTRb.rearrange("p (j d) -> p j d", d=64)[:, 0:16, :]
                            s.copy(dst3, src3).then_inc(S.sem["aS"], 1)
                for tc in range(4):
                    S.wait(s, ("nm8", ti, tc))
                    ins = None
                    for h in range(16):
                        ins = s.activation(Ee[:, h * 16:(h + 1) * 16], Sm[:, h * 16:(h + 1) * 16],
                                           AF.Exp, scale=consts[:, C_QK8:C_QK8 + 1],
                                           bias=nm8[:, h:h + 1], accum_out=esum[:, h:h + 1])
                    ins.then_inc(S.sem["aS"], 1)
                for tc in range(4):
                    S.wait(s, ("trO", ti, tc))
                    dst3 = OT[:, :].rearrange("p (j t) -> p j t", t=512)[:, :, tc * 128:(tc + 1) * 128]
                    s.copy(dst3, pTR_src3d()).then_inc(S.sem["aS"], 1)


def attn0_inputs(w, xcore):
    return {"xin": xcore, "wqkvb": w["wqkvb"], "bqkv": w["bqkv"], "woutb": w["woutb"],
            "boutb": w["boutb"], "pos2": w["pos2"], "consts": w["consts"], "ident": w["ident"]}



def emit_attn1(nc, block, S, sh, dr, snap):
    NSEQ = TPC // SEQ
    NCH = SEQ // 128
    dxin, dxout = dr["xin"], dr["xout"]
    dwq, dwk, dwv = dr["wqb"], dr["wkb"], dr["wvb"]
    dbq, dbk, dbv, dwo, dbo = dr["bqc"], dr["bkc"], dr["bvb"], dr["wob"], dr["bob"]
    dones = dr["onesl"]
    consts, ident32, identbf = sh["consts"], sh["ident32"], sh["identbf"]
    xchunk, sqscr, stats = sh["xchunk"], sh["sqscr"], sh["stats"]
    xr4, outv, outp = sh["xr4"], sh["outv"], sh["outp"]
    S.inc("dX", n=5); S.setmark(("init",), "dX")
    S.inc("vS", ("ones1",))
    for q in range(NSEQ):
        for cc in range(NCH):
            G = q * NCH + cc
            S.inc("dX", ("l", "xload", G))
            ln_prepass(S, "l", G)
        for stg in range(4):
            S.inc("gD", n=8); S.setmark(("wld", q, stg), "gD")
        for stg in range(2):
            for oc in range(8):
                S.inc("tS", ("mmP", q, stg, oc))
                S.inc("vS", ("pev", q, stg, oc))
        for tc in range(16):
            S.inc("tS", ("mmV", q, tc))
            S.inc("vS", ("vev", q, tc))
        for h in range(16):
            S.inc("vS", ("csq", q, h, 0))
            S.inc("tS", ("cmm", q, h, 0))
            S.inc("vS", ("crd", q, h, 0))
            S.inc("vS", ("csq", q, h, 1))
            S.inc("tS", ("cmm", q, h, 1))
            S.inc("vS", ("crd", q, h, 1))
            S.inc("vS", ("cc2", q, h))
            S.inc("aS", ("cexp", q, h))
            S.inc("vS", ("cneg", q, h))
            S.inc("tS", ("cbc", q, h))
            S.inc("vS", ("cbe", q, h))
        first, last = q * 256, q * 256 + 255
        S.inc("tS", ("mmS", first))
        for ai in range(first, last + 1):
            S.inc("aS", ("exp", ai))
            if ai > first:
                S.inc("tS", ("trB", ai - 1))
                S.inc("aS", ("tevB", ai - 1))
            if ai < last:
                S.inc("tS", ("mmS", ai + 1))
            S.inc("tS", ("trP", ai))
            S.inc("vS", ("tevP", ai))
            S.inc("tS", ("mmO", ai))
            S.inc("vS", ("oev", ai))
        S.inc("tS", ("trB", last))
        S.inc("aS", ("tevB", last))
        for sc in range(16):
            oi = q * 16 + sc
            S.inc("tS", ("mmF", oi))
            S.inc("dX", ("oxr", oi))
            S.inc("vS", ("oev1", oi))
            S.inc("pS", ("oev2", oi))
            S.inc("dS", ("store", oi))

    with ExitStack() as st:
        sb = lambda n, shape, dt: st.enter_context(nc.sbuf_tensor("sb_" + S.pfx + n, shape, dt))
        xln_tok = [sb("xlntok0", [128, 1024], BF16), sb("xlntok1", [128, 1024], BF16)]
        xlnT = sb("xlnT", [128, 16384], BF16)
        attT = xlnT
        wsh = sb("wsh", [128, 8192], BF16)
        bqc = sb("bqc", [128, 8], F32)
        bkc = sb("bkc", [128, 8], F32)
        bvb = sb("bvb", [128, 1024], F32)
        bob = sb("bob", [128, 1024], F32)
        qT = sb("qT", [128, 16384], BF16)
        kT = sb("kT", [128, 16384], BF16)
        v_tok = sb("v_tok", [128, 16384], BF16)
        P_sb = [sb("P_sb0", [128, 2048], BF16), sb("P_sb1", [128, 2048], BF16)]
        PT = sb("PT", [128, 2048], BF16)
        O_s = [sb("O_s0", [128, 64], BF16), sb("O_s1", [128, 64], BF16)]
        sstat = sb("sstat", [128, 8], F32)
        ones_f = sb("ones_f", [128, 128], F32)
        ones_bf = sb("ones_bf", [128, 128], BF16)
        csc = sb("csc", [1, 8], F32)
        cneg_bf = sb("cneg_bf", [1, 1], BF16)
        cb = sb("cb", [128, 32], F32)
        pS = st.enter_context(nc.psum_tensor(S.pfx + "pS", [128, 2048], F32))
        pTr = st.enter_context(nc.psum_tensor(S.pfx + "pTr", [128, 1024], F32))
        pO = st.enter_context(nc.psum_tensor(S.pfx + "pO", [128, 128], F32))

        pTrb = pTr[:, :].bitcast(BF16)
        pTR_slots = [pTrb[:, r * 128:(r + 1) * 128] for r in range(16)]

        def xlnT_dst3d(cc):
            return xlnT[:, :].rearrange("p (r t) -> p r t", t=2048)[:, :, cc * 128:(cc + 1) * 128]

        def pTR_src3d(n=8):
            return pTrb.rearrange("p (r t) -> p r t", t=128)[:, 0:n, :]

        qpb = lambda h: (h % 2) * 64

        @block.sync
        def _(sync):
            S.barrier(sync, snap)
            for dst, srcdram in ((bqc, dbq), (bkc, dbk), (bvb, dbv), (bob, dbo),
                                 (ones_f, dones)):
                sync.dma_start(out=dst[:, :], in_=srcdram[:, :]).then_inc(S.sem["dX"], 16)
            for q in range(NSEQ):
                for cc in range(NCH):
                    G = q * NCH + cc
                    if G >= 2:
                        S.wait(sync, ("l", "apply", G - 2))
                    sync.dma_start(out=xchunk[G % 2][:, :],
                                   in_=dxin[G * 128:(G + 1) * 128, :]).then_inc(S.sem["dX"], 16)
                for sc in range(16):
                    oi = q * 16 + sc
                    if oi >= 1:
                        S.wait(sync, ("oev2", oi - 1))
                    row0 = oi * 128
                    sync.dma_start(out=xr4[:, :],
                                   in_=dxin[row0:row0 + 128, :]).then_inc(S.sem["dX"], 16)
                    S.wait(sync, ("oev2", oi))
                    sync.dma_start(out=dxout[row0:row0 + 128, :],
                                   in_=outp[:, :]).then_inc(S.sem["dS"], 16)
            sync.wait_ge(S.sem["dS"], 16 * S.final("dS"))

        @block.gpsimd
        def _(g):
            S.barrier(g, snap)
            for q in range(NSEQ):
                for stg, dw in enumerate((dwq, dwk, dwv, dwo)):
                    if stg == 0 and q >= 1:
                        S.wait(g, ("mmF", q * 16 - 1))
                    elif stg == 1:
                        S.wait(g, ("mmP", q, 0, 7))
                    elif stg == 2:
                        S.wait(g, ("mmP", q, 1, 7))
                    elif stg == 3:
                        S.wait(g, ("mmV", q, 15))
                    for r in range(8):
                        g.dma_start(out=wsh[:, r * 1024:(r + 1) * 1024],
                                    in_=dw[r][:, :]).then_inc(S.sem["gD"], 16)
                for sc in range(16):
                    oi = q * 16 + sc
                    S.wait_max(g, [("oev1", oi), ("oxr", oi)] +
                               ([("store", oi - 1)] if oi >= 1 else []))
                    g.tensor_add(outp[:, :], outv[:, :], xr4[:, :])
                    g.drain()
                    g.engine_nop().then_inc(S.sem["pS"], 1)

        @block.tensor
        def _(t):
            S.barrier(t, snap)
            ptr_prev = []
            for q in range(NSEQ):
                for cc in range(NCH):
                    G = q * NCH + cc
                    waits = list(ptr_prev)
                    ln_emit_tensor(t, S, "l", G, xln_tok[G % 2], identbf,
                                   pTR_slots[0:8], waits)
                    ptr_prev = [("l", "xevac", G)]
                for stg in range(2):
                    for oc in range(8):
                        waits = [("wld", q, stg)]
                        if stg == 0 and oc == 0:
                            waits.append(("l", "xevac", q * NCH + NCH - 1))
                        prev = ("pev", q, stg, oc - 1) if oc >= 1 else \
                               (("pev", q, 0, 7) if stg == 1 else None)
                        if prev is None and q >= 1 and stg == 0:
                            prev = ("mmF", q * 16 - 1)
                            prev = ("oev1", q * 16 - 1)
                        if prev is not None:
                            waits.append(prev)
                        S.wait_max(t, waits)
                        ins = None
                        for n in range(4):
                            for r in range(8):
                                ins = t.matmul(pS[:, n * 512:(n + 1) * 512],
                                               wsh[:, r * 1024 + oc * 128: r * 1024 + (oc + 1) * 128],
                                               xlnT[:, r * 2048 + n * 512: r * 2048 + (n + 1) * 512],
                                               start=(r == 0), stop=(r == 7))
                        ins.then_inc(S.sem["tS"], 1)
                for tc in range(16):
                    waits = [("wld", q, 2)]
                    if tc < 2:
                        waits.append(("pev", q, 1, 7))
                    else:
                        waits.append(("vev", q, tc - 2))
                    S.wait_max(t, waits)
                    half = (tc % 2) * 1024
                    ins = None
                    for r in range(8):
                        for n in range(2):
                            ins = t.matmul(pS[:, half + n * 512: half + (n + 1) * 512],
                                           xlnT[:, r * 2048 + tc * 128: r * 2048 + (tc + 1) * 128],
                                           wsh[:, r * 1024 + n * 512: r * 1024 + (n + 1) * 512],
                                           start=(r == 0), stop=(r == 7))
                    ins.then_inc(S.sem["tS"], 1)
                def emit_mmS(ai2):
                    h2, sc2 = (ai2 // 16) % 16, ai2 % 16
                    ins = None
                    for n in range(4):
                        ins = t.matmul(pS[:, n * 512:(n + 1) * 512],
                                       qT[qpb(h2):qpb(h2) + 64,
                                          (h2 // 2) * 2048 + sc2 * 128: (h2 // 2) * 2048 + (sc2 + 1) * 128],
                                       kT[qpb(h2):qpb(h2) + 64,
                                          (h2 // 2) * 2048 + n * 512: (h2 // 2) * 2048 + (n + 1) * 512],
                                       start=True, stop=True)
                    ins.then_inc(S.sem["tS"], 1)

                first, last = q * 256, q * 256 + 255
                for h2 in range(16):
                    b = qpb(h2)
                    waits = [("csq", q, h2, 0)]
                    if h2 == 0:
                        waits.append(("vev", q, 15))
                        if q >= 1:
                            waits.append(("oev1", q * 16 - 1))
                    S.wait_max(t, waits)
                    ins = None
                    for n in range(4):
                        ins = t.matmul(pS[0:1, n * 512:(n + 1) * 512],
                                       ones_bf[b:b + 64, 0:1],
                                       P_sb[0][b:b + 64, n * 512:(n + 1) * 512],
                                       start=True, stop=True)
                    ins.then_inc(S.sem["tS"], 1)
                    S.wait_max(t, [("csq", q, h2, 1), ("crd", q, h2, 0)])
                    ins = None
                    for n in range(4):
                        ins = t.matmul(pS[0:1, n * 512:(n + 1) * 512],
                                       ones_bf[b:b + 64, 0:1],
                                       P_sb[0][b:b + 64, n * 512:(n + 1) * 512],
                                       start=True, stop=True)
                    ins.then_inc(S.sem["tS"], 1)
                    S.wait(t, ("cneg", q, h2))
                    t.matmul(pO[:, 0:1], ones_bf[0:1, :], cneg_bf[0:1, 0:1],
                             start=True, stop=True).then_inc(S.sem["tS"], 1)
                S.wait(t, ("crd", q, 15, 1))
                emit_mmS(first)
                for ai in range(first, last + 1):
                    h, pi = (ai // 16) % 16, ai % 2
                    if ai > first:
                        S.wait(t, ("oev", ai - 1))
                        t.transpose(pTrb[0:64, 0:128], O_s[(ai - 1) % 2][:, :],
                                    identbf[:, :]).then_inc(S.sem["tS"], 1)
                    if ai < last:
                        S.wait(t, ("exp", ai))
                        emit_mmS(ai + 1)
                    waits = [("exp", ai)]
                    if ai >= 1:
                        waits += [("tevB", ai - 1), ("tevP", ai - 1)]
                    if ai == first:
                        waits.append(("l", "xevac", q * NCH + NCH - 1))
                    S.wait_max(t, waits)
                    ins = None
                    for j in range(16):
                        ins = t.transpose(pTR_slots[j],
                                          P_sb[pi][:, j * 128:(j + 1) * 128], identbf[:, :])
                    ins.then_inc(S.sem["tS"], 1)
                    waits = [("tevP", ai)]
                    if ai >= 2:
                        waits.append(("oev", ai - 2))
                    S.wait_max(t, waits)
                    ins = None
                    for j in range(16):
                        ins = t.matmul(pO[:, pi * 64:(pi + 1) * 64],
                                       PT[:, j * 128:(j + 1) * 128],
                                       v_tok[:, j * 1024 + h * 64: j * 1024 + (h + 1) * 64],
                                       start=(j == 0), stop=(j == 15))
                    ins.then_inc(S.sem["tS"], 1)
                S.wait(t, ("oev", last))
                t.transpose(pTrb[0:64, 0:128], O_s[last % 2][:, :],
                            identbf[:, :]).then_inc(S.sem["tS"], 1)
                ptr_prev = [("tevB", last)]
                for sc in range(16):
                    oi = q * 16 + sc
                    waits = []
                    if sc == 0:
                        waits += [("tevB", (q * 16 + 15) * 16 + 15), ("exp", (q * 16 + 15) * 16 + 15),
                                  ("wld", q, 3)]
                    if oi >= 2:
                        waits.append(("oev1", oi - 2))
                    if waits:
                        S.wait_max(t, waits)
                    half = (sc % 2) * 1024
                    ins = None
                    for r in range(8):
                        for n in range(2):
                            ins = t.matmul(pS[:, half + n * 512: half + (n + 1) * 512],
                                           attT[:, r * 2048 + sc * 128: r * 2048 + (sc + 1) * 128],
                                           wsh[:, r * 1024 + n * 512: r * 1024 + (n + 1) * 512],
                                           start=(r == 0), stop=(r == 7))
                    ins.then_inc(S.sem["tS"], 1)

        @block.vector
        def _(v):
            S.barrier(v, snap)
            S.wait(v, ("init",))
            v.tensor_copy(ones_bf[:, :], ones_f[:, :]).then_inc(S.sem["vS"], 1)
            for q in range(NSEQ):
                for cc in range(NCH):
                    G = q * NCH + cc
                    extra = [("l", "tr", G - 2)] if G >= 2 else []
                    ln_emit_vector(v, S, "l", G, xchunk[G % 2], stats, consts,
                                   pTR_src3d(8), xlnT_dst3d(cc), xln_tok[G % 2], extra)
                for stg in range(2):
                    bc = bqc if stg == 0 else bkc
                    dst = qT if stg == 0 else kT
                    for oc in range(8):
                        S.wait(v, ("mmP", q, stg, oc))
                        if stg == 0:
                            v.tensor_scalar(dst[:, oc * 2048:(oc + 1) * 2048], pS[:, :],
                                            bc[:, oc:oc + 1], consts[:, C_QK8:C_QK8 + 1],
                                            ALU.add, ALU.mult).then_inc(S.sem["vS"], 1)
                        else:
                            v.tensor_scalar_add(dst[:, oc * 2048:(oc + 1) * 2048], pS[:, :],
                                                bc[:, oc:oc + 1]).then_inc(S.sem["vS"], 1)
                for tc in range(16):
                    S.wait(v, ("mmV", q, tc))
                    half = (tc % 2) * 1024
                    v.tensor_add(v_tok[:, tc * 1024:(tc + 1) * 1024],
                                 pS[:, half:half + 1024], bvb[:, :]).then_inc(S.sem["vS"], 1)
                first, last = q * 256, q * 256 + 255
                for h2 in range(16):
                    b = qpb(h2)
                    ksl = kT[b:b + 64, (h2 // 2) * 2048:(h2 // 2 + 1) * 2048]
                    qsl = qT[b:b + 64, (h2 // 2) * 2048:(h2 // 2 + 1) * 2048]
                    waits = []
                    if h2 == 0 and q >= 1:
                        waits.append(("trP", q * 256 - 2))
                    if h2 == 0:
                        waits.append(("pev", q, 1, 7))
                    if h2 >= 1:
                        waits.append(("cmm", q, h2 - 1, 1))
                    if waits:
                        S.wait_max(v, waits)
                    v.tensor_mul(P_sb[0][b:b + 64, :], ksl, ksl).then_inc(S.sem["vS"], 1)
                    S.wait(v, ("cmm", q, h2, 0))
                    v.reduce_max(csc[0:1, 0:1], pS[0:1, :], axis=AX.X)
                    v.drain()
                    v.engine_nop().then_inc(S.sem["vS"], 1)
                    v.tensor_mul(P_sb[0][b:b + 64, :], qsl, qsl).then_inc(S.sem["vS"], 1)
                    S.wait(v, ("cmm", q, h2, 1))
                    v.reduce_max(csc[0:1, 1:2], pS[0:1, :], axis=AX.X)
                    v.drain()
                    v.engine_nop().then_inc(S.sem["vS"], 1)
                    v.tensor_mul(csc[0:1, 2:3], csc[0:1, 0:1],
                                 csc[0:1, 1:2]).then_inc(S.sem["vS"], 1)
                    S.wait(v, ("cexp", q, h2))
                    v.tensor_scalar_mul(cneg_bf[0:1, 0:1], csc[0:1, 4:5],
                                        consts[0:1, C_NEG1:C_NEG1 + 1]).then_inc(S.sem["vS"], 1)
                    S.wait(v, ("cbc", q, h2))
                    v.tensor_copy(cb[:, q * 16 + h2:q * 16 + h2 + 1],
                                  pO[:, 0:1]).then_inc(S.sem["vS"], 1)
                for ai in range(first, last + 1):
                    pi = ai % 2
                    S.wait(v, ("trP", ai))
                    v.tensor_copy(PT[:, :],
                                  pTrb[:, :]).then_inc(S.sem["vS"], 1)
                    waits = [("mmO", ai)]
                    if ai >= 2:
                        waits.append(("trB", ai - 2))
                    S.wait_max(v, waits)
                    v.reciprocal(sstat[:, 4 + pi:5 + pi], sstat[:, 2 + pi:3 + pi])
                    v.drain()
                    v.tensor_scalar_mul(O_s[pi][:, :], pO[:, pi * 64:(pi + 1) * 64],
                                        sstat[:, 4 + pi:5 + pi]).then_inc(S.sem["vS"], 1)
                for sc in range(16):
                    oi = q * 16 + sc
                    waits = [("mmF", oi)]
                    if oi >= 1:
                        waits.append(("oev2", oi - 1))
                    S.wait_max(v, waits)
                    half = (sc % 2) * 1024
                    v.tensor_add(outv[:, :], pS[:, half:half + 1024], bob[:, :])
                    v.drain()
                    v.engine_nop().then_inc(S.sem["vS"], 1)

        @block.scalar
        def _(s):
            S.barrier(s, snap)
            for q in range(NSEQ):
                for cc in range(NCH):
                    G = q * NCH + cc
                    ln_emit_scalar(s, S, "l", G, xchunk[G % 2], stats, consts, sqscr)
                first, last = q * 256, q * 256 + 255
                for h2 in range(16):
                    S.wait(s, ("cc2", q, h2))
                    s.activation(csc[0:1, 3:4], csc[0:1, 2:3], AF.Ln)
                    s.drain()
                    s.activation(csc[0:1, 4:5], csc[0:1, 3:4], AF.Exp,
                                 scale=0.5).then_inc(S.sem["aS"], 1)
                for ai in range(first, last + 1):
                    h, sc, pi = (ai // 16) % 16, ai % 16, ai % 2
                    waits = [("mmS", ai)]
                    if sc == 0:
                        waits.append(("cbe", q, h))
                    if ai >= 2:
                        waits += [("trP", ai - 2), ("oev", ai - 2)]
                    S.wait_max(s, waits)
                    s.activation(P_sb[pi][:, :], pS[:, :], AF.Exp,
                                 bias=cb[:, q * 16 + h:q * 16 + h + 1],
                                 accum_out=sstat[:, 2 + pi:3 + pi]).then_inc(S.sem["aS"], 1)
                    if ai > first:
                        h1, sc1 = ((ai - 1) // 16) % 16, (ai - 1) % 16
                        S.wait(s, ("trB", ai - 1))
                        s.copy(attT[qpb(h1):qpb(h1) + 64,
                                    (h1 // 2) * 2048 + sc1 * 128:(h1 // 2) * 2048 + (sc1 + 1) * 128],
                               pTrb[0:64, 0:128]).then_inc(S.sem["aS"], 1)
                h1, sc1 = 15, 15
                S.wait(s, ("trB", last))
                s.copy(attT[qpb(h1):qpb(h1) + 64,
                            (h1 // 2) * 2048 + sc1 * 128:(h1 // 2) * 2048 + (sc1 + 1) * 128],
                       pTrb[0:64, 0:128]).then_inc(S.sem["aS"], 1)


def attn1_inputs(w):
    return {"wqb": w["wqb"], "wkb": w["wkb"], "wvb": w["wvb"], "bqc": w["bqc"],
            "bkc": w["bkc"], "bvb": w["bvb"], "wob": w["wob"], "bob": w["bob"],
            "onesl": w["onesl"]}


def attn0_inputs(w):
    return {"wqkvb": w["wqkvb"], "bqkv": w["bqkv"], "woutb": w["woutb"],
            "boutb": w["boutb"], "pos2": w["pos2"], "onesb": w["onesb"]}


def ffn_inputs(w, L):
    keys = ("w1b", "w2b", "h1b", "h2b", "w3b", "sc1", "sc2", "sch1", "sch2", "s3b")
    return {k + str(L): w[k + str(L)] for k in keys}



def _dram_attn0(nc):
    return {
        "wqkvb": nc.dram_tensor("wqkvb", [8, 128, 3072], F32, kind="ExternalInput"),
        "bqkv": nc.dram_tensor("bqkv", [128, 24], F32, kind="ExternalInput"),
        "woutb": nc.dram_tensor("woutb", [8, 128, 1024], F32, kind="ExternalInput"),
        "boutb": nc.dram_tensor("boutb", [128, 1024], F32, kind="ExternalInput"),
        "pos2": nc.dram_tensor("pos2", [128, 2048], F32, kind="ExternalInput"),
        "onesb": nc.dram_tensor("onesb", [1, 128], F32, kind="ExternalInput"),
    }


def _dram_attn1(nc):
    return {
        "wqb": nc.dram_tensor("wqb", [8, 128, 1024], F32, kind="ExternalInput"),
        "wkb": nc.dram_tensor("wkb", [8, 128, 1024], F32, kind="ExternalInput"),
        "wvb": nc.dram_tensor("wvb", [8, 128, 1024], F32, kind="ExternalInput"),
        "bqc": nc.dram_tensor("bqc", [128, 8], F32, kind="ExternalInput"),
        "bkc": nc.dram_tensor("bkc", [128, 8], F32, kind="ExternalInput"),
        "bvb": nc.dram_tensor("bvb", [128, 1024], F32, kind="ExternalInput"),
        "wob": nc.dram_tensor("wob", [8, 128, 1024], F32, kind="ExternalInput"),
        "onesl": nc.dram_tensor("onesl", [128, 128], F32, kind="ExternalInput"),
        "bob": nc.dram_tensor("bob", [128, 1024], F32, kind="ExternalInput"),
    }


def _dram_ffn(nc, L):
    s = str(L)
    return {
        "w1b": nc.dram_tensor("w1b" + s, [32, 128, 8, 128], F32, kind="ExternalInput"),
        "w2b": nc.dram_tensor("w2b" + s, [32, 128, 8, 128], F32, kind="ExternalInput"),
        "h1b": nc.dram_tensor("h1b" + s, [32, 128, 32, 128], F32, kind="ExternalInput"),
        "h2b": nc.dram_tensor("h2b" + s, [32, 128, 32, 128], F32, kind="ExternalInput"),
        "w3b": nc.dram_tensor("w3b" + s, [32, 128, 1024], F32, kind="ExternalInput"),
        "sc1": nc.dram_tensor("sc1" + s, [128, 32], F32, kind="ExternalInput"),
        "sc2": nc.dram_tensor("sc2" + s, [128, 32], F32, kind="ExternalInput"),
        "sch1": nc.dram_tensor("sch1" + s, [128, 32], F32, kind="ExternalInput"),
        "sch2": nc.dram_tensor("sch2" + s, [128, 32], F32, kind="ExternalInput"),
        "s3b": nc.dram_tensor("s3b" + s, [128, 1024], F32, kind="ExternalInput"),
    }



def _shared_and_init(nc, st, S, block, dcon, dide):
    sb = lambda n, shape, dt: st.enter_context(nc.sbuf_tensor("sh_" + n, shape, dt))
    sh = {
        "xchunk": [sb("xchunk0", [128, 1024], F32), sb("xchunk1", [128, 1024], F32)],
        "sqscr": sb("sqscr", [128, 1024], BF16),
        "stats": sb("stats", [128, 16], F32),
        "consts": sb("consts", [128, 8], F32),
        "ident32": sb("ident32", [128, 128], F32),
        "identbf": sb("identbf", [128, 128], BF16),
        "xr4": sb("xr4", [128, 1024], F32),
        "outv": sb("outv", [128, 1024], F32),
        "outp": sb("outp", [128, 1024], F32),
    }
    S.pfx = "g:"
    S.inc("dX", n=2); S.setmark(("init",), "dX")
    S.inc("vS", ("identbf",))

    @block.sync
    def _(sync):
        sync.dma_start(out=sh["consts"][:, :], in_=dcon[:, :]).then_inc(S.sem["dX"], 16)
        sync.dma_start(out=sh["ident32"][:, :], in_=dide[:, :]).then_inc(S.sem["dX"], 16)

    @block.vector
    def _(v):
        S.wait(v, ("init",))
        v.tensor_copy(sh["identbf"][:, :], sh["ident32"][:, :]).then_inc(S.sem["vS"], 1)

    return sh


def build_fused_nc():
    nc = bass.Bass("TRN2", target_bir_lowering=False)
    dxin = nc.dram_tensor("xin", [TPC, D], F32, kind="ExternalInput")
    dx1 = nc.dram_tensor("x1", [TPC, D], F32)
    dx2 = nc.dram_tensor("x2", [TPC, D], F32)
    dx3 = nc.dram_tensor("x3", [TPC, D], F32)
    dxout = nc.dram_tensor("xout", [TPC, D], F32, kind="ExternalOutput")
    dcon = nc.dram_tensor("consts", [128, 8], F32, kind="ExternalInput")
    dide = nc.dram_tensor("ident", [128, 128], F32, kind="ExternalInput")
    dr_a0 = _dram_attn0(nc); dr_a0.update(xin=dxin, xout=dx1)
    dr_f0 = _dram_ffn(nc, 0); dr_f0.update(xin=dx1, xout=dx2)
    dr_l1 = _dram_attn1(nc); dr_l1.update(xin=dx2, xout=dx3)
    dr_f1 = _dram_ffn(nc, 1); dr_f1.update(xin=dx3, xout=dxout)
    for L, dr_f in ((0, dr_f0), (1, dr_f1)):
        s = str(L)
        dr_f.update(
            w1c=nc.dram_tensor("w1c" + s, [32, 128, 1024], BF16),
            w2c=nc.dram_tensor("w2c" + s, [32, 128, 1024], BF16),
            h1c=nc.dram_tensor("h1c" + s, [32, 128, 4096], BF16),
            h2c=nc.dram_tensor("h2c" + s, [32, 128, 4096], BF16),
            w3c=nc.dram_tensor("w3c" + s, [32, 128, 1024], BF16),
        )
    dr_a0["precast"] = []
    for dr_f in (dr_f0, dr_f1):
        dr_a0["precast"] += [
            (dr_f["w1b"], dr_f["w1c"], 4), (dr_f["w2b"], dr_f["w2c"], 4),
            (dr_f["h1b"], dr_f["h1c"], 1), (dr_f["h2b"], dr_f["h2c"], 1),
            (dr_f["w3b"], dr_f["w3c"], 4),
        ]

    S = Sched()
    with ExitStack() as st:
        for name in SEMS:
            S.sem[name] = st.enter_context(nc.semaphore(name))
        block = st.enter_context(nc.Block())
        sh = _shared_and_init(nc, st, S, block, dcon, dide)
        snap = S.snapshot()
        S.pfx = "a0:"
        emit_attn0(nc, block, S, sh, dr_a0, snap)
        snap = S.snapshot()
        S.pfx = "f0:"
        emit_ffn(nc, block, S, sh, dr_f0, snap)
        snap = S.snapshot()
        S.pfx = "l1:"
        emit_attn1(nc, block, S, sh, dr_l1, snap)
        snap = S.snapshot()
        S.pfx = "f1:"
        emit_ffn(nc, block, S, sh, dr_f1, snap)
    return nc



def _build_single(which, L=0):
    nc = bass.Bass("TRN2", target_bir_lowering=False)
    dxin = nc.dram_tensor("xin", [TPC, D], F32, kind="ExternalInput")
    dxout = nc.dram_tensor("xout", [TPC, D], F32, kind="ExternalOutput")
    dcon = nc.dram_tensor("consts", [128, 8], F32, kind="ExternalInput")
    dide = nc.dram_tensor("ident", [128, 128], F32, kind="ExternalInput")
    if which == "a0":
        dr = _dram_attn0(nc)
    elif which == "l1":
        dr = _dram_attn1(nc)
    else:
        dr = _dram_ffn(nc, L)
    dr.update(xin=dxin, xout=dxout)
    S = Sched()
    with ExitStack() as st:
        for name in SEMS:
            S.sem[name] = st.enter_context(nc.semaphore(name))
        block = st.enter_context(nc.Block())
        sh = _shared_and_init(nc, st, S, block, dcon, dide)
        snap = S.snapshot()
        S.pfx = which + ":"
        if which == "a0":
            emit_attn0(nc, block, S, sh, dr, snap)
        elif which == "l1":
            emit_attn1(nc, block, S, sh, dr, snap)
        else:
            emit_ffn(nc, block, S, sh, dr, snap)
    return nc



def kernel(**inputs):
    w = prep_shared(inputs)
    x = np.ascontiguousarray(np.asarray(inputs["L0_attn_x"], np.float32).reshape(NCORES * TPC, D))
    shared_w = {"consts": w["consts"], "ident": w["ident"]}
    shared_w.update(attn0_inputs(w))
    shared_w.update(ffn_inputs(w, 0))
    shared_w.update(attn1_inputs(w))
    shared_w.update(ffn_inputs(w, 1))

    nc = build_fused_nc()
    in_maps = []
    for c in range(NCORES):
        m = dict(shared_w)
        m["xin"] = np.ascontiguousarray(x[c * TPC:(c + 1) * TPC])
        in_maps.append(m)
    res = run_bass_kernel_spmd(nc, in_maps, core_ids=list(range(NCORES)))
    out = np.concatenate([res.results[c]["xout"] for c in range(NCORES)], axis=0)
    return out.reshape(16, 2048, D).astype(np.float32)
